# version 47
# baseline (speedup 1.0000x reference)
import numpy as np
import ml_dtypes
import concourse.bass as bass
import concourse.mybir as mybir
from concourse.bass_utils import run_bass_kernel_spmd
from contextlib import ExitStack

F32 = mybir.dt.float32
BF16 = mybir.dt.bfloat16
AX = mybir.AxisListType.X
AL = mybir.AluOpType
AF = mybir.ActivationFunctionType
BF = ml_dtypes.bfloat16
MAGIC = 12582912.0

D = 1024
T = 2048
NSC = 8
SC = 256
HEADS = 4
DVT = 2048
MIDR = 256
MIDS = 512
NR = 16
NS = 2

ENGINES = ("sync", "vector", "scalar", "tensor", "gpsimd")


def _nm(ap):
    t = getattr(ap, "tensor", None)
    return t.name if t is not None else ap.name


class Sched:
    def __init__(self):
        self.ops = []
        self.lastw = {}
        self.readers = {}

    def add(self, eng, fn, ins=(), outs=(), dma=None, tag=""):
        reads = [a if isinstance(a, str) else _nm(a) for a in ins]
        writes = [a if isinstance(a, str) else _nm(a) for a in outs]
        deps = set()
        for r in reads:
            i = self.lastw.get(r)
            if i is not None:
                deps.add(i)
        for w in writes:
            i = self.lastw.get(w)
            if i is not None:
                deps.add(i)
            for rd in self.readers.get(w, ()):
                deps.add(rd)
        idx = len(self.ops)
        self.ops.append(dict(eng=eng, fn=fn, deps=deps, dma=dma, tag=tag))
        for r in reads:
            self.readers.setdefault(r, []).append(idx)
        for w in writes:
            self.lastw[w] = idx
            self.readers[w] = []
        return idx

    def emit(self, nc, block, stack):
        n = len(self.ops)
        dkeys = sorted({op["dma"] for op in self.ops if op["dma"] is not None})
        esem = {e: stack.enter_context(nc.semaphore(f"pg_{e}")) for e in ENGINES}
        dsem = {k: stack.enter_context(nc.semaphore(f"dm_{k}")) for k in dkeys}
        inc_needed = [False] * n

        def same_safe(oi, op):
            return (oi["dma"] is None and op["dma"] is None
                    and oi["eng"] == op["eng"] and oi["eng"] == "tensor")

        for j, op in enumerate(self.ops):
            for i in op["deps"]:
                oi = self.ops[i]
                if oi["dma"] is not None:
                    continue
                if same_safe(oi, op):
                    continue
                inc_needed[i] = True
        ccount = {e: 0 for e in ENGINES}
        dcount = {}
        milestone = [None] * n
        for j, op in enumerate(self.ops):
            if op["dma"] is not None:
                k = op["dma"]
                dcount[k] = dcount.get(k, 0) + (1 if k == "cc" else 16)
                milestone[j] = ("d", k, dcount[k])
            elif inc_needed[j]:
                e = op["eng"]
                ccount[e] += 1
                milestone[j] = ("e", e, ccount[e])
        waits = []
        for j, op in enumerate(self.ops):
            need = {}
            for i in op["deps"]:
                oi = self.ops[i]
                if same_safe(oi, op):
                    continue
                m = milestone[i]
                key = (m[0], m[1])
                need[key] = max(need.get(key, 0), m[2])
            waits.append(need)
        per_engine = {e: [] for e in ENGINES}
        for j, op in enumerate(self.ops):
            per_engine[op["eng"]].append(j)

        def mk(engname):
            def body(e):
                watched = {}
                for j in per_engine[engname]:
                    op = self.ops[j]
                    for (kind, key), v in sorted(waits[j].items()):
                        if watched.get((kind, key), 0) >= v:
                            continue
                        e.wait_ge(esem[key] if kind == "e" else dsem[key], v)
                        watched[(kind, key)] = v
                    ins = op["fn"](e)
                    m = milestone[j]
                    if m is not None:
                        if m[0] == "d":
                            ins.then_inc(dsem[m[1]], 1 if m[1] == "cc" else 16)
                        else:
                            ins.then_inc(esem[m[1]], 1)
                if engname == "sync":
                    for k in sorted(dcount):
                        e.wait_ge(dsem[k], dcount[k])
                    for en in ENGINES:
                        if en != "sync" and ccount[en] > 0:
                            e.wait_ge(esem[en], ccount[en])
            return body

        for name in ENGINES:
            getattr(block, name)(mk(name))


def build_program(taps=()):
    nc = bass.Bass("TRN2", target_bir_lowering=False, debug=False)
    S = Sched()
    BUFS = {}
    TAPS = list(taps)

    def din(name, shape, dtype=F32):
        return nc.dram_tensor(name, shape, dtype, kind="ExternalInput").ap()

    xT = din("xT", [D, T])
    wqT = din("wqT", [D, D])
    wkT = din("wkT", [D, D])
    wvT = din("wvT", [D, DVT])
    wgT = din("wgT", [D, DVT])
    wgk1T = din("wgk1T", [D, 16])
    wgk2T = din("wgk2T", [16, D])
    woT = din("woT", [DVT, D])
    gains = din("gains", [128, 40])
    no_c = din("no_c", [128, 16])
    bgk2_c = din("bgk2_c", [128, 8])
    ngk2_b = din("ngk2_b", [128, 16])
    alpha = din("alpha", [128, 1])
    ident_f = din("ident_f", [128, 128])
    ident_b = din("ident_b", [128, 128], BF16)
    maskU_d = din("maskU", [128, 128])
    sel16_d = din("sel16_d", [16, 2048], BF16)
    f_rw = din("f_rw", [D, 16])
    f_wg_r = din("f_wg_r", [NR, D, MIDR])
    f_wu_r = din("f_wu_r", [NR, D, MIDR])
    f_wd_r = din("f_wd_r", [NR, MIDR, D])
    f_wg_s = din("f_wg_s", [NS, D, MIDS])
    f_wu_s = din("f_wu_s", [NS, D, MIDS])
    f_wd_s = din("f_wd_s", [NS, MIDS, D])
    out_d = nc.dram_tensor("out", [D, T], F32, kind="ExternalOutput").ap()

    def dscr(name, shape, dtype=BF16):
        return nc.dram_tensor(name, shape, dtype, kind="Internal").ap()

    o_scr = dscr("o_scr", [DVT, T])
    qg_scr = dscr("qg_scr", [D, T])
    xqg_scr = dscr("xqg_scr", [D, T])
    wg_scr = dscr("wg_scr", [D, DVT])
    wo_scr = dscr("wo_scr", [DVT, D])
    invsg_scr = dscr("invsg_scr", [8, 256], F32)
    s_wgr = dscr("s_wgr", [NR, D, MIDR])
    s_wur = dscr("s_wur", [NR, D, MIDR])
    s_wdr = dscr("s_wdr", [NR, MIDR, D])
    s_wgs = dscr("s_wgs", [NS, D, MIDS])
    s_wus = dscr("s_wus", [NS, D, MIDS])
    s_wds = dscr("s_wds", [NS, MIDS, D])
    s_rw = dscr("s_rw", [D, 16])
    cc_in = dscr("cc_in", [D, 512])
    cc_out = dscr("cc_out", [2 * D, 512])

    stack = ExitStack()

    def sbuf(name, shape, dtype):
        t = stack.enter_context(nc.sbuf_tensor(name, shape, dtype))
        BUFS[name] = t
        return t

    def psum(name, shape, dtype=F32):
        return stack.enter_context(nc.psum_tensor(name, shape, dtype))

    with stack:
        T32a = sbuf("T32a", [128, 16384], BF16)
        T32b = sbuf("T32b", [128, 16384], BF16)
        T32c = sbuf("T32c", [128, 16384], BF16)
        xt0 = sbuf("xt0", [128, 2048], F32)
        xt1 = sbuf("xt1", [128, 2048], F32)
        qT_b = sbuf("qT_b", [128, 2048], BF16)
        kT_b = sbuf("kT_b", [128, 2048], BF16)
        xqg2 = sbuf("xqg2", [128, 2048], BF16)
        v_b = sbuf("v_b", [128, 4096], BF16)
        gkT_b = sbuf("gkT_b", [128, 2048], F32)
        o_st = sbuf("o_st", [128, 4096], BF16)
        qg_st = sbuf("qg_st", [128, 2048], BF16)
        sgb = sbuf("sgb", [128, 4096], BF16)
        sqw = sbuf("sqw", [128, 512], F32)
        sq0 = sqw[:, 0:256]
        sq1 = sqw[:, 256:512]
        srow = sq0
        mfold = sbuf("mfold", [128, 1280], F32)
        statb = sbuf("statb", [128, 48], F32)
        t1b = sbuf("t1b", [128, 256], F32)
        t1tm = sbuf("t1tm", [128, 36], F32)
        xq2tm = sbuf("xq2tm", [128, 20], F32)
        xq2b = sbuf("xq2b", [16, 256], BF16)
        rscrw = sbuf("rscrw", [128, 512], F32)
        rscr = rscrw[:, 0:256]
        rscr2 = rscrw[:, 256:512]
        ab0, ab1 = rscr, rscr2
        Ubuf = sbuf("Ubuf", [128, 1280], F32)
        invsb = sbuf("invsb", [128, 1280], BF16)
        urow5 = sbuf("urow5", [1, 256], F32)
        irow5 = sbuf("irow5", [1, 256], F32)
        i2row = sbuf("i2row", [1, 256], F32)
        abr = sbuf("abr", [1, 512], F32)
        expP = sbuf("expP", [128, 256], BF16)
        expM = sbuf("expM", [128, 256], BF16)
        gscr = sbuf("gscr", [128, 256], F32)
        Kbtm = sbuf("Kbtm", [128, 256], BF16)
        ATm = sbuf("ATm", [128, 128], BF16)
        ATm2 = sbuf("ATm2", [128, 128], BF16)
        T0e = sbuf("T0e", [128, 1024], BF16)
        Scar = sbuf("Scar", [128, 4096], BF16)
        eclb = sbuf("eclb", [128, 8], F32)
        egsum = sbuf("egsum", [128, 8], F32)
        gsml = sbuf("gsml", [128, 8], F32)
        identf = sbuf("identf", [128, 128], F32)
        identb = sbuf("identb", [128, 128], BF16)
        maskUb = sbuf("maskUb", [128, 128], F32)
        sel16 = sbuf("sel16", [16, 2048], BF16)
        ctile = sbuf("ctile", [16, 1024], BF16)
        ones_c = sbuf("ones_c", [128, 1], F32)
        ones_r = sbuf("ones_r", [1, 128], F32)
        gainb = sbuf("gainb", [128, 40], F32)
        gainab = sbuf("gainab", [128, 40], F32)
        nob = sbuf("nob", [128, 16], F32)
        bgk2b = sbuf("bgk2b", [128, 8], F32)
        ngk2bb = sbuf("ngk2bb", [128, 16], F32)
        alphab = sbuf("alphab", [128, 1], F32)
        amrep = sbuf("amrep", [128, 8], F32)
        swrep = sbuf("swrep", [128, 8], F32)
        amsml = sbuf("amsml", [1, 8], F32)
        ltm = sbuf("ltm", [128, 20], F32)
        combtm = sbuf("combtm", [128, 16], F32)
        rwb = sbuf("rwb", [128, 128], BF16)
        wgk1_i = sbuf("wgk1_i", [128, 128], BF16)
        wgk2_i = sbuf("wgk2_i", [16, 1024], BF16)

        pA = psum("pA", [128, 512])
        pB = psum("pB", [128, 512])
        pT0 = psum("pT0", [128, 512])
        pT1 = psum("pT1", [128, 512])
        pAT = psum("pAT", [128, 512])
        pO = psum("pO", [128, 512])
        pS = psum("pS", [128, 512])
        pBT = psum("pBT", [128, 512], BF16)

        Abc = Ubuf[:, 256:512]
        Bbc = Ubuf[:, 512:768]
        hh = sgb[:, 0:512]
        print(f"[build] sbuf free/partition: {nc.sbuf_bytes_remaining // 128}B")

        def dma(eng, out, in_, key, tag=""):
            S.add(eng, lambda e, o=out, i=in_: e.dma_start(out=o, in_=i),
                  ins=[in_], outs=[out], dma=key, tag=tag)

        def cop(eng, fn, ins, outs, tag=""):
            S.add(eng, fn, ins=ins, outs=outs, tag=tag)

        i2bc = Ubuf[:, 0:256]

        def tap(point):
            for (pt, bn) in TAPS:
                if pt != point:
                    continue
                t = BUFS[bn]
                shp = list(t.shape)
                dt_ = t.dtype
                o = nc.dram_tensor(f"dbg_{point}_{bn}", shp, dt_,
                                   kind="ExternalOutput").ap()
                dma("sync", o, t[:], f"dbg{point}{bn}")

        dma("sync", identf[:], ident_f[:], "identf")
        dma("sync", identb[:], ident_b[:], "identb")
        dma("sync", maskUb[:], maskU_d[:], "maskUb")
        dma("sync", sel16[:], sel16_d[:], "sel16")
        dma("sync", gainb[:], gains[:], "gainb")
        dma("sync", nob[:], no_c[:], "nob")
        dma("sync", bgk2b[:], bgk2_c[:], "bgk2b")
        dma("sync", ngk2bb[:], ngk2_b[:], "ngk2bb")
        dma("sync", alphab[:], alpha[:], "alphab")
        cop("gpsimd", lambda e: e.memset(ones_c[:], 1.0), [], [ones_c[:]])
        cop("gpsimd", lambda e: e.memset(ones_r[:], 1.0), [], [ones_r[:]])
        cop("gpsimd", lambda e: e.memset(t1b[:], 0.0), [], [t1b[:]])
        cop("gpsimd", lambda e: e.memset(Scar[:], 0.0), [], [Scar[:]])
        cop("gpsimd", lambda e: e.memset(swrep[:, 7:8], 1e-8), [], [swrep[:]])
        cop("gpsimd", lambda e: e.memset(amrep[:, 7:8], 1e-5), [], [amrep[:]])
        cop("scalar", lambda e: e.activation(out=gainab[:], in_=gainb[:],
                                             func=AF.Abs, bias=0.0, scale=1.0),
            [gainb[:]], [gainab[:]])
        for kcg in range(8):
            cop("vector", lambda e, k=kcg: e.tensor_copy(out=egsum[:, k:k + 1],
                                                         in_=alphab[:]),
                [alphab[:]], [egsum[:]])

        cop("gpsimd", lambda e: e.memset(amsml[:], 0.0), [], [amsml[:]])

        def quant_weight(widx, dram, Din, Dout, dest, tag):
            ndc = (Din + 127) // 128
            ncs = (Dout + 1023) // 1024
            colw = min(Dout, 1024)
            for dc in range(ndc):
                h = min(128, Din - dc * 128)
                for cs in range(ncs):
                    st = xt0 if (dc * ncs + cs) % 2 == 0 else xt1
                    src = dram[dc * 128:dc * 128 + h, cs * 1024:cs * 1024 + colw]
                    dma("sync", st[0:h, 0:colw], src, _nm(st), tag=f"P0L{tag}")
                    cop("scalar", lambda e, s=st, hh_=h: e.activation(
                        out=s[0:hh_, 0:colw], in_=s[0:hh_, 0:colw], func=AF.Abs,
                        bias=0.0, scale=1.0), [st[:]], [st[:]])
                    nns = (colw + 511) // 512
                    for ns in range(nns):
                        a, b = ns * 512, min(colw, ns * 512 + 512)
                        cop("tensor", lambda e, s=st, hh_=h, a_=a, b_=b,
                            first=(ns == 0), last=(ns == nns - 1): e.matmul(
                                pS[0:1, 0:b_ - a_], ones_c[0:hh_, :],
                                s[0:hh_, a_:b_], start=first, stop=last),
                            [st[:], ones_c[:]], [pS[:]])
                    red_w = min(512, Dout)
                    cop("vector", lambda e: e.reduce_sum(
                        out=amsml[0:1, 7:8], in_=pS[0:1, 0:red_w], axis=AX),
                        [pS[:]], [amsml[:]])
                    cop("vector", lambda e: e.tensor_tensor(
                        out=amsml[0:1, widx:widx + 1],
                        in0=amsml[0:1, widx:widx + 1], in1=amsml[0:1, 7:8],
                        op=AL.add), [amsml[:]], [amsml[:]])
            cnt = float(Din * Dout)
            cop("vector", lambda e: e.tensor_scalar(
                out=amsml[0:1, widx:widx + 1], in0=amsml[0:1, widx:widx + 1],
                scalar1=1.0 / cnt, scalar2=1e-5, op0=AL.mult, op1=AL.max),
                [amsml[:]], [amsml[:]])
            cop("tensor", lambda e: e.matmul(pS[:, 0:1], ones_r[:],
                                             amsml[0:1, widx:widx + 1],
                                             start=True, stop=True),
                [amsml[:], ones_r[:]], [pS[:]])
            sc_mul = (1.0 / 16.0) if widx == 0 else 1.0
            cop("scalar", lambda e, m=sc_mul: e.activation(
                out=amrep[:, widx:widx + 1], in_=pS[:, 0:1], func=AF.Copy,
                bias=0.0, scale=m), [pS[:]], [amrep[:]])
            cop("vector", lambda e: e.reciprocal(out=swrep[:, widx:widx + 1],
                                                 in_=amrep[:, widx:widx + 1]),
                [amrep[:]], [swrep[:]])
            if widx == 0:
                cop("vector", lambda e: e.tensor_scalar(
                    out=swrep[:, 0:1], in0=swrep[:, 0:1], scalar1=1.0 / 16.0,
                    scalar2=None, op0=AL.mult), [swrep[:]], [swrep[:]])
            for dc in range(ndc):
                h = min(128, Din - dc * 128)
                for cs in range(ncs):
                    st = xt0 if (dc * ncs + cs) % 2 == 0 else xt1
                    src = dram[dc * 128:dc * 128 + h, cs * 1024:cs * 1024 + colw]
                    dma("sync", st[0:h, 0:colw], src, _nm(st), tag=f"P0R{tag}")
                    cop("vector", lambda e, s=st, hh_=h: e.tensor_scalar(
                        out=s[0:hh_, 0:colw], in0=s[0:hh_, 0:colw],
                        scalar1=swrep[0:hh_, widx:widx + 1], scalar2=MAGIC,
                        op0=AL.mult, op1=AL.add), [st[:], swrep[:]], [st[:]])
                    cop("vector", lambda e, s=st, hh_=h: e.tensor_scalar(
                        out=s[0:hh_, 0:colw], in0=s[0:hh_, 0:colw],
                        scalar1=MAGIC, scalar2=1.0, op0=AL.subtract, op1=AL.min),
                        [st[:]], [st[:]])
                    if dest[0] == "res":
                        dst, dname = dest[1](dc, cs, colw)
                        cop("vector", lambda e, s=st, hh_=h, d=dst: e.tensor_scalar(
                            out=d, in0=s[0:hh_, 0:colw], scalar1=-1.0,
                            scalar2=None, op0=AL.max), [st[:]], [dname])
                    else:
                        cb = qT_b if (dc * ncs + cs) % 2 == 0 else kT_b
                        cop("vector", lambda e, s=st, hh_=h, c=cb: e.tensor_scalar(
                            out=c[0:hh_, 0:colw], in0=s[0:hh_, 0:colw],
                            scalar1=-1.0, scalar2=None, op0=AL.max),
                            [st[:]], [cb[:]])
                        dma("gpsimd",
                            dest[1][dc * 128:dc * 128 + h,
                                    cs * 1024:cs * 1024 + colw],
                            cb[0:h, 0:colw], "st_" + _nm(cb), tag=f"P0S{tag}")

        IWQ, IWK, IWV, IWG, IWO, IGK1, IGK2 = 0, 1, 2, 3, 4, 5, 6
        quant_weight(IWQ, wqT, D, D,
                     ("res", lambda dc, cs, w: (T32b[:, dc * D:dc * D + w],
                                                "T32b")), "wq")
        quant_weight(IWK, wkT, D, D,
                     ("res", lambda dc, cs, w: (T32b[:, 8192 + dc * D:
                                                     8192 + dc * D + w],
                                                "T32b")), "wk")
        quant_weight(IGK1, wgk1T, D, 16,
                     ("res", lambda dc, cs, w: (wgk1_i[:, dc * 16:dc * 16 + w],
                                                "wgk1_i")), "wgk1")
        quant_weight(IGK2, wgk2T, 16, D,
                     ("res", lambda dc, cs, w: (wgk2_i[0:16, 0:w], "wgk2_i")),
                     "wgk2")
        quant_weight(IWV, wvT, D, DVT,
                     ("res", lambda dc, cs, w: (T32a[:, dc * DVT + cs * 1024:
                                                     dc * DVT + cs * 1024 + w],
                                                "T32a")), "wv")
        quant_weight(IWG, wgT, D, DVT, ("scr", wg_scr), "wg")
        quant_weight(IWO, woT, DVT, D, ("scr", wo_scr), "wo")

        pairs = []
        for ei in range(NR):
            pairs += [(f_wg_r[ei], s_wgr[ei], D, MIDR),
                      (f_wu_r[ei], s_wur[ei], D, MIDR),
                      (f_wd_r[ei], s_wdr[ei], MIDR, D)]
        for ei in range(NS):
            pairs += [(f_wg_s[ei], s_wgs[ei], D, MIDS),
                      (f_wu_s[ei], s_wus[ei], D, MIDS),
                      (f_wd_s[ei], s_wds[ei], MIDS, D)]
        pairs += [(f_rw, s_rw, D, 16)]
        cast_units = []
        _ci = [0]

        def mk_cast_unit(src, dst, rows, cols, c0, nch):
            def emit():
                par = _ci[0] % 2
                _ci[0] += 1
                slot = sgb[:, par * 2048:par * 2048 + nch * cols]
                rname = f"sgbH{par}"
                S.add("gpsimd", lambda e: e.dma_start(
                    out=slot.rearrange("p (c t) -> p c t", t=cols),
                    in_=src.rearrange("(c p) t -> p c t", p=128)[:, c0:c0 + nch]),
                    ins=[_nm(src)], outs=[rname], dma=f"ffc{par}")
                S.add("gpsimd", lambda e: e.dma_start(
                    out=dst.rearrange("(c p) t -> p c t", p=128)[:, c0:c0 + nch],
                    in_=slot.rearrange("p (c t) -> p c t", t=cols)),
                    ins=[rname], outs=[_nm(dst)], dma=f"ffs{par}")
            return emit

        for (src, dst, rows, cols) in pairs:
            nch_tot = (rows + 127) // 128
            step = max(1, 2048 // cols)
            for c0 in range(0, nch_tot, step):
                nch = min(step, nch_tot - c0)
                cast_units.append(mk_cast_unit(src, dst, rows, cols, c0, nch))

        def emit_cast_units(k):
            while k > 0 and cast_units:
                cast_units.pop(0)()
                k -= 1

        GQ, GK, GV, GG, GGK1 = 0, 1, 2, 3, 4

        def xq_sl(n, dc, a, b):
            return T32c[:, n * 2048 + dc * 256 + a:n * 2048 + dc * 256 + b]

        def wq_sl(dc, a, b):
            return T32b[:, dc * D + a:dc * D + b]

        def wk_sl(dc, a, b):
            return T32b[:, 8192 + dc * D + a:8192 + dc * D + b]

        def wv_sl(dc, a, b):
            return T32a[:, dc * DVT + a:dc * DVT + b]

        def row_extract(col_ap, src_name, tt, region, P=None):
            P = pS if P is None else P
            cop("tensor", lambda e, c=col_ap, r=region, t=tt, p=P: e.transpose(
                p[0:1, r + t * 128:r + (t + 1) * 128], c, identf[:]),
                [src_name, identf[:]], [_nm(P)])

        def a1_spine_chunks(sc):
            t0 = sc * SC
            xt = xt0 if sc % 2 == 0 else xt1

            def c_load():
                dma("sync", xt[:].rearrange("p (c t) -> p c t", t=256),
                    xT.rearrange("(c p) t -> p c t", p=128)[:, :, t0:t0 + SC],
                    _nm(xt), tag="xload")

            def mk_fold(pr):
                def c_fold():
                    for dc in (2 * pr, 2 * pr + 1):
                        sq = sq0 if dc % 2 == 0 else sq1
                        ab = ab0 if dc % 2 == 0 else ab1
                        cop("scalar", lambda e, d=dc, s=sq, x=xt: e.activation(
                            out=s[:], in_=x[:, d * 256:(d + 1) * 256],
                            func=AF.Square, bias=0.0, scale=1.0),
                            [xt[:]], [sq[:]])
                        cop("tensor", lambda e, s=sq, first=(dc == 0),
                            last=(dc == 7): e.matmul(
                                pS[0:1, 0:256], ones_c[:], s[:],
                                start=first, stop=last),
                            [sq[:], ones_c[:]], [pS[:]])
                        cop("scalar", lambda e, d=dc, a=ab, x=xt: e.activation(
                            out=a[:], in_=x[:, d * 256:(d + 1) * 256],
                            func=AF.Abs, bias=0.0, scale=1.0), [xt[:]], [ab[:]])
                        for n in range(5):
                            if dc == 0:
                                cop("vector", lambda e, a=ab, g=n: e.tensor_scalar(
                                    out=mfold[:, g * 256:(g + 1) * 256], in0=a[:],
                                    scalar1=gainab[:, g * 8:g * 8 + 1],
                                    scalar2=None, op0=AL.mult),
                                    [ab[:], gainab[:]], [mfold[:]])
                            else:
                                cop("vector", lambda e, a=ab, g=n, d=dc:
                                    e.scalar_tensor_tensor(
                                        out=mfold[:, g * 256:(g + 1) * 256],
                                        in0=a[:],
                                        scalar=gainab[:, g * 8 + d:g * 8 + d + 1],
                                        in1=mfold[:, g * 256:(g + 1) * 256],
                                        op0=AL.mult, op1=AL.max),
                                    [ab[:], gainab[:], mfold[:]], [mfold[:]])
                return c_fold

            def c_stat():
                cop("scalar", lambda e: e.activation(
                    out=srow[0:1, :], in_=pS[0:1, 0:256],
                    func=AF.Copy, bias=0.0, scale=1.0), [pS[:]], [srow[:]])
                for tt in range(2):
                    sb_ = statb[:, tt * 24:(tt + 1) * 24]
                    cop("tensor", lambda e, t=tt: e.transpose(
                        pS[:, 0:128], srow[:, t * 128:(t + 1) * 128], identf[:]),
                        [srow[:], identf[:]], [pS[:]])
                    cop("vector", lambda e, s=sb_: e.tensor_copy(
                        out=s[:, 0:1], in_=pS[:, 0:1]), [pS[:]], [statb[:]])
                    for n in range(5):
                        cop("tensor", lambda e, t=tt, g=n: e.transpose(
                            pS[:, 0:128],
                            mfold[:, g * 256 + t * 128:g * 256 + (t + 1) * 128],
                            identf[:]), [mfold[:], identf[:]], [pS[:]])
                        cop("vector", lambda e, s=sb_, g=n: e.reduce_max(
                            out=s[:, 1 + g:2 + g], in_=pS[:, 0:128], axis=AX),
                            [pS[:]], [statb[:]])
                    cop("scalar", lambda e, s=sb_: e.activation(
                        out=s[:, 6:7], in_=s[:, 0:1], func=AF.Sqrt,
                        bias=swrep[:, 7:8], scale=1.0 / 1024.0),
                        [statb[:], swrep[:]], [statb[:]])
                    cop("vector", lambda e, s=sb_: e.reciprocal(
                        out=s[:, 7:8], in_=s[:, 6:7]), [statb[:]], [statb[:]])
                    cop("vector", lambda e, s=sb_: e.tensor_scalar(
                        out=s[:, 8:13], in0=s[:, 1:6], scalar1=s[:, 7:8],
                        scalar2=1e-5, op0=AL.mult, op1=AL.max),
                        [statb[:]], [statb[:]])
                    cop("vector", lambda e, s=sb_: e.tensor_scalar(
                        out=s[:, 13:18], in0=s[:, 8:13], scalar1=1.0 / 127.0,
                        scalar2=None, op0=AL.mult), [statb[:]], [statb[:]])
                    cop("vector", lambda e, s=sb_: e.reciprocal(
                        out=s[:, 18:23], in_=s[:, 8:13]), [statb[:]], [statb[:]])
                    cop("vector", lambda e, s=sb_: e.tensor_scalar(
                        out=s[:, 18:23], in0=s[:, 18:23], scalar1=s[:, 7:8],
                        scalar2=127.0, op0=AL.mult, op1=AL.mult),
                        [statb[:]], [statb[:]])

            def emit_extract_bcast(n):
                for tt in range(2):
                    row_extract(statb[:, tt * 24 + 18 + n:tt * 24 + 19 + n],
                                "statb", tt, 0)
                    row_extract(statb[:, tt * 24 + 13 + n:tt * 24 + 14 + n],
                                "statb", tt, 256)
                cop("scalar", lambda e: e.activation(
                    out=urow5[0:1, :], in_=pS[0:1, 0:256],
                    func=AF.Copy, bias=0.0, scale=1.0), [pS[:]], [urow5[:]])
                cop("scalar", lambda e: e.activation(
                    out=irow5[0:1, :], in_=pS[0:1, 256:512],
                    func=AF.Copy, bias=0.0, scale=1.0), [pS[:]], [irow5[:]])
                if n == GG:
                    dma("sync", invsg_scr[sc:sc + 1, :], irow5[0:1, :],
                        "st_invsg")
                cop("tensor", lambda e: e.matmul(
                    pS[:, 0:256], ones_r[:], urow5[0:1, :],
                    start=True, stop=True), [urow5[:], ones_r[:]], [pS[:]])
                cop("scalar", lambda e, g=n: e.activation(
                    out=Ubuf[:, g * 256:(g + 1) * 256], in_=pS[:, 0:256],
                    func=AF.Copy, bias=0.0, scale=1.0), [pS[:]], [f"UbufG{n}"])
                cop("tensor", lambda e: e.matmul(
                    pS[:, 0:256], ones_r[:], irow5[0:1, :],
                    start=True, stop=True), [irow5[:], ones_r[:]], [pS[:]])
                cop("scalar", lambda e, g=n: e.activation(
                    out=invsb[:, g * 256:(g + 1) * 256], in_=pS[:, 0:256],
                    func=AF.Copy, bias=0.0, scale=1.0), [pS[:]], [invsb[:]])

            def emit_rounds(n):
                for dc in range(8):
                    rs = mfold[:, (dc % 4) * 256:(dc % 4 + 1) * 256]
                    cop("vector", lambda e, d=dc, g=n, r=rs, x=xt:
                        e.scalar_tensor_tensor(
                            out=r, in0=x[:, d * 256:(d + 1) * 256],
                            scalar=gainb[:, g * 8 + d:g * 8 + d + 1],
                            in1=Ubuf[:, g * 256:(g + 1) * 256],
                            op0=AL.mult, op1=AL.mult),
                        [xt[:], gainb[:], f"UbufG{n}"], ["mfold"])
                    if dc % 4 == 3:
                        cop("vector", lambda e, d=dc, g=n: e.tensor_scalar(
                            out=xq_sl(g, d - 3, 0, 1024), in0=mfold[:, 0:1024],
                            scalar1=MAGIC, scalar2=MAGIC,
                            op0=AL.add, op1=AL.subtract),
                            ["mfold"], [f"xqG{n}"])

            def mk_gain(n):
                def c_gain():
                    emit_extract_bcast(n)
                    emit_rounds(n)
                return c_gain

            return ([c_load] + [mk_fold(p) for p in range(4)] + [c_stat]
                    + [mk_gain(n) for n in (GQ, GGK1, GK, GV, GG)])

        def a1_heavy_a(sc):
            t0 = sc * SC

            def proj_fm(xqn, wsl, wname, outb, widx, gidx):
                for op2 in range(4):
                    pp = pA if op2 % 2 == 0 else pB
                    for oh in range(2):
                        oc = op2 * 2 + oh
                        for dc in range(8):
                            cop("tensor", lambda e, o=oc, d=dc, p=pp, h2=oh:
                                e.matmul(
                                    p[:, h2 * 256:(h2 + 1) * 256],
                                    wsl(d, o * 128, (o + 1) * 128),
                                    xq_sl(xqn, d, 0, 256),
                                    start=(d == 0), stop=(d == 7)),
                                [f"xqG{xqn}", wname], [_nm(pp)])
                    ivr = invsb[:, gidx * 256:(gidx + 1) * 256].rearrange(
                        "p (a t) -> p a t", a=1).to_broadcast([128, 2, 256])
                    cop("vector", lambda e, o2=op2, p=pp, iv=ivr:
                        e.scalar_tensor_tensor(
                            out=outb[:, o2 * 512:(o2 + 1) * 512].rearrange(
                                "p (a t) -> p a t", t=256),
                            in0=p[:, 0:512].rearrange("p (a t) -> p a t", t=256),
                            scalar=amrep[:, widx:widx + 1],
                            in1=iv, op0=AL.mult, op1=AL.mult),
                        [_nm(pp), amrep[:], invsb[:]], [outb[:]])

            proj_fm(GQ, wq_sl, "T32b", qT_b, IWQ, GQ)
            proj_fm(GK, wk_sl, "T32b", kT_b, IWK, GK)
            for dc in range(8):
                cop("tensor", lambda e, d=dc: e.matmul(
                    pB[0:16, 0:256], wgk1_i[:, d * 16:(d + 1) * 16],
                    xq_sl(GGK1, d, 0, 256), start=(d == 0), stop=(d == 7)),
                    [f"xqG{GGK1}", "wgk1_i"], [pB[:]])
            cop("vector", lambda e: e.scalar_tensor_tensor(
                out=t1b[0:16, :], in0=pB[0:16, 0:256],
                scalar=amrep[0:16, IGK1:IGK1 + 1],
                in1=invsb[0:16, GGK1 * 256:(GGK1 + 1) * 256],
                op0=AL.mult, op1=AL.mult), [pB[:], amrep[:], invsb[:]], [t1b[:]])
            for tc in range(2):
                for nv in range(4):
                    pp = pA if nv % 2 == 0 else pB
                    for dc in range(8):
                        cop("tensor", lambda e, t=tc, n=nv, d=dc, p=pp: e.matmul(
                            p[:, 0:512], xq_sl(GV, d, t * 128, (t + 1) * 128),
                            wv_sl(d, n * 512, (n + 1) * 512),
                            start=(d == 0), stop=(d == 7)),
                            [f"xqG{GV}", "T32a"], [_nm(pp)])
                    cop("vector", lambda e, t=tc, n=nv, p=pp: e.tensor_scalar(
                        out=v_b[:, t * 2048 + n * 512:t * 2048 + (n + 1) * 512],
                        in0=p[:, 0:512],
                        scalar1=statb[:, t * 24 + 13 + GV:t * 24 + 14 + GV],
                        scalar2=amrep[:, IWV:IWV + 1], op0=AL.mult, op1=AL.mult),
                        [_nm(pp), statb[:], amrep[:]], [v_b[:]])
            S.add("sync", lambda e, t0_=t0: e.dma_start(
                out=xqg_scr.rearrange("(c p) t -> p c t", p=128)[:, :, t0_:t0_ + SC],
                in_=T32c[:, GG * 2048:(GG + 1) * 2048].rearrange(
                    "p (c t) -> p c t", t=256)),
                ins=[f"xqG{GG}"], outs=["xqg_scr"], dma="st_xqg")

        def a1_heavy_b(sc, nxt):
            t0 = sc * SC

            def pop(k):
                for _ in range(k):
                    if nxt:
                        nxt.pop(0)()

            pop(2)
            for tt in range(2):
                cop("tensor", lambda e, t=tt: e.transpose(
                    pT0[:, t * 128:(t + 1) * 128],
                    t1b[:, t * 128:(t + 1) * 128], identf[:]),
                    [t1b[:], identf[:]], [_nm(pT0)])
            mchains = []
            for tt in range(2):
                c0 = tt * 18
                X = xq2tm if tt == 0 else gscr
                Xn = xq2tm[:] if tt == 0 else "gscr"
                ops = [
                    ("vector", lambda e, c=c0, t=tt: e.tensor_copy(
                        out=t1tm[:, c:c + 16],
                        in_=pT0[:, t * 128:t * 128 + 16]),
                     [_nm(pT0)], [t1tm[:]]),
                    ("scalar", lambda e, c=c0, x=X: e.activation(
                        out=x[:, 0:16], in_=t1tm[:, c:c + 16], func=AF.Square,
                        bias=0.0, scale=1.0), [t1tm[:]], [Xn]),
                    ("vector", lambda e, c=c0, x=X: e.reduce_sum(
                        out=t1tm[:, c + 16:c + 17], in_=x[:, 0:16], axis=AX),
                     [Xn], [t1tm[:]]),
                    ("scalar", lambda e, c=c0: e.activation(
                        out=t1tm[:, c + 16:c + 17], in_=t1tm[:, c + 16:c + 17],
                        func=AF.Sqrt, bias=swrep[:, 7:8], scale=1.0 / 16.0),
                     [t1tm[:], swrep[:]], [t1tm[:]]),
                    ("vector", lambda e, c=c0: e.reciprocal(
                        out=t1tm[:, c + 17:c + 18], in_=t1tm[:, c + 16:c + 17]),
                     [t1tm[:]], [t1tm[:]]),
                    ("vector", lambda e, c=c0, x=X: e.tensor_tensor(
                        out=x[:, 0:16], in0=t1tm[:, c:c + 16], in1=ngk2bb[:],
                        op=AL.mult), [t1tm[:], ngk2bb[:]], [Xn]),
                    ("vector", lambda e, x=X: e.reduce_max(
                        out=x[:, 16:17], in_=x[:, 0:16], axis=AX,
                        apply_absolute_value=True), [Xn], [Xn]),
                    ("vector", lambda e, c=c0, x=X: e.tensor_scalar(
                        out=x[:, 16:17], in0=x[:, 16:17],
                        scalar1=t1tm[:, c + 17:c + 18], scalar2=1e-5,
                        op0=AL.mult, op1=AL.max), [Xn, t1tm[:]], [Xn]),
                    ("vector", lambda e, x=X: e.tensor_scalar(
                        out=x[:, 18:19], in0=x[:, 16:17], scalar1=1.0 / 127.0,
                        scalar2=None, op0=AL.mult), [Xn], [Xn]),
                    ("vector", lambda e, x=X: e.reciprocal(
                        out=x[:, 17:18], in_=x[:, 16:17]), [Xn], [Xn]),
                    ("vector", lambda e, c=c0, x=X: e.tensor_scalar(
                        out=x[:, 17:18], in0=x[:, 17:18],
                        scalar1=t1tm[:, c + 17:c + 18], scalar2=127.0,
                        op0=AL.mult, op1=AL.mult), [Xn, t1tm[:]], [Xn]),
                    ("vector", lambda e, x=X: e.tensor_scalar(
                        out=x[:, 0:16], in0=x[:, 0:16],
                        scalar1=x[:, 17:18], scalar2=MAGIC,
                        op0=AL.mult, op1=AL.add), [Xn], [Xn]),
                    ("vector", lambda e, x=X: e.tensor_scalar(
                        out=x[:, 0:16], in0=x[:, 0:16], scalar1=MAGIC,
                        scalar2=None, op0=AL.subtract), [Xn], [Xn]),
                ]
                mchains.append(ops)
            for i in range(13):
                for tt in range(2):
                    eng, fn, ri, ro = mchains[tt][i]
                    cop(eng, fn, ri, ro)
            for tt in range(2):
                X = xq2tm if tt == 0 else gscr
                Xn = xq2tm[:] if tt == 0 else "gscr"
                cop("tensor", lambda e, x=X: e.transpose(
                    pT0[0:16, 0:128], x[:, 0:16], identf[:]),
                    [Xn, identf[:]], [_nm(pT0)])
                cop("vector", lambda e, t=tt: e.tensor_copy(
                    out=xq2b[:, t * 128:(t + 1) * 128], in_=pT0[0:16, 0:128]),
                    [_nm(pT0)], [xq2b[:]])
                row_extract(X[:, 18:19], Xn, tt, 256, P=pT0)
            cop("scalar", lambda e: e.activation(
                out=i2row[0:1, :], in_=pT0[0:1, 256:512], func=AF.Copy,
                bias=0.0, scale=1.0), [_nm(pT0)], [i2row[:]])
            cop("tensor", lambda e: e.matmul(pT1[:, 0:256], ones_r[:],
                                             i2row[0:1, :], start=True, stop=True),
                [i2row[:], ones_r[:]], [_nm(pT1)])
            cop("scalar", lambda e: e.activation(out=t1b[:], in_=pT1[:, 0:256],
                                                 func=AF.Copy, bias=0.0, scale=1.0),
                [_nm(pT1)], [t1b[:]])
            pop(1)
            for oc in range(8):
                pp = pA if oc % 2 == 0 else pB
                cop("tensor", lambda e, o=oc, p=pp: e.matmul(
                    p[:, 0:256], wgk2_i[0:16, o * 128:(o + 1) * 128], xq2b[:],
                    start=True, stop=True), ["wgk2_i", xq2b[:]], [_nm(pp)])
                cop("vector", lambda e, p=pp: e.scalar_tensor_tensor(
                    out=gscr[:], in0=p[:, 0:256], scalar=amrep[:, IGK2:IGK2 + 1],
                    in1=t1b[:], op0=AL.mult, op1=AL.mult),
                    [_nm(pp), amrep[:], t1b[:]], [gscr[:]])
                cop("scalar", lambda e, o=oc: e.activation(
                    out=gkT_b[:, o * 256:(o + 1) * 256], in_=gscr[:],
                    func=AF.Sigmoid, bias=bgk2b[:, o:o + 1], scale=1.0),
                    [gscr[:], bgk2b[:]], [gkT_b[:]])
            for oc in range(8):
                cop("scalar", lambda e, o=oc: e.activation(
                    out=gkT_b[:, o * 256:(o + 1) * 256],
                    in_=gkT_b[:, o * 256:(o + 1) * 256], func=AF.Ln,
                    bias=0.0, scale=1.0), [gkT_b[:]], [gkT_b[:]])
                cop("vector", lambda e, o=oc: e.tensor_tensor_scan(
                    out=gkT_b[:, o * 256:(o + 1) * 256],
                    data0=gkT_b[:, o * 256:(o + 1) * 256],
                    data1=gkT_b[:, o * 256:(o + 1) * 256],
                    initial=0.0, op0=AL.add, op1=AL.bypass),
                    [gkT_b[:]], [gkT_b[:]])
            cumT = gkT_b
            pop(1)

            def qb_sl(kcg, a, b):
                return T32c[:, 10240 + kcg * 256 + a:10240 + kcg * 256 + b]

            def kb_sl(kcg, a, b):
                return T32c[:, 12288 + kcg * 256 + a:12288 + kcg * 256 + b]

            def kbt_sl(kcg, a, b):
                return T32c[:, 14336 + kcg * 256 + a:14336 + kcg * 256 + b]

            for kcg in range(8):
                cop("scalar", lambda e, k=kcg: e.activation(
                    out=eclb[:, k:k + 1],
                    in_=cumT[:, k * 256 + 255:k * 256 + 256],
                    func=AF.Exp, bias=0.0, scale=1.0 / 16.0),
                    [cumT[:]], [eclb[:]])
                for c in range(2):
                    cs_ = cumT[:, kcg * 256 + c * 128:kcg * 256 + (c + 1) * 128]
                    ep = expP[:, c * 128:(c + 1) * 128]
                    em = expM[:, c * 128:(c + 1) * 128]
                    cop("scalar", lambda e, i=cs_, o=ep: e.activation(
                        out=o, in_=i, func=AF.Exp, bias=0.0, scale=1.0 / 16.0),
                        [cumT[:]], [expP[:]])
                    cop("scalar", lambda e, i=cs_, o=em: e.activation(
                        out=o, in_=i, func=AF.Exp, bias=0.0, scale=-1.0 / 16.0),
                        [cumT[:]], [expM[:]])
                    cop("vector", lambda e, k=kcg, cc=c, o=ep: e.tensor_tensor(
                        out=qb_sl(k, cc * 128, (cc + 1) * 128),
                        in0=qT_b[:, k * 256 + cc * 128:k * 256 + (cc + 1) * 128],
                        in1=o, op=AL.mult),
                        [qT_b[:], expP[:]], ["scnQ"])
                    cop("vector", lambda e, k=kcg, cc=c, o=em: e.tensor_tensor(
                        out=kb_sl(k, cc * 128, (cc + 1) * 128),
                        in0=kT_b[:, k * 256 + cc * 128:k * 256 + (cc + 1) * 128],
                        in1=o, op=AL.mult),
                        [kT_b[:], expM[:]], ["scnK"])
                    cop("vector", lambda e, k=kcg, cc=c: e.tensor_scalar(
                        out=qg_st[:, k * 256 + cc * 128:k * 256 + (cc + 1) * 128],
                        in0=qb_sl(k, cc * 128, (cc + 1) * 128),
                        scalar1=egsum[:, k:k + 1], scalar2=None, op0=AL.mult),
                        ["scnQ", egsum[:]], [qg_st[:]])
                    par = (kcg * 2 + c) % 2
                    cop("tensor", lambda e, k=kcg, cc=c, p=par: e.transpose(
                        pBT[:, p * 256 + cc * 0:p * 256 + 128],
                        kb_sl(k, cc * 128, (cc + 1) * 128), identb[:]),
                        ["scnK", identb[:]], [f"pBT{par}"])
                    cop("scalar", lambda e, k=kcg, cc=c, p=par: e.activation(
                        out=kbt_sl(k, cc * 128, (cc + 1) * 128),
                        in_=pBT[:, p * 256:p * 256 + 128], func=AF.Copy,
                        bias=0.0, scale=1.0), [f"pBT{par}"], ["scnKT"])
            for h in range(HEADS):
                pop(2 if h < 2 else (2 if h == 2 else 1))
                for kc in range(2):
                    kcg = h * 2 + kc
                    pT = pT0 if kc == 0 else pT1
                    cop("tensor", lambda e, k=kcg, p=pT: e.matmul(
                        p[:], identb[:], Scar[:, k * 512:(k + 1) * 512],
                        start=True, stop=False), [identb[:], "Scar"], [_nm(pT)])
                for c in range(2):
                    hp = (h * 2 + c) % 2
                    atm = ATm if hp == 0 else ATm2
                    for kc in range(2):
                        kcg = h * 2 + kc
                        cop("tensor", lambda e, k=kcg, cc=c, p=hp,
                            st=(kc == 0), sp=(kc == 1): e.matmul(
                            pAT[:, p * 128:(p + 1) * 128],
                            kb_sl(k, cc * 128, (cc + 1) * 128),
                            qb_sl(k, cc * 128, (cc + 1) * 128),
                            start=st, stop=sp),
                            ["scnK", "scnQ"], [f"pAT{hp}"])
                    cop("vector", lambda e, p=hp, a=atm: e.tensor_tensor(
                        out=a[:], in0=pAT[:, p * 128:(p + 1) * 128],
                        in1=maskUb[:], op=AL.mult),
                        [f"pAT{hp}", maskUb[:]], [_nm(atm)])
                    for vc in range(4):
                        cop("tensor", lambda e, v=vc, cc=c, h_=h, a=atm: e.matmul(
                            pO[:, v * 128:(v + 1) * 128],
                            v_b[:, cc * 2048 + h_ * 512 + v * 128:
                                cc * 2048 + h_ * 512 + (v + 1) * 128],
                            a[:], start=True, stop=False),
                            [v_b[:], _nm(atm)], [pO[:]])
                        for kc in range(2):
                            kcg = h * 2 + kc
                            if c == 0:
                                lhs = Scar[:, kcg * 512 + vc * 128:
                                           kcg * 512 + (vc + 1) * 128]
                                ln = "Scar"
                            else:
                                lhs = T0e[:, kc * 512 + vc * 128:
                                          kc * 512 + (vc + 1) * 128]
                                ln = "T0e"
                            cop("tensor", lambda e, v=vc, k=kcg, cc=c, L=lhs,
                                last=(kc == 1): e.matmul(
                                    pO[:, v * 128:(v + 1) * 128], L,
                                    qb_sl(k, cc * 128, (cc + 1) * 128),
                                    start=False, stop=last),
                                [ln, "scnQ"], [pO[:]])
                    cop("scalar", lambda e, h_=h, cc=c: e.activation(
                        out=o_st[:].rearrange("p (v t) -> p v t", t=256)
                        [:, h_ * 4:(h_ + 1) * 4, cc * 128:(cc + 1) * 128],
                        in_=pO[:].rearrange("p (v t) -> p v t", t=128),
                        func=AF.Copy, bias=0.0, scale=1.0), [pO[:]], [o_st[:]])
                    for kc in range(2):
                        pT = pT0 if kc == 0 else pT1
                        kcg = h * 2 + kc
                        cop("tensor", lambda e, k=kcg, cc=c, h_=h, p=pT,
                            last=(c == 1): e.matmul(
                                p[:], kbt_sl(k, cc * 128, (cc + 1) * 128),
                                v_b[:, cc * 2048 + h_ * 512:
                                    cc * 2048 + (h_ + 1) * 512],
                                start=False, stop=last),
                            ["scnKT", v_b[:]], [_nm(pT)])
                    if c == 0:
                        for kc in range(2):
                            pT = pT0 if kc == 0 else pT1
                            cop("scalar", lambda e, k=kc, p=pT: e.activation(
                                out=T0e[:, k * 512:(k + 1) * 512], in_=p[:],
                                func=AF.Copy, bias=0.0, scale=1.0),
                                [_nm(pT)], ["T0e"])
                for kc in range(2):
                    pT = pT0 if kc == 0 else pT1
                    kcg = h * 2 + kc
                    cop("scalar", lambda e, k=kcg, p=pT: e.activation(
                        out=Scar[:, k * 512:(k + 1) * 512], in_=p[:], func=AF.Copy,
                        bias=0.0, scale=eclb[:, k:k + 1]),
                        [_nm(pT), eclb[:]], ["Scar"])
                    cop("vector", lambda e, k=kcg: e.tensor_tensor(
                        out=egsum[:, k:k + 1], in0=egsum[:, k:k + 1],
                        in1=eclb[:, k:k + 1], op=AL.mult),
                        [egsum[:], eclb[:]], [egsum[:]])
            dma("sync",
                o_scr.rearrange("(c p) t -> p c t", p=128)[:, :, t0:t0 + SC],
                o_st[:].rearrange("p (c t) -> p c t", t=256), "st_o")
            dma("sync",
                qg_scr.rearrange("(c p) t -> p c t", p=128)[:, :, t0:t0 + SC],
                qg_st[:].rearrange("p (c t) -> p c t", t=256), "st_qg")
            pop(len(nxt))
            emit_cast_units(8 if sc < 7 else 1000)
            tap(f"a1s{sc}")

        for f in a1_spine_chunks(0):
            f()
        for sc in range(NSC):
            a1_heavy_a(sc)
            nxt = a1_spine_chunks(sc + 1) if sc < NSC - 1 else []
            a1_heavy_b(sc, nxt)

        cop("gpsimd", lambda e: e.memset(sgb[0:1, 0:1], 0.0),
            ["sgbH0", "sgbH1"], [sgb[:]])

        dma("sync", cc_in.rearrange("(c p) v -> p c v", p=128),
            Scar[:].rearrange("p (c v) -> p c v", v=512), "st_Scar")
        S.add("gpsimd", lambda e: e.collective_compute(
            "AllGather", AL.bypass, [[0, 1], [2, 3], [4, 5], [6, 7]],
            ins=[cc_in.tensor.ap().opt()], outs=[cc_out.tensor.ap().opt()]),
            ins=["cc_in"], outs=["cc_out"], dma="cc")
        Sin = Scar
        dma("gpsimd", Sin[:].rearrange("p (c v) -> p c v", v=512),
            cc_out.rearrange("(c p) v -> p c v", p=128)[:, 0:8], "ld_Sin")
        tap("cc")

        dma("scalar", T32a[:].rearrange("p (c t) -> p c t", t=2048),
            wg_scr.rearrange("(c p) t -> p c t", p=128), "T32a")
        dma("scalar", T32b[:].rearrange("p (c t) -> p c t", t=1024),
            wo_scr.rearrange("(c p) t -> p c t", p=128), "T32b")

        def wg_sl(dc, a, b):
            return T32a[:, dc * DVT + a:dc * DVT + b]

        def wo_sl(vcg, a, b):
            return T32b[:, vcg * D + a:vcg * D + b]

        ot_bufs = (o_st, v_b)
        qgt_bufs = (qg_st, qT_b)
        xqgt_bufs = (kT_b, xqg2)

        def emit_ln_apply(src_sl2, src_name, dst_writer):
            cop("scalar", lambda e: e.activation(
                out=srow[0:1, 0:256], in_=pS[0:1, 0:256], func=AF.Copy,
                bias=0.0, scale=1.0 / 1024.0), [pS[:]], [srow[:]])
            cop("scalar", lambda e: e.activation(
                out=t1b[0:1, 0:256], in_=pS[0:1, 256:512], func=AF.Copy,
                bias=0.0, scale=1.0 / 1024.0), [pS[:]], [t1b[:]])
            GB = (gsml, t1tm)
            GN = (gsml[:], t1tm[:])
            PP = (pT0, pT1)
            for tt in range(2):
                cop("tensor", lambda e, t=tt, p=PP[tt]: e.transpose(
                    p[:, 0:128], srow[:, t * 128:(t + 1) * 128], identf[:]),
                    [srow[:], identf[:]], [_nm(PP[tt])])
                cop("tensor", lambda e, t=tt, p=PP[tt]: e.transpose(
                    p[:, 128:256], t1b[:, t * 128:(t + 1) * 128], identf[:]),
                    [t1b[:], identf[:]], [_nm(PP[tt])])
            lops = []
            for tt in range(2):
                g, gn, p = GB[tt], GN[tt], PP[tt]
                pn = _nm(p)
                lops.append([
                    ("vector", lambda e, g=g, p=p: e.tensor_copy(
                        out=g[:, 0:1], in_=p[:, 0:1]), [pn], [gn]),
                    ("vector", lambda e, g=g, p=p: e.tensor_copy(
                        out=g[:, 1:2], in_=p[:, 128:129]), [pn], [gn]),
                    ("scalar", lambda e, g=g: e.activation(
                        out=g[:, 2:3], in_=g[:, 0:1], func=AF.Square,
                        bias=0.0, scale=1.0), [gn], [gn]),
                    ("vector", lambda e, g=g: e.tensor_tensor(
                        out=g[:, 3:4], in0=g[:, 1:2], in1=g[:, 2:3],
                        op=AL.subtract), [gn], [gn]),
                    ("scalar", lambda e, g=g: e.activation(
                        out=g[:, 3:4], in_=g[:, 3:4], func=AF.Sqrt,
                        bias=amrep[:, 7:8], scale=1.0), [gn, amrep[:]], [gn]),
                    ("vector", lambda e, g=g: e.reciprocal(
                        out=g[:, 4:5], in_=g[:, 3:4]), [gn], [gn]),
                    ("vector", lambda e, g=g: e.tensor_tensor(
                        out=g[:, 5:6], in0=g[:, 0:1], in1=g[:, 4:5],
                        op=AL.mult), [gn], [gn]),
                    ("vector", lambda e, g=g: e.tensor_scalar(
                        out=g[:, 5:6], in0=g[:, 5:6], scalar1=-1.0,
                        scalar2=None, op0=AL.mult), [gn], [gn]),
                ])
            for i in range(8):
                for tt in range(2):
                    eng, fn, ri, ro = lops[tt][i]
                    cop(eng, fn, ri, ro)
            for tt in range(2):
                g, gn = GB[tt], GN[tt]
                cop("tensor", lambda e, t=tt, g=g, gn=gn: e.transpose(
                    pT0[0:1, 256 + t * 128:256 + (t + 1) * 128], g[:, 4:5],
                    identf[:]), [gn, identf[:]], [_nm(pT0)])
                cop("tensor", lambda e, t=tt, g=g, gn=gn: e.transpose(
                    pT1[0:1, 256 + t * 128:256 + (t + 1) * 128], g[:, 5:6],
                    identf[:]), [gn, identf[:]], [_nm(pT1)])
            cop("scalar", lambda e: e.activation(
                out=abr[0:1, 0:256], in_=pT0[0:1, 256:512], func=AF.Copy,
                bias=0.0, scale=1.0), [_nm(pT0)], [abr[:]])
            cop("scalar", lambda e: e.activation(
                out=abr[0:1, 256:512], in_=pT1[0:1, 256:512], func=AF.Copy,
                bias=0.0, scale=1.0), [_nm(pT1)], [abr[:]])
            cop("tensor", lambda e: e.matmul(
                pT0[:, 0:256], ones_r[:], abr[0:1, 0:256], start=True, stop=True),
                [abr[:], ones_r[:]], [_nm(pT0)])
            cop("scalar", lambda e: e.activation(
                out=Abc[:], in_=pT0[:, 0:256], func=AF.Copy, bias=0.0, scale=1.0),
                [_nm(pT0)], ["UbufG1"])
            cop("tensor", lambda e: e.matmul(
                pT1[:, 0:256], ones_r[:], abr[0:1, 256:512], start=True, stop=True),
                [abr[:], ones_r[:]], [_nm(pT1)])
            cop("scalar", lambda e: e.activation(
                out=Bbc[:], in_=pT1[:, 0:256], func=AF.Copy, bias=0.0, scale=1.0),
                [_nm(pT1)], ["UbufG2"])
            abr_ = Abc.rearrange("p (a t) -> p a t", a=1).to_broadcast(
                [128, 2, 256])
            for d2 in range(4):
                cop("vector", lambda e, d=d2, ab=abr_: e.tensor_tensor(
                    out=rscrw[:].rearrange("p (a t) -> p a t", t=256),
                    in0=src_sl2(d), in1=ab, op=AL.mult),
                    [src_name, "UbufG1"], ["rscrw"])
                dst_writer(d2)

        def a2_vars(sc):
            par = sc % 2
            return (sc * SC, par, ot_bufs[par], qgt_bufs[par], xqgt_bufs[par],
                    xt0 if par == 0 else xt1)

        def a2_front(sc, weave):
            t0, par, ot, qgt, xqgt, xt = a2_vars(sc)

            def wv(k):
                for _ in range(k):
                    if weave:
                        weave.pop(0)()

            dma("sync", ot[:].rearrange("p (c t) -> p c t", t=256),
                o_scr.rearrange("(c p) t -> p c t", p=128)[:, :, t0:t0 + SC],
                _nm(ot))
            dma("sync", qgt[:].rearrange("p (c t) -> p c t", t=256),
                qg_scr.rearrange("(c p) t -> p c t", p=128)[:, :, t0:t0 + SC],
                _nm(qgt))
            dma("sync", xqgt[:].rearrange("p (c t) -> p c t", t=256),
                xqg_scr.rearrange("(c p) t -> p c t", p=128)[:, :, t0:t0 + SC],
                _nm(xqgt))
            dma("sync", xt[:].rearrange("p (c t) -> p c t", t=256),
                xT.rearrange("(c p) t -> p c t", p=128)[:, :, t0:t0 + SC],
                "a2x0" if par == 0 else "a2x1")

            def corr_part():
                for h in range(HEADS):
                    for vc in range(4):
                        vcg = h * 4 + vc
                        pp = pT0 if vc < 2 else pT1
                        ph = (vc % 2) * 256
                        cop("tensor", lambda e, v=vcg, p=pp, o=ot, q=ph: e.matmul(
                            p[:, q:q + 256], identb[:],
                            o[:, v * 256:(v + 1) * 256],
                            start=True, stop=False), [identb[:], ot[:]], [_nm(pp)])
                        for kc in range(2):
                            kcg = h * 2 + kc
                            cop("tensor", lambda e, k=kcg, v=vc, p=pp, q=qgt,
                                g=ph, last=(kc == 1): e.matmul(
                                    p[:, g:g + 256],
                                    Sin[:, k * 512 + v * 128:
                                        k * 512 + (v + 1) * 128],
                                    q[:, k * 256:(k + 1) * 256],
                                    start=False, stop=last),
                                ["Scar", qgt[:]], [_nm(pp)])
                        if vc % 2 == 1:
                            cop("scalar", lambda e, v=vcg, p=pp, o=ot:
                                e.activation(
                                    out=o[:, (v - 1) * 256:(v + 1) * 256],
                                    in_=p[:, 0:512], func=AF.Copy, bias=0.0,
                                    scale=1.0), [_nm(pp)], [ot[:]])
                    wv(1)
                wv(1)
            if sc > 0:
                corr_part()
            dma("sync", i2row[0:1, :], invsg_scr[sc:sc + 1, :], "i2row")
            i2a = Ubuf[:, 768:1024]
            cop("tensor", lambda e: e.matmul(
                pS[:, 0:256], ones_r[:], i2row[0:1, :], start=True, stop=True),
                [i2row[:], ones_r[:]], [pS[:]])
            cop("scalar", lambda e, o=i2a: e.activation(out=o, in_=pS[:, 0:256],
                                                 func=AF.Copy, bias=0.0, scale=1.0),
                [pS[:]], ["UbufG3"])
            for op2 in range(8):
                pp = pA if op2 % 2 == 0 else pB
                for oh in range(2):
                    ocg = op2 * 2 + oh
                    for dc in range(8):
                        cop("tensor", lambda e, o=ocg, d=dc, p=pp, x=xqgt,
                            h2=oh: e.matmul(
                                p[:, h2 * 256:(h2 + 1) * 256],
                                wg_sl(d, o * 128, (o + 1) * 128),
                                x[:, d * 256:(d + 1) * 256],
                                start=(d == 0), stop=(d == 7)),
                            ["T32a", xqgt[:]], [_nm(pp)])
                for oh in range(2):
                    gs = Ubuf[:, 1024:1280] if oh == 0 else gscr[:]
                    gn = "UbufG4" if oh == 0 else "gscr"
                    cop("vector", lambda e, p=pp, h2=oh, g=gs, iv=i2a:
                        e.scalar_tensor_tensor(
                            out=g, in0=p[:, h2 * 256:(h2 + 1) * 256],
                            scalar=amrep[:, IWG:IWG + 1], in1=iv,
                            op0=AL.mult, op1=AL.mult),
                        [_nm(pp), amrep[:], "UbufG3"], [gn])
                    cop("scalar", lambda e, o2=op2, h2=oh, g=gs: e.activation(
                        out=sgb[:, o2 * 512 + h2 * 256:o2 * 512 + (h2 + 1) * 256],
                        in_=g, func=AF.Silu, bias=0.0, scale=1.0),
                        [gn], [sgb[:]])
                if op2 <= 4:
                    wv(1)
            if sc == 0:
                corr_part()
            for v2 in range(8):
                cop("vector", lambda e, v=v2, o=ot: e.tensor_tensor(
                    out=o[:, v * 512:(v + 1) * 512],
                    in0=o[:, v * 512:(v + 1) * 512],
                    in1=sgb[:, v * 512:(v + 1) * 512], op=AL.mult),
                    [ot[:], sgb[:]], [ot[:]])
            wv(len(weave))

        def a2_back_chunks(sc):
            t0, par, ot, qgt, xqgt, xt = a2_vars(sc)
            out = []

            def mk_zf(q):
                def zf():
                    for vcg in range(q * 4, q * 4 + 4):
                        if vcg == 0:
                            cop("vector", lambda e, o=ot: e.tensor_scalar(
                                out=mfold[:, 0:256], in0=o[:, 0:256],
                                scalar1=nob[:, 0:1], scalar2=None, op0=AL.mult),
                                [ot[:], nob[:]], [mfold[:]])
                            cop("vector", lambda e: e.tensor_copy(
                                out=mfold[:, 256:512], in_=mfold[:, 0:256]),
                                [mfold[:]], [mfold[:]])
                        else:
                            cop("vector", lambda e, o=ot, v=vcg:
                                e.scalar_tensor_tensor(
                                    out=mfold[:, 0:256],
                                    in0=o[:, v * 256:(v + 1) * 256],
                                    scalar=nob[:, v:v + 1], in1=mfold[:, 0:256],
                                    op0=AL.mult, op1=AL.max),
                                [ot[:], nob[:], mfold[:]], [mfold[:]])
                            cop("vector", lambda e, o=ot, v=vcg:
                                e.scalar_tensor_tensor(
                                    out=mfold[:, 256:512],
                                    in0=o[:, v * 256:(v + 1) * 256],
                                    scalar=nob[:, v:v + 1], in1=mfold[:, 256:512],
                                    op0=AL.mult, op1=AL.min),
                                [ot[:], nob[:], mfold[:]], [mfold[:]])
                        cop("scalar", lambda e, o=ot, v=vcg: e.activation(
                            out=gscr[:], in_=o[:, v * 256:(v + 1) * 256],
                            func=AF.Square, bias=0.0, scale=1.0),
                            [ot[:]], ["gscr"])
                        cop("tensor", lambda e, first=(vcg == 0), last=(vcg == 15):
                            e.matmul(pS[0:1, 0:256], ones_c[:], gscr[:],
                                     start=first, stop=last),
                            ["gscr", ones_c[:]], [pS[:]])
                return zf

            def zstat():
                cop("vector", lambda e: e.tensor_scalar(
                    out=mfold[:, 256:512], in0=mfold[:, 256:512], scalar1=-1.0,
                    scalar2=None, op0=AL.mult), [mfold[:]], [mfold[:]])
                cop("vector", lambda e: e.tensor_tensor(
                    out=mfold[:, 0:256], in0=mfold[:, 0:256],
                    in1=mfold[:, 256:512], op=AL.max), [mfold[:]], [mfold[:]])
                cop("scalar", lambda e: e.activation(
                    out=srow[0:1, :], in_=pS[0:1, 0:256], func=AF.Copy,
                    bias=0.0, scale=1.0), [pS[:]], [srow[:]])
                for tt in range(2):
                    sb_ = statb[:, tt * 24:(tt + 1) * 24]
                    cop("tensor", lambda e, t=tt: e.transpose(
                        pS[:, 0:128], srow[:, t * 128:(t + 1) * 128], identf[:]),
                        [srow[:], identf[:]], [pS[:]])
                    cop("vector", lambda e, s=sb_: e.tensor_copy(
                        out=s[:, 0:1], in_=pS[:, 0:1]), [pS[:]], [statb[:]])
                    cop("tensor", lambda e, t=tt: e.transpose(
                        pS[:, 0:128], mfold[:, t * 128:(t + 1) * 128], identf[:]),
                        [mfold[:], identf[:]], [pS[:]])
                    cop("vector", lambda e, s=sb_: e.reduce_max(
                        out=s[:, 1:2], in_=pS[:, 0:128], axis=AX),
                        [pS[:]], [statb[:]])
                    cop("scalar", lambda e, s=sb_: e.activation(
                        out=s[:, 6:7], in_=s[:, 0:1], func=AF.Sqrt,
                        bias=swrep[:, 7:8], scale=1.0 / 2048.0),
                        [statb[:], swrep[:]], [statb[:]])
                    cop("vector", lambda e, s=sb_: e.reciprocal(
                        out=s[:, 7:8], in_=s[:, 6:7]), [statb[:]], [statb[:]])
                    cop("vector", lambda e, s=sb_: e.tensor_scalar(
                        out=s[:, 8:9], in0=s[:, 1:2], scalar1=s[:, 7:8],
                        scalar2=1e-5, op0=AL.mult, op1=AL.max),
                        [statb[:]], [statb[:]])
                    cop("vector", lambda e, s=sb_: e.tensor_scalar(
                        out=s[:, 13:14], in0=s[:, 8:9], scalar1=1.0 / 127.0,
                        scalar2=None, op0=AL.mult), [statb[:]], [statb[:]])
                    cop("vector", lambda e, s=sb_: e.reciprocal(
                        out=s[:, 18:19], in_=s[:, 8:9]), [statb[:]], [statb[:]])
                    cop("vector", lambda e, s=sb_: e.tensor_scalar(
                        out=s[:, 18:19], in0=s[:, 18:19], scalar1=s[:, 7:8],
                        scalar2=127.0, op0=AL.mult, op1=AL.mult),
                        [statb[:]], [statb[:]])
            out = [mk_zf(q) for q in range(4)] + [zstat]

            def zext():
                for tt in range(2):
                    row_extract(statb[:, tt * 24 + 18:tt * 24 + 19],
                                "statb", tt, 0)
                    row_extract(statb[:, tt * 24 + 13:tt * 24 + 14],
                                "statb", tt, 256)
                cop("scalar", lambda e: e.activation(
                    out=urow5[0:1, 0:256], in_=pS[0:1, 0:256], func=AF.Copy,
                    bias=0.0, scale=1.0), [pS[:]], [urow5[:]])
                cop("scalar", lambda e: e.activation(
                    out=irow5[0:1, 0:256], in_=pS[0:1, 256:512], func=AF.Copy,
                    bias=0.0, scale=1.0), [pS[:]], [irow5[:]])
                cop("tensor", lambda e: e.matmul(
                    pS[:, 0:256], ones_r[:], urow5[0:1, 0:256],
                    start=True, stop=True), [urow5[:], ones_r[:]], [pS[:]])
                cop("scalar", lambda e: e.activation(
                    out=Ubuf[:, 0:256], in_=pS[:, 0:256], func=AF.Copy, bias=0.0,
                    scale=1.0), [pS[:]], ["UbufG0"])
                cop("tensor", lambda e: e.matmul(
                    pS[:, 0:256], ones_r[:], irow5[0:1, 0:256],
                    start=True, stop=True), [irow5[:], ones_r[:]], [pS[:]])
                cop("scalar", lambda e: e.activation(
                    out=invsb[:, 0:256], in_=pS[:, 0:256], func=AF.Copy, bias=0.0,
                    scale=1.0), [pS[:]], [invsb[:]])

            def mk_zr(q):
                def zr():
                    for vcg in range(q * 4, q * 4 + 4):
                        rs = rscr if vcg % 2 == 0 else rscr2
                        cop("vector", lambda e, v=vcg, o=ot, r=rs:
                            e.scalar_tensor_tensor(
                                out=r, in0=o[:, v * 256:(v + 1) * 256],
                                scalar=nob[:, v:v + 1], in1=Ubuf[:, 0:256],
                                op0=AL.mult, op1=AL.mult),
                            [ot[:], nob[:], "UbufG0"], ["rscrw"])
                        if vcg % 2 == 1:
                            cop("vector", lambda e, v=vcg, o=ot: e.tensor_scalar(
                                out=o[:, (v - 1) * 256:(v + 1) * 256],
                                in0=rscrw[:], scalar1=MAGIC, scalar2=MAGIC,
                                op0=AL.add, op1=AL.subtract),
                                ["rscrw"], [ot[:]])
                return zr

            return out + [zext] + [mk_zr(q) for q in range(4)]

        def a2_tail(sc):
            t0, par, ot, qgt, xqgt, xt = a2_vars(sc)
            for op2 in range(4):
                pp = pAT if op2 % 2 == 0 else pO
                for oh in range(2):
                    oc = op2 * 2 + oh
                    for vcg in range(16):
                        cop("tensor", lambda e, o=oc, v=vcg, p=pp, z=ot,
                            h2=oh: e.matmul(
                                p[:, h2 * 256:(h2 + 1) * 256],
                                wo_sl(v, o * 128, (o + 1) * 128),
                                z[:, v * 256:(v + 1) * 256],
                                start=(v == 0), stop=(v == 15)),
                            ["T32b", ot[:]], [_nm(pp)])
                ivr = invsb[:, 0:256].rearrange(
                    "p (a t) -> p a t", a=1).to_broadcast([128, 2, 256])
                cop("vector", lambda e, p=pp, iv=ivr: e.scalar_tensor_tensor(
                    out=rscrw[:].rearrange("p (a t) -> p a t", t=256),
                    in0=p[:, 0:512].rearrange("p (a t) -> p a t", t=256),
                    scalar=amrep[:, IWO:IWO + 1],
                    in1=iv, op0=AL.mult, op1=AL.mult),
                    [_nm(pp), amrep[:], invsb[:]], ["rscrw"])
                cop("vector", lambda e, o2=op2, x=xt: e.tensor_tensor(
                    out=gkT_b[:, o2 * 512:(o2 + 1) * 512], in0=rscrw[:],
                    in1=x[:, o2 * 512:(o2 + 1) * 512], op=AL.add),
                    ["rscrw", xt[:]], ["gkT_b"])
            for dc in range(8):
                cop("tensor", lambda e, d=dc: e.matmul(
                    pS[0:1, 0:256], ones_c[:], gkT_b[:, d * 256:(d + 1) * 256],
                    start=(d == 0), stop=(d == 7)),
                    ["gkT_b", ones_c[:]], [pS[:]])
            for d2 in range(4):
                cop("scalar", lambda e, d=d2: e.activation(
                    out=sqw[:], in_=gkT_b[:, d * 512:(d + 1) * 512],
                    func=AF.Square, bias=0.0, scale=1.0), ["gkT_b"], ["sqw"])
                cop("tensor", lambda e, d=d2: e.matmul(
                    pS[0:1, 256:512], ones_c[:], sqw[:, 0:256],
                    start=(d == 0), stop=False), ["sqw", ones_c[:]], [pS[:]])
                cop("tensor", lambda e, d=d2: e.matmul(
                    pS[0:1, 256:512], ones_c[:], sqw[:, 256:512],
                    start=False, stop=(d == 3)), ["sqw", ones_c[:]], [pS[:]])
            bbr_ = Bbc.rearrange("p (a t) -> p a t", a=1).to_broadcast(
                [128, 2, 256])
            emit_ln_apply(
                lambda d: gkT_b[:, d * 512:(d + 1) * 512].rearrange(
                    "p (a t) -> p a t", t=256), "gkT_b",
                lambda d2: cop("vector", lambda e, d=d2, t0_=t0, bb=bbr_:
                               e.tensor_tensor(
                    out=T32c[:].rearrange("p (c t) -> p c t", t=2048)
                    [:, 2 * d:2 * d + 2, t0_:t0_ + SC],
                    in0=rscrw[:].rearrange("p (a t) -> p a t", t=256),
                    in1=bb, op=AL.add),
                    ["rscrw", "UbufG2"], ["T32c"]))
            tap(f"a2s{sc}")

        a2_front(0, [])
        for sc in range(NSC):
            bch = a2_back_chunks(sc)
            if sc < NSC - 1:
                a2_front(sc + 1, bch)
            else:
                for f in bch:
                    f()
            a2_tail(sc)

        def xp_sl(dc, a, b):
            return T32c[:, dc * 2048 + a:dc * 2048 + b]

        def h_sl(unit, a, b):
            if unit < 16:
                return T32a[:, unit * 1024 + a:unit * 1024 + b]
            return T32b[:, (unit - 16) * 1024 + a:(unit - 16) * 1024 + b]

        def h_name(unit):
            return "T32a" if unit < 16 else "T32b"

        def ffnacc_sl(oc, a, b):
            return T32b[:, 8192 + oc * 1024 + a:8192 + oc * 1024 + b]

        dma("sync", rwb[:].rearrange("p (c t) -> p c t", t=16),
            s_rw.rearrange("(c p) t -> p c t", p=128), "rwb")
        GRPS = [list(range(0, 10)), list(range(10, 18))]
        lgb = mfold
        sgf = mfold

        for tile in range(2):
            tb = tile * 1024
            for ns in range(2):
                for dc in range(8):
                    cop("tensor", lambda e, n=ns, d=dc, tb_=tb: e.matmul(
                        pS[0:16, 0:512], rwb[:, d * 16:(d + 1) * 16],
                        xp_sl(d, tb_ + n * 512, tb_ + (n + 1) * 512),
                        start=(d == 0), stop=(d == 7)),
                        [rwb[:], "T32c"], [pS[:]])
                cop("scalar", lambda e, n=ns: e.activation(
                    out=lgb[0:16, n * 512:(n + 1) * 512], in_=pS[0:16, 0:512],
                    func=AF.Copy, bias=0.0, scale=1.0), [pS[:]], ["mfold"])
            Lr = rscrw[:, 0:128]
            Tr = rscrw[:, 128:256]
            smr = rscrw[:, 256:288]
            L3 = Lr.rearrange("p (a t) -> p a t", t=16)
            T3 = Tr.rearrange("p (a t) -> p a t", t=16)

            def sm3(c0):
                return smr[:, c0:c0 + 8].rearrange("p (a o) -> p a o", o=1)

            def smb(c0):
                return smr[:, c0:c0 + 8].rearrange(
                    "p (a o) -> p a o", o=1).to_broadcast([128, 8, 16])

            for tg in range(8):
                rot = tg % 4
                cop("tensor", lambda e, t=tg, r=rot: e.transpose(
                    pS[:, r * 128:(r + 1) * 128],
                    lgb[:, t * 128:(t + 1) * 128], identf[:]),
                    ["mfold", identf[:]], [pS[:]])
                cop("vector", lambda e, t=tg, r=rot: e.tensor_copy(
                    out=Lr[:, t * 16:(t + 1) * 16],
                    in_=pS[:, r * 128:r * 128 + 16]), [pS[:]], ["rscrw"])
            cop("vector", lambda e: e.reduce_max(
                out=sm3(0), in_=L3, axis=AX), ["rscrw"], ["rscrw"])
            cop("vector", lambda e: e.tensor_tensor(
                out=L3, in0=L3, in1=smb(0), op=AL.subtract),
                ["rscrw"], ["rscrw"])
            cop("scalar", lambda e: e.activation(
                out=Lr, in_=Lr, func=AF.Exp, bias=0.0, scale=1.0),
                ["rscrw"], ["rscrw"])
            cop("vector", lambda e: e.reduce_sum(
                out=sm3(8), in_=L3, axis=AX), ["rscrw"], ["rscrw"])
            cop("vector", lambda e: e.reciprocal(
                out=smr[:, 8:16], in_=smr[:, 8:16]), ["rscrw"], ["rscrw"])
            cop("vector", lambda e: e.tensor_tensor(
                out=L3, in0=L3, in1=smb(8), op=AL.mult), ["rscrw"], ["rscrw"])
            cop("vector", lambda e: e.reduce_max(
                out=sm3(0), in_=L3, axis=AX), ["rscrw"], ["rscrw"])
            cop("vector", lambda e: e.tensor_tensor(
                out=T3, in0=L3, in1=smb(0), op=AL.is_equal),
                ["rscrw"], ["rscrw"])
            cop("vector", lambda e: e.scalar_tensor_tensor(
                out=Tr, in0=Tr, scalar=-1e30, in1=Lr, op0=AL.mult, op1=AL.add),
                ["rscrw"], ["rscrw"])
            cop("vector", lambda e: e.reduce_max(
                out=sm3(16), in_=T3, axis=AX), ["rscrw"], ["rscrw"])
            cop("vector", lambda e: e.scalar_tensor_tensor(
                out=smr[:, 24:32], in0=smr[:, 0:8], scalar=1e-9,
                in1=smr[:, 16:24], op0=AL.add, op1=AL.add),
                ["rscrw"], ["rscrw"])
            cop("vector", lambda e: e.reciprocal(
                out=smr[:, 24:32], in_=smr[:, 24:32]), ["rscrw"], ["rscrw"])
            cop("vector", lambda e: e.tensor_tensor(
                out=T3, in0=L3, in1=smb(16), op=AL.is_ge), ["rscrw"], ["rscrw"])
            cop("vector", lambda e: e.tensor_tensor(
                out=L3, in0=L3, in1=smb(24), op=AL.mult), ["rscrw"], ["rscrw"])
            cop("vector", lambda e: e.tensor_tensor(
                out=Tr, in0=Tr, in1=Lr, op=AL.mult), ["rscrw"], ["rscrw"])
            for tg in range(8):
                rot = tg % 4
                cop("tensor", lambda e, t=tg, r=rot: e.transpose(
                    pS[0:16, r * 128:(r + 1) * 128],
                    Tr[:, t * 16:(t + 1) * 16], identf[:]),
                    ["rscrw", identf[:]], [pS[:]])
                cop("scalar", lambda e, t=tg, r=rot: e.activation(
                    out=ctile[0:16, t * 128:(t + 1) * 128],
                    in_=pS[0:16, r * 128:(r + 1) * 128],
                    func=AF.Copy, bias=0.0, scale=1.0), [pS[:]], [ctile[:]])
            tap(f"rt{tile}")

            for gi, grp in enumerate(GRPS):
                unit = 0
                units_of = {}
                for e_id in grp:
                    routed = e_id < 16
                    mid = MIDR if routed else MIDS
                    nkc = mid // 128
                    units_of[e_id] = (unit, nkc)
                    wg_s_ = s_wgr[e_id] if routed else s_wgs[e_id - 16]
                    wu_s_ = s_wur[e_id] if routed else s_wus[e_id - 16]
                    if routed:
                        wgt = (o_st, v_b)[e_id % 2]
                        wut = (kT_b, xqg2)[e_id % 2]
                    else:
                        wgt = sgb
                        wut = (o_st, v_b)[e_id % 2]
                    dma("sync", wgt[:, 0:8 * mid].rearrange(
                        "p (c t) -> p c t", t=mid),
                        wg_s_.rearrange("(c p) t -> p c t", p=128),
                        "bwg_" + _nm(wgt))
                    dma("gpsimd", wut[:, 0:8 * mid].rearrange(
                        "p (c t) -> p c t", t=mid),
                        wu_s_.rearrange("(c p) t -> p c t", p=128), _nm(wut))
                    for ns2 in range(2):
                        ta = tb + ns2 * 512
                        if routed:
                            cop("tensor", lambda e, eid=e_id, n=ns2: e.matmul(
                                pS[:, 0:512],
                                sel16[0:16, eid * 128:(eid + 1) * 128],
                                ctile[0:16, n * 512:(n + 1) * 512],
                                start=True, stop=True),
                                [sel16[:], ctile[:]], [pS[:]])
                        pg = pA if ns2 == 0 else pAT
                        pu = pB if ns2 == 0 else pO
                        sgfv = sgf[:, 0:512] if ns2 == 0 else sgf[:, 512:1024]
                        for m in range(nkc):
                            for dc in range(8):
                                cop("tensor", lambda e, mm=m, d=dc, a=ta, w=wgt,
                                    md=mid, p=pg: e.matmul(
                                        p[:, 0:512],
                                        w[:, d * md + mm * 128:
                                          d * md + (mm + 1) * 128],
                                        xp_sl(d, a, a + 512),
                                        start=(d == 0), stop=(d == 7)),
                                    [_nm(wgt), "T32c"], [_nm(pg)])
                            for dc in range(8):
                                cop("tensor", lambda e, mm=m, d=dc, a=ta, w=wut,
                                    md=mid, p=pu: e.matmul(
                                        p[:, 0:512],
                                        w[:, d * md + mm * 128:
                                          d * md + (mm + 1) * 128],
                                        xp_sl(d, a, a + 512),
                                        start=(d == 0), stop=(d == 7)),
                                    [_nm(wut), "T32c"], [_nm(pu)])
                            cop("scalar", lambda e, p=pg, sg=sgfv: e.activation(
                                out=sg, in_=p[:, 0:512], func=AF.Silu,
                                bias=0.0, scale=1.0), [_nm(pg)], ["mfold"])
                            if routed:
                                cop("vector", lambda e, p=pu, sg=sgfv: e.tensor_tensor(
                                    out=hh[:], in0=sg, in1=p[:, 0:512],
                                    op=AL.mult), ["mfold", _nm(pu)], [hh[:]])
                                cop("vector", lambda e, u=unit, mm=m, n=ns2:
                                    e.tensor_tensor(
                                        out=h_sl(u + mm, n * 512, (n + 1) * 512),
                                        in0=hh[:], in1=pS[:, 0:512], op=AL.mult),
                                    [hh[:], pS[:]], [h_name(unit + m)])
                            else:
                                cop("vector", lambda e, u=unit, mm=m, n=ns2,
                                    p=pu, sg=sgfv: e.tensor_tensor(
                                        out=h_sl(u + mm, n * 512, (n + 1) * 512),
                                        in0=sg, in1=p[:, 0:512],
                                        op=AL.mult),
                                    ["mfold", _nm(pu)], [h_name(unit + m)])
                    unit += nkc
                DPS = {(0, 0): pT0, (1, 0): pT1, (0, 1): pAT, (1, 1): pO}
                for dg in range(4):
                    for ei, e_id in enumerate(grp):
                        routed = e_id < 16
                        u0, nkc = units_of[e_id]
                        wd_s_ = s_wdr[e_id] if routed else s_wds[e_id - 16]
                        if routed:
                            wdt = (qg_st, qT_b)[e_id % 2]
                            so = ((ei // 2) % 2) * 512
                        else:
                            wdt = sgb if e_id == 16 else o_st
                            so = 0
                        sname = (f"wds_{_nm(wdt)}{so}" if routed
                                 else _nm(wdt))
                        dma_key = "wd17" if e_id == 17 else "wd_" + _nm(wdt)
                        S.add("sync" if ei % 2 == 0 else "gpsimd",
                              lambda e, w=wdt, so_=so, nk=nkc, dg_=dg, ws=wd_s_:
                              e.dma_start(
                                  out=w[:, so_:so_ + nk * 256].rearrange(
                                      "p (c t) -> p c t", t=256),
                                  in_=ws.rearrange("(c p) t -> p c t", p=128)
                                  [:, :, dg_ * 256:(dg_ + 1) * 256]),
                              ins=[_nm(wd_s_)], outs=[sname, _nm(wdt)],
                              dma=dma_key)
                        for oo in range(2):
                            oc = dg * 2 + oo
                            for ns2 in range(2):
                                pp = DPS[(oo, ns2)]
                                if gi == 1 and ei == 0:
                                    cop("tensor", lambda e, o=oc, n=ns2, p=pp:
                                        e.matmul(
                                            p[:, 0:512], identb[:],
                                            ffnacc_sl(o, n * 512, (n + 1) * 512),
                                            start=True, stop=False),
                                        [identb[:], "T32b"], [_nm(pp)])
                                for kc in range(nkc):
                                    cop("tensor", lambda e, o=oc, n=ns2, k=kc,
                                        u=u0, p=pp, w=wdt, so_=so,
                                        st=(gi == 0 and ei == 0 and kc == 0),
                                        sp=(ei == len(grp) - 1 and kc == nkc - 1):
                                        e.matmul(
                                            p[:, 0:512],
                                            w[:, so_ + k * 256 + (o % 2) * 128:
                                              so_ + k * 256 + (o % 2 + 1) * 128],
                                            h_sl(u + k, n * 512, (n + 1) * 512),
                                            start=st, stop=sp),
                                        [sname, h_name(u0 + kc)], [_nm(pp)])
                                if ei == len(grp) - 1:
                                    cop("scalar", lambda e, o=oc, n=ns2, p=pp:
                                        e.activation(
                                            out=ffnacc_sl(o, n * 512,
                                                          (n + 1) * 512),
                                            in_=p[:, 0:512], func=AF.Copy,
                                            bias=0.0, scale=1.0),
                                        [_nm(pp)], ["T32b"])
            tap(f"bt{tile}")
            for sci in range(4):
                t0 = tb + sci * SC
                outf = xt0 if sci % 2 == 0 else xt1
                for d2 in range(4):
                    cop("vector", lambda e, d=d2, t0_=t0, tb_=tb: e.tensor_tensor(
                        out=gkT_b[:, d * 512:(d + 1) * 512].rearrange(
                            "p (a t) -> p a t", t=256),
                        in0=T32c[:].rearrange("p (c t) -> p c t", t=2048)
                        [:, 2 * d:2 * d + 2, t0_:t0_ + SC],
                        in1=T32b[:, 8192:16384].rearrange(
                            "p (c t) -> p c t", t=1024)
                        [:, 2 * d:2 * d + 2, t0_ - tb_:t0_ - tb_ + SC],
                        op=AL.add),
                        ["T32c", "T32b"], ["gkT_b"])
                for dc in range(8):
                    cop("tensor", lambda e, d=dc: e.matmul(
                        pS[0:1, 0:256], ones_c[:],
                        gkT_b[:, d * 256:(d + 1) * 256],
                        start=(d == 0), stop=(d == 7)),
                        ["gkT_b", ones_c[:]], [pS[:]])
                for d2 in range(4):
                    cop("scalar", lambda e, d=d2: e.activation(
                        out=sqw[:], in_=gkT_b[:, d * 512:(d + 1) * 512],
                        func=AF.Square, bias=0.0, scale=1.0), ["gkT_b"], ["sqw"])
                    cop("tensor", lambda e, d=d2: e.matmul(
                        pS[0:1, 256:512], ones_c[:], sqw[:, 0:256],
                        start=(d == 0), stop=False), ["sqw", ones_c[:]], [pS[:]])
                    cop("tensor", lambda e, d=d2: e.matmul(
                        pS[0:1, 256:512], ones_c[:], sqw[:, 256:512],
                        start=False, stop=(d == 3)), ["sqw", ones_c[:]], [pS[:]])
                bbr2 = Bbc.rearrange("p (a t) -> p a t", a=1).to_broadcast(
                    [128, 2, 256])
                emit_ln_apply(
                    lambda d: gkT_b[:, d * 512:(d + 1) * 512].rearrange(
                        "p (a t) -> p a t", t=256), "gkT_b",
                    lambda d2: cop("vector", lambda e, d=d2, o=outf, bb=bbr2:
                                   e.tensor_tensor(
                                       out=o[:, d * 512:(d + 1) * 512].rearrange(
                                           "p (a t) -> p a t", t=256),
                                       in0=rscrw[:].rearrange(
                                           "p (a t) -> p a t", t=256),
                                       in1=bb, op=AL.add),
                                   ["rscrw", "UbufG2"], [outf[:]]))
                dma("sync",
                    out_d.rearrange("(c p) t -> p c t", p=128)[:, :, t0:t0 + SC],
                    outf[:].rearrange("p (c t) -> p c t", t=256), "st_out")

        with nc.Block() as block:
            S.emit(nc, block, stack)

    print(f"[build] ops={len(S.ops)}")
    return nc


_NC_CACHE = {}


def _get_nc():
    if "nc" not in _NC_CACHE:
        _NC_CACHE["nc"] = build_program()
    return _NC_CACHE["nc"]


def kernel(**inp):
    nc = _get_nc()
    f32 = np.float32
    tr = lambda a: np.ascontiguousarray(np.asarray(a, dtype=f32).T)
    sel = np.zeros((16, 2048), dtype=BF)
    for e in range(16):
        sel[e, e * 128:(e + 1) * 128] = 1
    base = {
        "wqT": tr(inp["L0_attn_wq"]),
        "wkT": tr(inp["L0_attn_wk"]),
        "wvT": tr(inp["L0_attn_wv"]),
        "wgT": tr(inp["L0_attn_wg"]),
        "wgk1T": tr(inp["L0_attn_wgk1"]),
        "wgk2T": tr(inp["L0_attn_wgk2"]),
        "woT": tr(inp["L0_attn_wo"]),
        "f_rw": np.ascontiguousarray(np.asarray(inp["L0_ffn_router_w"], f32)),
        "f_wg_r": np.ascontiguousarray(np.asarray(inp["L0_ffn_wg_r"], f32)),
        "f_wu_r": np.ascontiguousarray(np.asarray(inp["L0_ffn_wu_r"], f32)),
        "f_wd_r": np.ascontiguousarray(np.asarray(inp["L0_ffn_wd_r"], f32)),
        "f_wg_s": np.ascontiguousarray(np.asarray(inp["L0_ffn_wg_s"], f32)),
        "f_wu_s": np.ascontiguousarray(np.asarray(inp["L0_ffn_wu_s"], f32)),
        "f_wd_s": np.ascontiguousarray(np.asarray(inp["L0_ffn_wd_s"], f32)),
        "ident_f": np.eye(128, dtype=f32),
        "ident_b": np.eye(128).astype(BF),
        "maskU": np.triu(np.ones((128, 128), f32)),
        "sel16_d": sel,
        "ngk2_b": np.ascontiguousarray(
            np.broadcast_to(np.asarray(inp["L0_attn_ngk2"], f32), (128, 16))),
        "bgk2_c": np.ascontiguousarray(
            np.asarray(inp["L0_attn_bgk2"], f32).reshape(8, 128).T),
        "no_c": np.ascontiguousarray(
            np.asarray(inp["L0_attn_no"], f32).reshape(16, 128).T),
    }
    g = np.stack([np.asarray(inp["L0_attn_" + k], f32).reshape(8, 128).T
                  for k in ("nq", "nk", "nv", "ng", "ngk1")], axis=1)
    base["gains"] = np.ascontiguousarray(g.reshape(128, 40))

    x = np.asarray(inp["L0_attn_hidden_states"], f32)
    in_maps = []
    for core in range(8):
        b, half = core // 2, core % 2
        m = dict(base)
        m["xT"] = np.ascontiguousarray(x[b, half * T:(half + 1) * T, :].T)
        m["alpha"] = np.full((128, 1), float(half), f32)
        in_maps.append(m)
    res = run_bass_kernel_spmd(nc, in_maps, core_ids=list(range(8)))
    out = np.empty((4, 4096, 1024), f32)
    for core in range(8):
        b, half = core // 2, core % 2
        out[b, half * T:(half + 1) * T, :] = res.results[core]["out"].T
    return out


# version 48
# speedup vs baseline: 1.0079x; 1.0079x over previous
import numpy as np
import ml_dtypes
import concourse.bass as bass
import concourse.mybir as mybir
from concourse.bass_utils import run_bass_kernel_spmd
from contextlib import ExitStack

F32 = mybir.dt.float32
BF16 = mybir.dt.bfloat16
AX = mybir.AxisListType.X
AL = mybir.AluOpType
AF = mybir.ActivationFunctionType
BF = ml_dtypes.bfloat16
MAGIC = 12582912.0

D = 1024
T = 2048
NSC = 8
SC = 256
HEADS = 4
DVT = 2048
MIDR = 256
MIDS = 512
NR = 16
NS = 2

ENGINES = ("sync", "vector", "scalar", "tensor", "gpsimd")


def _nm(ap):
    t = getattr(ap, "tensor", None)
    return t.name if t is not None else ap.name


class Sched:
    def __init__(self):
        self.ops = []
        self.lastw = {}
        self.readers = {}

    def add(self, eng, fn, ins=(), outs=(), dma=None, tag=""):
        reads = [a if isinstance(a, str) else _nm(a) for a in ins]
        writes = [a if isinstance(a, str) else _nm(a) for a in outs]
        deps = set()
        for r in reads:
            i = self.lastw.get(r)
            if i is not None:
                deps.add(i)
        for w in writes:
            i = self.lastw.get(w)
            if i is not None:
                deps.add(i)
            for rd in self.readers.get(w, ()):
                deps.add(rd)
        idx = len(self.ops)
        self.ops.append(dict(eng=eng, fn=fn, deps=deps, dma=dma, tag=tag))
        for r in reads:
            self.readers.setdefault(r, []).append(idx)
        for w in writes:
            self.lastw[w] = idx
            self.readers[w] = []
        return idx

    def emit(self, nc, block, stack):
        n = len(self.ops)
        dkeys = sorted({op["dma"] for op in self.ops if op["dma"] is not None})
        esem = {e: stack.enter_context(nc.semaphore(f"pg_{e}")) for e in ENGINES}
        dsem = {k: stack.enter_context(nc.semaphore(f"dm_{k}")) for k in dkeys}
        inc_needed = [False] * n

        def same_safe(oi, op):
            return (oi["dma"] is None and op["dma"] is None
                    and oi["eng"] == op["eng"] and oi["eng"] == "tensor")

        for j, op in enumerate(self.ops):
            for i in op["deps"]:
                oi = self.ops[i]
                if oi["dma"] is not None:
                    continue
                if same_safe(oi, op):
                    continue
                inc_needed[i] = True
        ccount = {e: 0 for e in ENGINES}
        dcount = {}
        milestone = [None] * n
        for j, op in enumerate(self.ops):
            if op["dma"] is not None:
                k = op["dma"]
                dcount[k] = dcount.get(k, 0) + (1 if k == "cc" else 16)
                milestone[j] = ("d", k, dcount[k])
            elif inc_needed[j]:
                e = op["eng"]
                ccount[e] += 1
                milestone[j] = ("e", e, ccount[e])
        waits = []
        for j, op in enumerate(self.ops):
            need = {}
            for i in op["deps"]:
                oi = self.ops[i]
                if same_safe(oi, op):
                    continue
                m = milestone[i]
                key = (m[0], m[1])
                need[key] = max(need.get(key, 0), m[2])
            waits.append(need)
        per_engine = {e: [] for e in ENGINES}
        for j, op in enumerate(self.ops):
            per_engine[op["eng"]].append(j)

        def mk(engname):
            def body(e):
                watched = {}
                for j in per_engine[engname]:
                    op = self.ops[j]
                    for (kind, key), v in sorted(waits[j].items()):
                        if watched.get((kind, key), 0) >= v:
                            continue
                        e.wait_ge(esem[key] if kind == "e" else dsem[key], v)
                        watched[(kind, key)] = v
                    ins = op["fn"](e)
                    m = milestone[j]
                    if m is not None:
                        if m[0] == "d":
                            ins.then_inc(dsem[m[1]], 1 if m[1] == "cc" else 16)
                        else:
                            ins.then_inc(esem[m[1]], 1)
                if engname == "sync":
                    for k in sorted(dcount):
                        e.wait_ge(dsem[k], dcount[k])
                    for en in ENGINES:
                        if en != "sync" and ccount[en] > 0:
                            e.wait_ge(esem[en], ccount[en])
            return body

        for name in ENGINES:
            getattr(block, name)(mk(name))


def build_program(taps=()):
    nc = bass.Bass("TRN2", target_bir_lowering=False, debug=False)
    S = Sched()
    BUFS = {}
    TAPS = list(taps)

    def din(name, shape, dtype=F32):
        return nc.dram_tensor(name, shape, dtype, kind="ExternalInput").ap()

    xT = din("xT", [D, T])
    wqT = din("wqT", [D, D])
    wkT = din("wkT", [D, D])
    wvT = din("wvT", [D, DVT])
    wgT = din("wgT", [D, DVT])
    wgk1T = din("wgk1T", [D, 16])
    wgk2T = din("wgk2T", [16, D])
    woT = din("woT", [DVT, D])
    gains = din("gains", [128, 40])
    no_c = din("no_c", [128, 16])
    bgk2_c = din("bgk2_c", [128, 8])
    ngk2_b = din("ngk2_b", [128, 16])
    alpha = din("alpha", [128, 1])
    ident_f = din("ident_f", [128, 128])
    ident_b = din("ident_b", [128, 128], BF16)
    maskU_d = din("maskU", [128, 128])
    sel16_d = din("sel16_d", [16, 2048], BF16)
    f_rw = din("f_rw", [D, 16])
    f_wg_r = din("f_wg_r", [NR, D, MIDR])
    f_wu_r = din("f_wu_r", [NR, D, MIDR])
    f_wd_r = din("f_wd_r", [NR, MIDR, D])
    f_wg_s = din("f_wg_s", [NS, D, MIDS])
    f_wu_s = din("f_wu_s", [NS, D, MIDS])
    f_wd_s = din("f_wd_s", [NS, MIDS, D])
    out_d = nc.dram_tensor("out", [D, T], F32, kind="ExternalOutput").ap()

    def dscr(name, shape, dtype=BF16):
        return nc.dram_tensor(name, shape, dtype, kind="Internal").ap()

    o_scr = dscr("o_scr", [DVT, T])
    qg_scr = dscr("qg_scr", [D, T])
    xqg_scr = dscr("xqg_scr", [D, T])
    wg_scr = dscr("wg_scr", [D, DVT])
    wo_scr = dscr("wo_scr", [DVT, D])
    invsg_scr = dscr("invsg_scr", [8, 256], F32)
    s_wgr = dscr("s_wgr", [NR, D, MIDR])
    s_wur = dscr("s_wur", [NR, D, MIDR])
    s_wdr = dscr("s_wdr", [NR, MIDR, D])
    s_wgs = dscr("s_wgs", [NS, D, MIDS])
    s_wus = dscr("s_wus", [NS, D, MIDS])
    s_wds = dscr("s_wds", [NS, MIDS, D])
    s_rw = dscr("s_rw", [D, 16])
    cc_in = dscr("cc_in", [D, 512])
    cc_out = dscr("cc_out", [2 * D, 512])

    stack = ExitStack()

    def sbuf(name, shape, dtype):
        t = stack.enter_context(nc.sbuf_tensor(name, shape, dtype))
        BUFS[name] = t
        return t

    def psum(name, shape, dtype=F32):
        return stack.enter_context(nc.psum_tensor(name, shape, dtype))

    with stack:
        T32a = sbuf("T32a", [128, 16384], BF16)
        T32b = sbuf("T32b", [128, 16384], BF16)
        T32c = sbuf("T32c", [128, 16384], BF16)
        xt0 = sbuf("xt0", [128, 2048], F32)
        xt1 = sbuf("xt1", [128, 2048], F32)
        qT_b = sbuf("qT_b", [128, 2048], BF16)
        kT_b = sbuf("kT_b", [128, 2048], BF16)
        xqg2 = sbuf("xqg2", [128, 2048], BF16)
        v_b = sbuf("v_b", [128, 4096], BF16)
        gkT_b = sbuf("gkT_b", [128, 2048], F32)
        o_st = sbuf("o_st", [128, 4096], BF16)
        qg_st = sbuf("qg_st", [128, 2048], BF16)
        sgb = sbuf("sgb", [128, 4096], BF16)
        sqw = sbuf("sqw", [128, 512], F32)
        sq0 = sqw[:, 0:256]
        sq1 = sqw[:, 256:512]
        srow = sq0
        mfold = sbuf("mfold", [128, 1280], F32)
        statb = sbuf("statb", [128, 48], F32)
        t1b = sbuf("t1b", [128, 256], F32)
        t1tm = sbuf("t1tm", [128, 36], F32)
        xq2tm = sbuf("xq2tm", [128, 20], F32)
        xq2b = sbuf("xq2b", [16, 256], BF16)
        rscrw = sbuf("rscrw", [128, 512], F32)
        rscr = rscrw[:, 0:256]
        rscr2 = rscrw[:, 256:512]
        ab0, ab1 = rscr, rscr2
        Ubuf = sbuf("Ubuf", [128, 1280], F32)
        invsb = sbuf("invsb", [128, 1280], BF16)
        urow5 = sbuf("urow5", [1, 256], F32)
        irow5 = sbuf("irow5", [1, 256], F32)
        i2row = sbuf("i2row", [1, 256], F32)
        abr = sbuf("abr", [1, 512], F32)
        expP = sbuf("expP", [128, 256], BF16)
        expM = sbuf("expM", [128, 256], BF16)
        gscr = sbuf("gscr", [128, 256], F32)
        Kbtm = sbuf("Kbtm", [128, 256], BF16)
        ATm = sbuf("ATm", [128, 128], BF16)
        ATm2 = sbuf("ATm2", [128, 128], BF16)
        T0e = sbuf("T0e", [128, 1024], BF16)
        Scar = sbuf("Scar", [128, 4096], BF16)
        eclb = sbuf("eclb", [128, 8], F32)
        egsum = sbuf("egsum", [128, 8], F32)
        gsml = sbuf("gsml", [128, 8], F32)
        identf = sbuf("identf", [128, 128], F32)
        identb = sbuf("identb", [128, 128], BF16)
        maskUb = sbuf("maskUb", [128, 128], F32)
        sel16 = sbuf("sel16", [16, 2048], BF16)
        ctile = sbuf("ctile", [16, 1024], BF16)
        ones_c = sbuf("ones_c", [128, 1], F32)
        ones_r = sbuf("ones_r", [1, 128], F32)
        gainb = sbuf("gainb", [128, 40], F32)
        gainab = sbuf("gainab", [128, 40], F32)
        nob = sbuf("nob", [128, 16], F32)
        bgk2b = sbuf("bgk2b", [128, 8], F32)
        ngk2bb = sbuf("ngk2bb", [128, 16], F32)
        alphab = sbuf("alphab", [128, 1], F32)
        amrep = sbuf("amrep", [128, 8], F32)
        swrep = sbuf("swrep", [128, 8], F32)
        amsml = sbuf("amsml", [1, 8], F32)
        ltm = sbuf("ltm", [128, 20], F32)
        combtm = sbuf("combtm", [128, 16], F32)
        rwb = sbuf("rwb", [128, 128], BF16)
        wgk1_i = sbuf("wgk1_i", [128, 128], BF16)
        wgk2_i = sbuf("wgk2_i", [16, 1024], BF16)

        pA = psum("pA", [128, 512])
        pB = psum("pB", [128, 512])
        pT0 = psum("pT0", [128, 512])
        pT1 = psum("pT1", [128, 512])
        pAT = psum("pAT", [128, 512])
        pO = psum("pO", [128, 512])
        pS = psum("pS", [128, 512])
        pBT = psum("pBT", [128, 512], BF16)

        Abc = Ubuf[:, 256:512]
        Bbc = Ubuf[:, 512:768]
        hh = sgb[:, 0:512]
        print(f"[build] sbuf free/partition: {nc.sbuf_bytes_remaining // 128}B")

        def dma(eng, out, in_, key, tag=""):
            S.add(eng, lambda e, o=out, i=in_: e.dma_start(out=o, in_=i),
                  ins=[in_], outs=[out], dma=key, tag=tag)

        def cop(eng, fn, ins, outs, tag=""):
            S.add(eng, fn, ins=ins, outs=outs, tag=tag)

        i2bc = Ubuf[:, 0:256]

        def tap(point):
            for (pt, bn) in TAPS:
                if pt != point:
                    continue
                t = BUFS[bn]
                shp = list(t.shape)
                dt_ = t.dtype
                o = nc.dram_tensor(f"dbg_{point}_{bn}", shp, dt_,
                                   kind="ExternalOutput").ap()
                dma("sync", o, t[:], f"dbg{point}{bn}")

        dma("sync", identf[:], ident_f[:], "identf")
        dma("sync", identb[:], ident_b[:], "identb")
        dma("sync", maskUb[:], maskU_d[:], "maskUb")
        dma("sync", sel16[:], sel16_d[:], "sel16")
        dma("sync", gainb[:], gains[:], "gainb")
        dma("sync", nob[:], no_c[:], "nob")
        dma("sync", bgk2b[:], bgk2_c[:], "bgk2b")
        dma("sync", ngk2bb[:], ngk2_b[:], "ngk2bb")
        dma("sync", alphab[:], alpha[:], "alphab")
        cop("gpsimd", lambda e: e.memset(ones_c[:], 1.0), [], [ones_c[:]])
        cop("gpsimd", lambda e: e.memset(ones_r[:], 1.0), [], [ones_r[:]])
        cop("gpsimd", lambda e: e.memset(t1b[:], 0.0), [], [t1b[:]])
        cop("gpsimd", lambda e: e.memset(Scar[:], 0.0), [], [Scar[:]])
        cop("gpsimd", lambda e: e.memset(swrep[:, 7:8], 1e-8), [], [swrep[:]])
        cop("gpsimd", lambda e: e.memset(amrep[:, 7:8], 1e-5), [], [amrep[:]])
        cop("scalar", lambda e: e.activation(out=gainab[:], in_=gainb[:],
                                             func=AF.Abs, bias=0.0, scale=1.0),
            [gainb[:]], [gainab[:]])
        for kcg in range(8):
            cop("vector", lambda e, k=kcg: e.tensor_copy(out=egsum[:, k:k + 1],
                                                         in_=alphab[:]),
                [alphab[:]], [egsum[:]])

        cop("gpsimd", lambda e: e.memset(amsml[:], 0.0), [], [amsml[:]])

        def quant_weight(widx, dram, Din, Dout, dest, tag):
            ndc = (Din + 127) // 128
            ncs = (Dout + 1023) // 1024
            colw = min(Dout, 1024)
            for dc in range(ndc):
                h = min(128, Din - dc * 128)
                for cs in range(ncs):
                    st = xt0 if (dc * ncs + cs) % 2 == 0 else xt1
                    src = dram[dc * 128:dc * 128 + h, cs * 1024:cs * 1024 + colw]
                    dma("sync", st[0:h, 0:colw], src, _nm(st), tag=f"P0L{tag}")
                    cop("scalar", lambda e, s=st, hh_=h: e.activation(
                        out=s[0:hh_, 0:colw], in_=s[0:hh_, 0:colw], func=AF.Abs,
                        bias=0.0, scale=1.0), [st[:]], [st[:]])
                    nns = (colw + 511) // 512
                    for ns in range(nns):
                        a, b = ns * 512, min(colw, ns * 512 + 512)
                        cop("tensor", lambda e, s=st, hh_=h, a_=a, b_=b,
                            first=(ns == 0), last=(ns == nns - 1): e.matmul(
                                pS[0:1, 0:b_ - a_], ones_c[0:hh_, :],
                                s[0:hh_, a_:b_], start=first, stop=last),
                            [st[:], ones_c[:]], [pS[:]])
                    red_w = min(512, Dout)
                    cop("vector", lambda e: e.reduce_sum(
                        out=amsml[0:1, 7:8], in_=pS[0:1, 0:red_w], axis=AX),
                        [pS[:]], [amsml[:]])
                    cop("vector", lambda e: e.tensor_tensor(
                        out=amsml[0:1, widx:widx + 1],
                        in0=amsml[0:1, widx:widx + 1], in1=amsml[0:1, 7:8],
                        op=AL.add), [amsml[:]], [amsml[:]])
            cnt = float(Din * Dout)
            cop("vector", lambda e: e.tensor_scalar(
                out=amsml[0:1, widx:widx + 1], in0=amsml[0:1, widx:widx + 1],
                scalar1=1.0 / cnt, scalar2=1e-5, op0=AL.mult, op1=AL.max),
                [amsml[:]], [amsml[:]])
            cop("tensor", lambda e: e.matmul(pS[:, 0:1], ones_r[:],
                                             amsml[0:1, widx:widx + 1],
                                             start=True, stop=True),
                [amsml[:], ones_r[:]], [pS[:]])
            sc_mul = (1.0 / 16.0) if widx == 0 else 1.0
            cop("scalar", lambda e, m=sc_mul: e.activation(
                out=amrep[:, widx:widx + 1], in_=pS[:, 0:1], func=AF.Copy,
                bias=0.0, scale=m), [pS[:]], [amrep[:]])
            cop("vector", lambda e: e.reciprocal(out=swrep[:, widx:widx + 1],
                                                 in_=amrep[:, widx:widx + 1]),
                [amrep[:]], [swrep[:]])
            if widx == 0:
                cop("vector", lambda e: e.tensor_scalar(
                    out=swrep[:, 0:1], in0=swrep[:, 0:1], scalar1=1.0 / 16.0,
                    scalar2=None, op0=AL.mult), [swrep[:]], [swrep[:]])
            for dc in range(ndc):
                h = min(128, Din - dc * 128)
                for cs in range(ncs):
                    st = xt0 if (dc * ncs + cs) % 2 == 0 else xt1
                    src = dram[dc * 128:dc * 128 + h, cs * 1024:cs * 1024 + colw]
                    dma("sync", st[0:h, 0:colw], src, _nm(st), tag=f"P0R{tag}")
                    cop("vector", lambda e, s=st, hh_=h: e.tensor_scalar(
                        out=s[0:hh_, 0:colw], in0=s[0:hh_, 0:colw],
                        scalar1=swrep[0:hh_, widx:widx + 1], scalar2=MAGIC,
                        op0=AL.mult, op1=AL.add), [st[:], swrep[:]], [st[:]])
                    cop("vector", lambda e, s=st, hh_=h: e.tensor_scalar(
                        out=s[0:hh_, 0:colw], in0=s[0:hh_, 0:colw],
                        scalar1=MAGIC, scalar2=1.0, op0=AL.subtract, op1=AL.min),
                        [st[:]], [st[:]])
                    if dest[0] == "res":
                        dst, dname = dest[1](dc, cs, colw)
                        cop("vector", lambda e, s=st, hh_=h, d=dst: e.tensor_scalar(
                            out=d, in0=s[0:hh_, 0:colw], scalar1=-1.0,
                            scalar2=None, op0=AL.max), [st[:]], [dname])
                    else:
                        cb = qT_b if (dc * ncs + cs) % 2 == 0 else kT_b
                        cop("vector", lambda e, s=st, hh_=h, c=cb: e.tensor_scalar(
                            out=c[0:hh_, 0:colw], in0=s[0:hh_, 0:colw],
                            scalar1=-1.0, scalar2=None, op0=AL.max),
                            [st[:]], [cb[:]])
                        dma("gpsimd",
                            dest[1][dc * 128:dc * 128 + h,
                                    cs * 1024:cs * 1024 + colw],
                            cb[0:h, 0:colw], "st_" + _nm(cb), tag=f"P0S{tag}")

        IWQ, IWK, IWV, IWG, IWO, IGK1, IGK2 = 0, 1, 2, 3, 4, 5, 6
        quant_weight(IWQ, wqT, D, D,
                     ("res", lambda dc, cs, w: (T32b[:, dc * D:dc * D + w],
                                                "T32b")), "wq")
        quant_weight(IWK, wkT, D, D,
                     ("res", lambda dc, cs, w: (T32b[:, 8192 + dc * D:
                                                     8192 + dc * D + w],
                                                "T32b")), "wk")
        quant_weight(IGK1, wgk1T, D, 16,
                     ("res", lambda dc, cs, w: (wgk1_i[:, dc * 16:dc * 16 + w],
                                                "wgk1_i")), "wgk1")
        quant_weight(IGK2, wgk2T, 16, D,
                     ("res", lambda dc, cs, w: (wgk2_i[0:16, 0:w], "wgk2_i")),
                     "wgk2")
        quant_weight(IWV, wvT, D, DVT,
                     ("res", lambda dc, cs, w: (T32a[:, dc * DVT + cs * 1024:
                                                     dc * DVT + cs * 1024 + w],
                                                "T32a")), "wv")
        quant_weight(IWG, wgT, D, DVT, ("scr", wg_scr), "wg")
        quant_weight(IWO, woT, DVT, D, ("scr", wo_scr), "wo")

        pairs = []
        for ei in range(NR):
            pairs += [(f_wg_r[ei], s_wgr[ei], D, MIDR),
                      (f_wu_r[ei], s_wur[ei], D, MIDR),
                      (f_wd_r[ei], s_wdr[ei], MIDR, D)]
        for ei in range(NS):
            pairs += [(f_wg_s[ei], s_wgs[ei], D, MIDS),
                      (f_wu_s[ei], s_wus[ei], D, MIDS),
                      (f_wd_s[ei], s_wds[ei], MIDS, D)]
        pairs += [(f_rw, s_rw, D, 16)]
        cast_units = []
        _ci = [0]

        def mk_cast_unit(src, dst, rows, cols, c0, nch):
            def emit():
                par = _ci[0] % 2
                _ci[0] += 1
                slot = sgb[:, par * 2048:par * 2048 + nch * cols]
                rname = f"sgbH{par}"
                S.add("gpsimd", lambda e: e.dma_start(
                    out=slot.rearrange("p (c t) -> p c t", t=cols),
                    in_=src.rearrange("(c p) t -> p c t", p=128)[:, c0:c0 + nch]),
                    ins=[_nm(src)], outs=[rname], dma=f"ffc{par}")
                S.add("gpsimd", lambda e: e.dma_start(
                    out=dst.rearrange("(c p) t -> p c t", p=128)[:, c0:c0 + nch],
                    in_=slot.rearrange("p (c t) -> p c t", t=cols)),
                    ins=[rname], outs=[_nm(dst)], dma=f"ffs{par}")
            return emit

        for (src, dst, rows, cols) in pairs:
            nch_tot = (rows + 127) // 128
            step = max(1, 2048 // cols)
            for c0 in range(0, nch_tot, step):
                nch = min(step, nch_tot - c0)
                cast_units.append(mk_cast_unit(src, dst, rows, cols, c0, nch))

        def emit_cast_units(k):
            while k > 0 and cast_units:
                cast_units.pop(0)()
                k -= 1

        GQ, GK, GV, GG, GGK1 = 0, 1, 2, 3, 4

        def xq_sl(n, dc, a, b):
            return T32c[:, n * 2048 + dc * 256 + a:n * 2048 + dc * 256 + b]

        def wq_sl(dc, a, b):
            return T32b[:, dc * D + a:dc * D + b]

        def wk_sl(dc, a, b):
            return T32b[:, 8192 + dc * D + a:8192 + dc * D + b]

        def wv_sl(dc, a, b):
            return T32a[:, dc * DVT + a:dc * DVT + b]

        def row_extract(col_ap, src_name, tt, region, P=None):
            P = pS if P is None else P
            cop("tensor", lambda e, c=col_ap, r=region, t=tt, p=P: e.transpose(
                p[0:1, r + t * 128:r + (t + 1) * 128], c, identf[:]),
                [src_name, identf[:]], [_nm(P)])

        def a1_spine_chunks(sc):
            t0 = sc * SC
            xt = xt0 if sc % 2 == 0 else xt1

            def c_load():
                dma("sync", xt[:].rearrange("p (c t) -> p c t", t=256),
                    xT.rearrange("(c p) t -> p c t", p=128)[:, :, t0:t0 + SC],
                    _nm(xt), tag="xload")

            def mk_fold(pr):
                def c_fold():
                    for dc in (2 * pr, 2 * pr + 1):
                        sq = sq0 if dc % 2 == 0 else sq1
                        ab = ab0 if dc % 2 == 0 else ab1
                        cop("scalar", lambda e, d=dc, s=sq, x=xt: e.activation(
                            out=s[:], in_=x[:, d * 256:(d + 1) * 256],
                            func=AF.Square, bias=0.0, scale=1.0),
                            [xt[:]], [sq[:]])
                        cop("tensor", lambda e, s=sq, first=(dc == 0),
                            last=(dc == 7): e.matmul(
                                pS[0:1, 0:256], ones_c[:], s[:],
                                start=first, stop=last),
                            [sq[:], ones_c[:]], [pS[:]])
                        cop("scalar", lambda e, d=dc, a=ab, x=xt: e.activation(
                            out=a[:], in_=x[:, d * 256:(d + 1) * 256],
                            func=AF.Abs, bias=0.0, scale=1.0), [xt[:]], [ab[:]])
                        for n in range(5):
                            if dc == 0:
                                cop("vector", lambda e, a=ab, g=n: e.tensor_scalar(
                                    out=mfold[:, g * 256:(g + 1) * 256], in0=a[:],
                                    scalar1=gainab[:, g * 8:g * 8 + 1],
                                    scalar2=None, op0=AL.mult),
                                    [ab[:], gainab[:]], [mfold[:]])
                            else:
                                cop("vector", lambda e, a=ab, g=n, d=dc:
                                    e.scalar_tensor_tensor(
                                        out=mfold[:, g * 256:(g + 1) * 256],
                                        in0=a[:],
                                        scalar=gainab[:, g * 8 + d:g * 8 + d + 1],
                                        in1=mfold[:, g * 256:(g + 1) * 256],
                                        op0=AL.mult, op1=AL.max),
                                    [ab[:], gainab[:], mfold[:]], [mfold[:]])
                return c_fold

            def c_stat():
                cop("scalar", lambda e: e.activation(
                    out=srow[0:1, :], in_=pS[0:1, 0:256],
                    func=AF.Copy, bias=0.0, scale=1.0), [pS[:]], [srow[:]])
                for tt in range(2):
                    sb_ = statb[:, tt * 24:(tt + 1) * 24]
                    cop("tensor", lambda e, t=tt: e.transpose(
                        pS[:, 0:128], srow[:, t * 128:(t + 1) * 128], identf[:]),
                        [srow[:], identf[:]], [pS[:]])
                    cop("vector", lambda e, s=sb_: e.tensor_copy(
                        out=s[:, 0:1], in_=pS[:, 0:1]), [pS[:]], [statb[:]])
                    for n in range(5):
                        cop("tensor", lambda e, t=tt, g=n: e.transpose(
                            pS[:, 0:128],
                            mfold[:, g * 256 + t * 128:g * 256 + (t + 1) * 128],
                            identf[:]), [mfold[:], identf[:]], [pS[:]])
                        cop("vector", lambda e, s=sb_, g=n: e.reduce_max(
                            out=s[:, 1 + g:2 + g], in_=pS[:, 0:128], axis=AX),
                            [pS[:]], [statb[:]])
                    cop("scalar", lambda e, s=sb_: e.activation(
                        out=s[:, 6:7], in_=s[:, 0:1], func=AF.Sqrt,
                        bias=swrep[:, 7:8], scale=1.0 / 1024.0),
                        [statb[:], swrep[:]], [statb[:]])
                    cop("vector", lambda e, s=sb_: e.reciprocal(
                        out=s[:, 7:8], in_=s[:, 6:7]), [statb[:]], [statb[:]])
                    cop("vector", lambda e, s=sb_: e.tensor_scalar(
                        out=s[:, 8:13], in0=s[:, 1:6], scalar1=s[:, 7:8],
                        scalar2=1e-5, op0=AL.mult, op1=AL.max),
                        [statb[:]], [statb[:]])
                    cop("vector", lambda e, s=sb_: e.tensor_scalar(
                        out=s[:, 13:18], in0=s[:, 8:13], scalar1=1.0 / 127.0,
                        scalar2=None, op0=AL.mult), [statb[:]], [statb[:]])
                    cop("vector", lambda e, s=sb_: e.reciprocal(
                        out=s[:, 18:23], in_=s[:, 8:13]), [statb[:]], [statb[:]])
                    cop("vector", lambda e, s=sb_: e.tensor_scalar(
                        out=s[:, 18:23], in0=s[:, 18:23], scalar1=s[:, 7:8],
                        scalar2=127.0, op0=AL.mult, op1=AL.mult),
                        [statb[:]], [statb[:]])

            def emit_extract_bcast(n):
                for tt in range(2):
                    row_extract(statb[:, tt * 24 + 18 + n:tt * 24 + 19 + n],
                                "statb", tt, 0)
                    row_extract(statb[:, tt * 24 + 13 + n:tt * 24 + 14 + n],
                                "statb", tt, 256)
                cop("scalar", lambda e: e.activation(
                    out=urow5[0:1, :], in_=pS[0:1, 0:256],
                    func=AF.Copy, bias=0.0, scale=1.0), [pS[:]], [urow5[:]])
                cop("scalar", lambda e: e.activation(
                    out=irow5[0:1, :], in_=pS[0:1, 256:512],
                    func=AF.Copy, bias=0.0, scale=1.0), [pS[:]], [irow5[:]])
                if n == GG:
                    dma("sync", invsg_scr[sc:sc + 1, :], irow5[0:1, :],
                        "st_invsg")
                cop("tensor", lambda e: e.matmul(
                    pS[:, 0:256], ones_r[:], urow5[0:1, :],
                    start=True, stop=True), [urow5[:], ones_r[:]], [pS[:]])
                cop("scalar", lambda e, g=n: e.activation(
                    out=Ubuf[:, g * 256:(g + 1) * 256], in_=pS[:, 0:256],
                    func=AF.Copy, bias=0.0, scale=1.0), [pS[:]], [f"UbufG{n}"])
                cop("tensor", lambda e: e.matmul(
                    pS[:, 0:256], ones_r[:], irow5[0:1, :],
                    start=True, stop=True), [irow5[:], ones_r[:]], [pS[:]])
                cop("scalar", lambda e, g=n: e.activation(
                    out=invsb[:, g * 256:(g + 1) * 256], in_=pS[:, 0:256],
                    func=AF.Copy, bias=0.0, scale=1.0), [pS[:]], [invsb[:]])

            def emit_rounds(n):
                for dc in range(8):
                    rs = rscr if dc % 2 == 0 else rscr2
                    cop("vector", lambda e, d=dc, g=n, r=rs, x=xt:
                        e.scalar_tensor_tensor(
                            out=r, in0=x[:, d * 256:(d + 1) * 256],
                            scalar=gainb[:, g * 8 + d:g * 8 + d + 1],
                            in1=Ubuf[:, g * 256:(g + 1) * 256],
                            op0=AL.mult, op1=AL.mult),
                        [xt[:], gainb[:], f"UbufG{n}"], ["rscrw"])
                    if dc % 2 == 1:
                        cop("vector", lambda e, d=dc, g=n: e.tensor_scalar(
                            out=xq_sl(g, d - 1, 0, 512), in0=rscrw[:],
                            scalar1=MAGIC, scalar2=MAGIC,
                            op0=AL.add, op1=AL.subtract),
                            ["rscrw"], [f"xqG{n}"])

            def mk_gain(n):
                def c_gain():
                    emit_extract_bcast(n)
                    emit_rounds(n)
                return c_gain

            return ([c_load] + [mk_fold(p) for p in range(4)] + [c_stat]
                    + [mk_gain(n) for n in (GQ, GGK1, GK, GV, GG)])

        def a1_heavy_a(sc):
            t0 = sc * SC

            def proj_fm(xqn, wsl, wname, outb, widx, gidx):
                for op2 in range(4):
                    pp = pA if op2 % 2 == 0 else pB
                    for oh in range(2):
                        oc = op2 * 2 + oh
                        for dc in range(8):
                            cop("tensor", lambda e, o=oc, d=dc, p=pp, h2=oh:
                                e.matmul(
                                    p[:, h2 * 256:(h2 + 1) * 256],
                                    wsl(d, o * 128, (o + 1) * 128),
                                    xq_sl(xqn, d, 0, 256),
                                    start=(d == 0), stop=(d == 7)),
                                [f"xqG{xqn}", wname], [_nm(pp)])
                    ivr = invsb[:, gidx * 256:(gidx + 1) * 256].rearrange(
                        "p (a t) -> p a t", a=1).to_broadcast([128, 2, 256])
                    cop("vector", lambda e, o2=op2, p=pp, iv=ivr:
                        e.scalar_tensor_tensor(
                            out=outb[:, o2 * 512:(o2 + 1) * 512].rearrange(
                                "p (a t) -> p a t", t=256),
                            in0=p[:, 0:512].rearrange("p (a t) -> p a t", t=256),
                            scalar=amrep[:, widx:widx + 1],
                            in1=iv, op0=AL.mult, op1=AL.mult),
                        [_nm(pp), amrep[:], invsb[:]], [outb[:]])

            proj_fm(GQ, wq_sl, "T32b", qT_b, IWQ, GQ)
            proj_fm(GK, wk_sl, "T32b", kT_b, IWK, GK)
            for dc in range(8):
                cop("tensor", lambda e, d=dc: e.matmul(
                    pB[0:16, 0:256], wgk1_i[:, d * 16:(d + 1) * 16],
                    xq_sl(GGK1, d, 0, 256), start=(d == 0), stop=(d == 7)),
                    [f"xqG{GGK1}", "wgk1_i"], [pB[:]])
            cop("vector", lambda e: e.scalar_tensor_tensor(
                out=t1b[0:16, :], in0=pB[0:16, 0:256],
                scalar=amrep[0:16, IGK1:IGK1 + 1],
                in1=invsb[0:16, GGK1 * 256:(GGK1 + 1) * 256],
                op0=AL.mult, op1=AL.mult), [pB[:], amrep[:], invsb[:]], [t1b[:]])
            for tc in range(2):
                for nv in range(4):
                    pp = pA if nv % 2 == 0 else pB
                    for dc in range(8):
                        cop("tensor", lambda e, t=tc, n=nv, d=dc, p=pp: e.matmul(
                            p[:, 0:512], xq_sl(GV, d, t * 128, (t + 1) * 128),
                            wv_sl(d, n * 512, (n + 1) * 512),
                            start=(d == 0), stop=(d == 7)),
                            [f"xqG{GV}", "T32a"], [_nm(pp)])
                    cop("vector", lambda e, t=tc, n=nv, p=pp: e.tensor_scalar(
                        out=v_b[:, t * 2048 + n * 512:t * 2048 + (n + 1) * 512],
                        in0=p[:, 0:512],
                        scalar1=statb[:, t * 24 + 13 + GV:t * 24 + 14 + GV],
                        scalar2=amrep[:, IWV:IWV + 1], op0=AL.mult, op1=AL.mult),
                        [_nm(pp), statb[:], amrep[:]], [v_b[:]])
            S.add("sync", lambda e, t0_=t0: e.dma_start(
                out=xqg_scr.rearrange("(c p) t -> p c t", p=128)[:, :, t0_:t0_ + SC],
                in_=T32c[:, GG * 2048:(GG + 1) * 2048].rearrange(
                    "p (c t) -> p c t", t=256)),
                ins=[f"xqG{GG}"], outs=["xqg_scr"], dma="st_xqg")

        def a1_heavy_b(sc, nxt):
            t0 = sc * SC

            def pop(k):
                for _ in range(k):
                    if nxt:
                        nxt.pop(0)()

            pop(2)
            for tt in range(2):
                cop("tensor", lambda e, t=tt: e.transpose(
                    pT0[:, t * 128:(t + 1) * 128],
                    t1b[:, t * 128:(t + 1) * 128], identf[:]),
                    [t1b[:], identf[:]], [_nm(pT0)])
            mchains = []
            for tt in range(2):
                c0 = tt * 18
                X = xq2tm if tt == 0 else gscr
                Xn = xq2tm[:] if tt == 0 else "gscr"
                ops = [
                    ("vector", lambda e, c=c0, t=tt: e.tensor_copy(
                        out=t1tm[:, c:c + 16],
                        in_=pT0[:, t * 128:t * 128 + 16]),
                     [_nm(pT0)], [t1tm[:]]),
                    ("scalar", lambda e, c=c0, x=X: e.activation(
                        out=x[:, 0:16], in_=t1tm[:, c:c + 16], func=AF.Square,
                        bias=0.0, scale=1.0), [t1tm[:]], [Xn]),
                    ("vector", lambda e, c=c0, x=X: e.reduce_sum(
                        out=t1tm[:, c + 16:c + 17], in_=x[:, 0:16], axis=AX),
                     [Xn], [t1tm[:]]),
                    ("scalar", lambda e, c=c0: e.activation(
                        out=t1tm[:, c + 16:c + 17], in_=t1tm[:, c + 16:c + 17],
                        func=AF.Sqrt, bias=swrep[:, 7:8], scale=1.0 / 16.0),
                     [t1tm[:], swrep[:]], [t1tm[:]]),
                    ("vector", lambda e, c=c0: e.reciprocal(
                        out=t1tm[:, c + 17:c + 18], in_=t1tm[:, c + 16:c + 17]),
                     [t1tm[:]], [t1tm[:]]),
                    ("vector", lambda e, c=c0, x=X: e.tensor_tensor(
                        out=x[:, 0:16], in0=t1tm[:, c:c + 16], in1=ngk2bb[:],
                        op=AL.mult), [t1tm[:], ngk2bb[:]], [Xn]),
                    ("vector", lambda e, x=X: e.reduce_max(
                        out=x[:, 16:17], in_=x[:, 0:16], axis=AX,
                        apply_absolute_value=True), [Xn], [Xn]),
                    ("vector", lambda e, c=c0, x=X: e.tensor_scalar(
                        out=x[:, 16:17], in0=x[:, 16:17],
                        scalar1=t1tm[:, c + 17:c + 18], scalar2=1e-5,
                        op0=AL.mult, op1=AL.max), [Xn, t1tm[:]], [Xn]),
                    ("vector", lambda e, x=X: e.tensor_scalar(
                        out=x[:, 18:19], in0=x[:, 16:17], scalar1=1.0 / 127.0,
                        scalar2=None, op0=AL.mult), [Xn], [Xn]),
                    ("vector", lambda e, x=X: e.reciprocal(
                        out=x[:, 17:18], in_=x[:, 16:17]), [Xn], [Xn]),
                    ("vector", lambda e, c=c0, x=X: e.tensor_scalar(
                        out=x[:, 17:18], in0=x[:, 17:18],
                        scalar1=t1tm[:, c + 17:c + 18], scalar2=127.0,
                        op0=AL.mult, op1=AL.mult), [Xn, t1tm[:]], [Xn]),
                    ("vector", lambda e, x=X: e.tensor_scalar(
                        out=x[:, 0:16], in0=x[:, 0:16],
                        scalar1=x[:, 17:18], scalar2=MAGIC,
                        op0=AL.mult, op1=AL.add), [Xn], [Xn]),
                    ("vector", lambda e, x=X: e.tensor_scalar(
                        out=x[:, 0:16], in0=x[:, 0:16], scalar1=MAGIC,
                        scalar2=None, op0=AL.subtract), [Xn], [Xn]),
                ]
                mchains.append(ops)
            for i in range(13):
                for tt in range(2):
                    eng, fn, ri, ro = mchains[tt][i]
                    cop(eng, fn, ri, ro)
            for tt in range(2):
                X = xq2tm if tt == 0 else gscr
                Xn = xq2tm[:] if tt == 0 else "gscr"
                cop("tensor", lambda e, x=X: e.transpose(
                    pT0[0:16, 0:128], x[:, 0:16], identf[:]),
                    [Xn, identf[:]], [_nm(pT0)])
                cop("vector", lambda e, t=tt: e.tensor_copy(
                    out=xq2b[:, t * 128:(t + 1) * 128], in_=pT0[0:16, 0:128]),
                    [_nm(pT0)], [xq2b[:]])
                row_extract(X[:, 18:19], Xn, tt, 256, P=pT0)
            cop("scalar", lambda e: e.activation(
                out=i2row[0:1, :], in_=pT0[0:1, 256:512], func=AF.Copy,
                bias=0.0, scale=1.0), [_nm(pT0)], [i2row[:]])
            cop("tensor", lambda e: e.matmul(pT1[:, 0:256], ones_r[:],
                                             i2row[0:1, :], start=True, stop=True),
                [i2row[:], ones_r[:]], [_nm(pT1)])
            cop("scalar", lambda e: e.activation(out=t1b[:], in_=pT1[:, 0:256],
                                                 func=AF.Copy, bias=0.0, scale=1.0),
                [_nm(pT1)], [t1b[:]])
            pop(1)
            for oc in range(8):
                pp = pA if oc % 2 == 0 else pB
                cop("tensor", lambda e, o=oc, p=pp: e.matmul(
                    p[:, 0:256], wgk2_i[0:16, o * 128:(o + 1) * 128], xq2b[:],
                    start=True, stop=True), ["wgk2_i", xq2b[:]], [_nm(pp)])
                cop("vector", lambda e, p=pp: e.scalar_tensor_tensor(
                    out=gscr[:], in0=p[:, 0:256], scalar=amrep[:, IGK2:IGK2 + 1],
                    in1=t1b[:], op0=AL.mult, op1=AL.mult),
                    [_nm(pp), amrep[:], t1b[:]], [gscr[:]])
                cop("scalar", lambda e, o=oc: e.activation(
                    out=gkT_b[:, o * 256:(o + 1) * 256], in_=gscr[:],
                    func=AF.Sigmoid, bias=bgk2b[:, o:o + 1], scale=1.0),
                    [gscr[:], bgk2b[:]], [gkT_b[:]])
            for oc in range(8):
                cop("scalar", lambda e, o=oc: e.activation(
                    out=gkT_b[:, o * 256:(o + 1) * 256],
                    in_=gkT_b[:, o * 256:(o + 1) * 256], func=AF.Ln,
                    bias=0.0, scale=1.0), [gkT_b[:]], [gkT_b[:]])
                cop("vector", lambda e, o=oc: e.tensor_tensor_scan(
                    out=gkT_b[:, o * 256:(o + 1) * 256],
                    data0=gkT_b[:, o * 256:(o + 1) * 256],
                    data1=gkT_b[:, o * 256:(o + 1) * 256],
                    initial=0.0, op0=AL.add, op1=AL.bypass),
                    [gkT_b[:]], [gkT_b[:]])
            cumT = gkT_b
            pop(1)

            def qb_sl(kcg, a, b):
                return T32c[:, 10240 + kcg * 256 + a:10240 + kcg * 256 + b]

            def kb_sl(kcg, a, b):
                return T32c[:, 12288 + kcg * 256 + a:12288 + kcg * 256 + b]

            def kbt_sl(kcg, a, b):
                return T32c[:, 14336 + kcg * 256 + a:14336 + kcg * 256 + b]

            for kcg in range(8):
                cop("scalar", lambda e, k=kcg: e.activation(
                    out=eclb[:, k:k + 1],
                    in_=cumT[:, k * 256 + 255:k * 256 + 256],
                    func=AF.Exp, bias=0.0, scale=1.0 / 16.0),
                    [cumT[:]], [eclb[:]])
                for c in range(2):
                    cs_ = cumT[:, kcg * 256 + c * 128:kcg * 256 + (c + 1) * 128]
                    ep = expP[:, c * 128:(c + 1) * 128]
                    em = expM[:, c * 128:(c + 1) * 128]
                    cop("scalar", lambda e, i=cs_, o=ep: e.activation(
                        out=o, in_=i, func=AF.Exp, bias=0.0, scale=1.0 / 16.0),
                        [cumT[:]], [expP[:]])
                    cop("scalar", lambda e, i=cs_, o=em: e.activation(
                        out=o, in_=i, func=AF.Exp, bias=0.0, scale=-1.0 / 16.0),
                        [cumT[:]], [expM[:]])
                    cop("vector", lambda e, k=kcg, cc=c, o=ep: e.tensor_tensor(
                        out=qb_sl(k, cc * 128, (cc + 1) * 128),
                        in0=qT_b[:, k * 256 + cc * 128:k * 256 + (cc + 1) * 128],
                        in1=o, op=AL.mult),
                        [qT_b[:], expP[:]], ["scnQ"])
                    cop("vector", lambda e, k=kcg, cc=c, o=em: e.tensor_tensor(
                        out=kb_sl(k, cc * 128, (cc + 1) * 128),
                        in0=kT_b[:, k * 256 + cc * 128:k * 256 + (cc + 1) * 128],
                        in1=o, op=AL.mult),
                        [kT_b[:], expM[:]], ["scnK"])
                    cop("vector", lambda e, k=kcg, cc=c: e.tensor_scalar(
                        out=qg_st[:, k * 256 + cc * 128:k * 256 + (cc + 1) * 128],
                        in0=qb_sl(k, cc * 128, (cc + 1) * 128),
                        scalar1=egsum[:, k:k + 1], scalar2=None, op0=AL.mult),
                        ["scnQ", egsum[:]], [qg_st[:]])
                    par = (kcg * 2 + c) % 2
                    cop("tensor", lambda e, k=kcg, cc=c, p=par: e.transpose(
                        pBT[:, p * 256 + cc * 0:p * 256 + 128],
                        kb_sl(k, cc * 128, (cc + 1) * 128), identb[:]),
                        ["scnK", identb[:]], [f"pBT{par}"])
                    cop("scalar", lambda e, k=kcg, cc=c, p=par: e.activation(
                        out=kbt_sl(k, cc * 128, (cc + 1) * 128),
                        in_=pBT[:, p * 256:p * 256 + 128], func=AF.Copy,
                        bias=0.0, scale=1.0), [f"pBT{par}"], ["scnKT"])
            for h in range(HEADS):
                pop(2 if h < 2 else (2 if h == 2 else 1))
                for kc in range(2):
                    kcg = h * 2 + kc
                    pT = pT0 if kc == 0 else pT1
                    cop("tensor", lambda e, k=kcg, p=pT: e.matmul(
                        p[:], identb[:], Scar[:, k * 512:(k + 1) * 512],
                        start=True, stop=False), [identb[:], "Scar"], [_nm(pT)])
                for c in range(2):
                    hp = (h * 2 + c) % 2
                    atm = ATm if hp == 0 else ATm2
                    for kc in range(2):
                        kcg = h * 2 + kc
                        cop("tensor", lambda e, k=kcg, cc=c, p=hp,
                            st=(kc == 0), sp=(kc == 1): e.matmul(
                            pAT[:, p * 128:(p + 1) * 128],
                            kb_sl(k, cc * 128, (cc + 1) * 128),
                            qb_sl(k, cc * 128, (cc + 1) * 128),
                            start=st, stop=sp),
                            ["scnK", "scnQ"], [f"pAT{hp}"])
                    cop("vector", lambda e, p=hp, a=atm: e.tensor_tensor(
                        out=a[:], in0=pAT[:, p * 128:(p + 1) * 128],
                        in1=maskUb[:], op=AL.mult),
                        [f"pAT{hp}", maskUb[:]], [_nm(atm)])
                    for vc in range(4):
                        cop("tensor", lambda e, v=vc, cc=c, h_=h, a=atm: e.matmul(
                            pO[:, v * 128:(v + 1) * 128],
                            v_b[:, cc * 2048 + h_ * 512 + v * 128:
                                cc * 2048 + h_ * 512 + (v + 1) * 128],
                            a[:], start=True, stop=False),
                            [v_b[:], _nm(atm)], [pO[:]])
                        for kc in range(2):
                            kcg = h * 2 + kc
                            if c == 0:
                                lhs = Scar[:, kcg * 512 + vc * 128:
                                           kcg * 512 + (vc + 1) * 128]
                                ln = "Scar"
                            else:
                                lhs = T0e[:, kc * 512 + vc * 128:
                                          kc * 512 + (vc + 1) * 128]
                                ln = "T0e"
                            cop("tensor", lambda e, v=vc, k=kcg, cc=c, L=lhs,
                                last=(kc == 1): e.matmul(
                                    pO[:, v * 128:(v + 1) * 128], L,
                                    qb_sl(k, cc * 128, (cc + 1) * 128),
                                    start=False, stop=last),
                                [ln, "scnQ"], [pO[:]])
                    cop("scalar", lambda e, h_=h, cc=c: e.activation(
                        out=o_st[:].rearrange("p (v t) -> p v t", t=256)
                        [:, h_ * 4:(h_ + 1) * 4, cc * 128:(cc + 1) * 128],
                        in_=pO[:].rearrange("p (v t) -> p v t", t=128),
                        func=AF.Copy, bias=0.0, scale=1.0), [pO[:]], [o_st[:]])
                    for kc in range(2):
                        pT = pT0 if kc == 0 else pT1
                        kcg = h * 2 + kc
                        cop("tensor", lambda e, k=kcg, cc=c, h_=h, p=pT,
                            last=(c == 1): e.matmul(
                                p[:], kbt_sl(k, cc * 128, (cc + 1) * 128),
                                v_b[:, cc * 2048 + h_ * 512:
                                    cc * 2048 + (h_ + 1) * 512],
                                start=False, stop=last),
                            ["scnKT", v_b[:]], [_nm(pT)])
                    if c == 0:
                        for kc in range(2):
                            pT = pT0 if kc == 0 else pT1
                            cop("scalar", lambda e, k=kc, p=pT: e.activation(
                                out=T0e[:, k * 512:(k + 1) * 512], in_=p[:],
                                func=AF.Copy, bias=0.0, scale=1.0),
                                [_nm(pT)], ["T0e"])
                for kc in range(2):
                    pT = pT0 if kc == 0 else pT1
                    kcg = h * 2 + kc
                    cop("scalar", lambda e, k=kcg, p=pT: e.activation(
                        out=Scar[:, k * 512:(k + 1) * 512], in_=p[:], func=AF.Copy,
                        bias=0.0, scale=eclb[:, k:k + 1]),
                        [_nm(pT), eclb[:]], ["Scar"])
                    cop("vector", lambda e, k=kcg: e.tensor_tensor(
                        out=egsum[:, k:k + 1], in0=egsum[:, k:k + 1],
                        in1=eclb[:, k:k + 1], op=AL.mult),
                        [egsum[:], eclb[:]], [egsum[:]])
            dma("sync",
                o_scr.rearrange("(c p) t -> p c t", p=128)[:, :, t0:t0 + SC],
                o_st[:].rearrange("p (c t) -> p c t", t=256), "st_o")
            dma("sync",
                qg_scr.rearrange("(c p) t -> p c t", p=128)[:, :, t0:t0 + SC],
                qg_st[:].rearrange("p (c t) -> p c t", t=256), "st_qg")
            pop(len(nxt))
            emit_cast_units(8 if sc < 7 else 1000)
            tap(f"a1s{sc}")

        for f in a1_spine_chunks(0):
            f()
        for sc in range(NSC):
            a1_heavy_a(sc)
            nxt = a1_spine_chunks(sc + 1) if sc < NSC - 1 else []
            a1_heavy_b(sc, nxt)
            if sc == 1:
                dma("sync", xqg2[:].rearrange("p (c t) -> p c t", t=256),
                    xqg_scr.rearrange("(c p) t -> p c t", p=128)[:, :, 0:SC],
                    "xqg2")

        cop("gpsimd", lambda e: e.memset(sgb[0:1, 0:1], 0.0),
            ["sgbH0", "sgbH1"], [sgb[:]])

        dma("sync", cc_in.rearrange("(c p) v -> p c v", p=128),
            Scar[:].rearrange("p (c v) -> p c v", v=512), "st_Scar")
        S.add("gpsimd", lambda e: e.collective_compute(
            "AllGather", AL.bypass, [[0, 1], [2, 3], [4, 5], [6, 7]],
            ins=[cc_in.tensor.ap().opt()], outs=[cc_out.tensor.ap().opt()]),
            ins=["cc_in"], outs=["cc_out"], dma="cc")
        Sin = Scar
        dma("gpsimd", Sin[:].rearrange("p (c v) -> p c v", v=512),
            cc_out.rearrange("(c p) v -> p c v", p=128)[:, 0:8], "ld_Sin")
        tap("cc")

        dma("scalar", T32a[:].rearrange("p (c t) -> p c t", t=2048),
            wg_scr.rearrange("(c p) t -> p c t", p=128), "T32a")
        dma("scalar", T32b[:].rearrange("p (c t) -> p c t", t=1024),
            wo_scr.rearrange("(c p) t -> p c t", p=128), "T32b")

        def wg_sl(dc, a, b):
            return T32a[:, dc * DVT + a:dc * DVT + b]

        def wo_sl(vcg, a, b):
            return T32b[:, vcg * D + a:vcg * D + b]

        ot_bufs = (o_st, v_b)
        qgt_bufs = (qg_st, qT_b)
        xqgt_bufs = (xqg2, kT_b)

        def emit_ln_apply(src_sl2, src_name, dst_writer):
            cop("scalar", lambda e: e.activation(
                out=srow[0:1, 0:256], in_=pS[0:1, 0:256], func=AF.Copy,
                bias=0.0, scale=1.0 / 1024.0), [pS[:]], [srow[:]])
            cop("scalar", lambda e: e.activation(
                out=t1b[0:1, 0:256], in_=pS[0:1, 256:512], func=AF.Copy,
                bias=0.0, scale=1.0 / 1024.0), [pS[:]], [t1b[:]])
            GB = (gsml, t1tm)
            GN = (gsml[:], t1tm[:])
            PP = (pT0, pT1)
            for tt in range(2):
                cop("tensor", lambda e, t=tt, p=PP[tt]: e.transpose(
                    p[:, 0:128], srow[:, t * 128:(t + 1) * 128], identf[:]),
                    [srow[:], identf[:]], [_nm(PP[tt])])
                cop("tensor", lambda e, t=tt, p=PP[tt]: e.transpose(
                    p[:, 128:256], t1b[:, t * 128:(t + 1) * 128], identf[:]),
                    [t1b[:], identf[:]], [_nm(PP[tt])])
            lops = []
            for tt in range(2):
                g, gn, p = GB[tt], GN[tt], PP[tt]
                pn = _nm(p)
                lops.append([
                    ("vector", lambda e, g=g, p=p: e.tensor_copy(
                        out=g[:, 0:1], in_=p[:, 0:1]), [pn], [gn]),
                    ("vector", lambda e, g=g, p=p: e.tensor_copy(
                        out=g[:, 1:2], in_=p[:, 128:129]), [pn], [gn]),
                    ("scalar", lambda e, g=g: e.activation(
                        out=g[:, 2:3], in_=g[:, 0:1], func=AF.Square,
                        bias=0.0, scale=1.0), [gn], [gn]),
                    ("vector", lambda e, g=g: e.tensor_tensor(
                        out=g[:, 3:4], in0=g[:, 1:2], in1=g[:, 2:3],
                        op=AL.subtract), [gn], [gn]),
                    ("scalar", lambda e, g=g: e.activation(
                        out=g[:, 3:4], in_=g[:, 3:4], func=AF.Sqrt,
                        bias=amrep[:, 7:8], scale=1.0), [gn, amrep[:]], [gn]),
                    ("vector", lambda e, g=g: e.reciprocal(
                        out=g[:, 4:5], in_=g[:, 3:4]), [gn], [gn]),
                    ("vector", lambda e, g=g: e.tensor_tensor(
                        out=g[:, 5:6], in0=g[:, 0:1], in1=g[:, 4:5],
                        op=AL.mult), [gn], [gn]),
                    ("vector", lambda e, g=g: e.tensor_scalar(
                        out=g[:, 5:6], in0=g[:, 5:6], scalar1=-1.0,
                        scalar2=None, op0=AL.mult), [gn], [gn]),
                ])
            for i in range(8):
                for tt in range(2):
                    eng, fn, ri, ro = lops[tt][i]
                    cop(eng, fn, ri, ro)
            for tt in range(2):
                g, gn = GB[tt], GN[tt]
                cop("tensor", lambda e, t=tt, g=g, gn=gn: e.transpose(
                    pT0[0:1, 256 + t * 128:256 + (t + 1) * 128], g[:, 4:5],
                    identf[:]), [gn, identf[:]], [_nm(pT0)])
                cop("tensor", lambda e, t=tt, g=g, gn=gn: e.transpose(
                    pT1[0:1, 256 + t * 128:256 + (t + 1) * 128], g[:, 5:6],
                    identf[:]), [gn, identf[:]], [_nm(pT1)])
            cop("scalar", lambda e: e.activation(
                out=abr[0:1, 0:256], in_=pT0[0:1, 256:512], func=AF.Copy,
                bias=0.0, scale=1.0), [_nm(pT0)], [abr[:]])
            cop("scalar", lambda e: e.activation(
                out=abr[0:1, 256:512], in_=pT1[0:1, 256:512], func=AF.Copy,
                bias=0.0, scale=1.0), [_nm(pT1)], [abr[:]])
            cop("tensor", lambda e: e.matmul(
                pT0[:, 0:256], ones_r[:], abr[0:1, 0:256], start=True, stop=True),
                [abr[:], ones_r[:]], [_nm(pT0)])
            cop("scalar", lambda e: e.activation(
                out=Abc[:], in_=pT0[:, 0:256], func=AF.Copy, bias=0.0, scale=1.0),
                [_nm(pT0)], ["UbufG1"])
            cop("tensor", lambda e: e.matmul(
                pT1[:, 0:256], ones_r[:], abr[0:1, 256:512], start=True, stop=True),
                [abr[:], ones_r[:]], [_nm(pT1)])
            cop("scalar", lambda e: e.activation(
                out=Bbc[:], in_=pT1[:, 0:256], func=AF.Copy, bias=0.0, scale=1.0),
                [_nm(pT1)], ["UbufG2"])
            abr_ = Abc.rearrange("p (a t) -> p a t", a=1).to_broadcast(
                [128, 2, 256])
            for d2 in range(4):
                cop("vector", lambda e, d=d2, ab=abr_: e.tensor_tensor(
                    out=rscrw[:].rearrange("p (a t) -> p a t", t=256),
                    in0=src_sl2(d), in1=ab, op=AL.mult),
                    [src_name, "UbufG1"], ["rscrw"])
                dst_writer(d2)

        def a2_vars(sc):
            par = sc % 2
            return (sc * SC, par, ot_bufs[par], qgt_bufs[par], xqgt_bufs[par],
                    xt0 if par == 0 else xt1)

        def a2_front(sc, weave):
            t0, par, ot, qgt, xqgt, xt = a2_vars(sc)

            def wv(k):
                for _ in range(k):
                    if weave:
                        weave.pop(0)()

            dma("sync", ot[:].rearrange("p (c t) -> p c t", t=256),
                o_scr.rearrange("(c p) t -> p c t", p=128)[:, :, t0:t0 + SC],
                _nm(ot))
            dma("sync", qgt[:].rearrange("p (c t) -> p c t", t=256),
                qg_scr.rearrange("(c p) t -> p c t", p=128)[:, :, t0:t0 + SC],
                _nm(qgt))
            if sc > 0:
                dma("sync", xqgt[:].rearrange("p (c t) -> p c t", t=256),
                    xqg_scr.rearrange("(c p) t -> p c t", p=128)
                    [:, :, t0:t0 + SC], _nm(xqgt))
            dma("sync", xt[:].rearrange("p (c t) -> p c t", t=256),
                xT.rearrange("(c p) t -> p c t", p=128)[:, :, t0:t0 + SC],
                "a2x0" if par == 0 else "a2x1")

            def corr_part():
                for h in range(HEADS):
                    for vc in range(4):
                        vcg = h * 4 + vc
                        pp = pT0 if vc < 2 else pT1
                        ph = (vc % 2) * 256
                        cop("tensor", lambda e, v=vcg, p=pp, o=ot, q=ph: e.matmul(
                            p[:, q:q + 256], identb[:],
                            o[:, v * 256:(v + 1) * 256],
                            start=True, stop=False), [identb[:], ot[:]], [_nm(pp)])
                        for kc in range(2):
                            kcg = h * 2 + kc
                            cop("tensor", lambda e, k=kcg, v=vc, p=pp, q=qgt,
                                g=ph, last=(kc == 1): e.matmul(
                                    p[:, g:g + 256],
                                    Sin[:, k * 512 + v * 128:
                                        k * 512 + (v + 1) * 128],
                                    q[:, k * 256:(k + 1) * 256],
                                    start=False, stop=last),
                                ["Scar", qgt[:]], [_nm(pp)])
                        if vc % 2 == 1:
                            cop("scalar", lambda e, v=vcg, p=pp, o=ot:
                                e.activation(
                                    out=o[:, (v - 1) * 256:(v + 1) * 256],
                                    in_=p[:, 0:512], func=AF.Copy, bias=0.0,
                                    scale=1.0), [_nm(pp)], [ot[:]])
                    wv(1)
                wv(1)
            if sc > 0:
                corr_part()
            dma("sync", i2row[0:1, :], invsg_scr[sc:sc + 1, :], "i2row")
            i2a = Ubuf[:, 768:1024]
            cop("tensor", lambda e: e.matmul(
                pS[:, 0:256], ones_r[:], i2row[0:1, :], start=True, stop=True),
                [i2row[:], ones_r[:]], [pS[:]])
            cop("scalar", lambda e, o=i2a: e.activation(out=o, in_=pS[:, 0:256],
                                                 func=AF.Copy, bias=0.0, scale=1.0),
                [pS[:]], ["UbufG3"])
            for op2 in range(8):
                pp = pA if op2 % 2 == 0 else pB
                for oh in range(2):
                    ocg = op2 * 2 + oh
                    for dc in range(8):
                        cop("tensor", lambda e, o=ocg, d=dc, p=pp, x=xqgt,
                            h2=oh: e.matmul(
                                p[:, h2 * 256:(h2 + 1) * 256],
                                wg_sl(d, o * 128, (o + 1) * 128),
                                x[:, d * 256:(d + 1) * 256],
                                start=(d == 0), stop=(d == 7)),
                            ["T32a", xqgt[:]], [_nm(pp)])
                for oh in range(2):
                    gs = Ubuf[:, 1024:1280] if oh == 0 else gscr[:]
                    gn = "UbufG4" if oh == 0 else "gscr"
                    cop("vector", lambda e, p=pp, h2=oh, g=gs, iv=i2a:
                        e.scalar_tensor_tensor(
                            out=g, in0=p[:, h2 * 256:(h2 + 1) * 256],
                            scalar=amrep[:, IWG:IWG + 1], in1=iv,
                            op0=AL.mult, op1=AL.mult),
                        [_nm(pp), amrep[:], "UbufG3"], [gn])
                    cop("scalar", lambda e, o2=op2, h2=oh, g=gs: e.activation(
                        out=sgb[:, o2 * 512 + h2 * 256:o2 * 512 + (h2 + 1) * 256],
                        in_=g, func=AF.Silu, bias=0.0, scale=1.0),
                        [gn], [sgb[:]])
                if op2 <= 4:
                    wv(1)
            if sc == 0:
                corr_part()
            for v2 in range(8):
                cop("vector", lambda e, v=v2, o=ot: e.tensor_tensor(
                    out=o[:, v * 512:(v + 1) * 512],
                    in0=o[:, v * 512:(v + 1) * 512],
                    in1=sgb[:, v * 512:(v + 1) * 512], op=AL.mult),
                    [ot[:], sgb[:]], [ot[:]])
            wv(len(weave))

        def a2_back_chunks(sc):
            t0, par, ot, qgt, xqgt, xt = a2_vars(sc)
            out = []

            def mk_zf(q):
                def zf():
                    for vcg in range(q * 4, q * 4 + 4):
                        if vcg == 0:
                            cop("vector", lambda e, o=ot: e.tensor_scalar(
                                out=mfold[:, 0:256], in0=o[:, 0:256],
                                scalar1=nob[:, 0:1], scalar2=None, op0=AL.mult),
                                [ot[:], nob[:]], [mfold[:]])
                            cop("vector", lambda e: e.tensor_copy(
                                out=mfold[:, 256:512], in_=mfold[:, 0:256]),
                                [mfold[:]], [mfold[:]])
                        else:
                            cop("vector", lambda e, o=ot, v=vcg:
                                e.scalar_tensor_tensor(
                                    out=mfold[:, 0:256],
                                    in0=o[:, v * 256:(v + 1) * 256],
                                    scalar=nob[:, v:v + 1], in1=mfold[:, 0:256],
                                    op0=AL.mult, op1=AL.max),
                                [ot[:], nob[:], mfold[:]], [mfold[:]])
                            cop("vector", lambda e, o=ot, v=vcg:
                                e.scalar_tensor_tensor(
                                    out=mfold[:, 256:512],
                                    in0=o[:, v * 256:(v + 1) * 256],
                                    scalar=nob[:, v:v + 1], in1=mfold[:, 256:512],
                                    op0=AL.mult, op1=AL.min),
                                [ot[:], nob[:], mfold[:]], [mfold[:]])
                        cop("scalar", lambda e, o=ot, v=vcg: e.activation(
                            out=gscr[:], in_=o[:, v * 256:(v + 1) * 256],
                            func=AF.Square, bias=0.0, scale=1.0),
                            [ot[:]], ["gscr"])
                        cop("tensor", lambda e, first=(vcg == 0), last=(vcg == 15):
                            e.matmul(pS[0:1, 0:256], ones_c[:], gscr[:],
                                     start=first, stop=last),
                            ["gscr", ones_c[:]], [pS[:]])
                return zf

            def zstat():
                cop("vector", lambda e: e.tensor_scalar(
                    out=mfold[:, 256:512], in0=mfold[:, 256:512], scalar1=-1.0,
                    scalar2=None, op0=AL.mult), [mfold[:]], [mfold[:]])
                cop("vector", lambda e: e.tensor_tensor(
                    out=mfold[:, 0:256], in0=mfold[:, 0:256],
                    in1=mfold[:, 256:512], op=AL.max), [mfold[:]], [mfold[:]])
                cop("scalar", lambda e: e.activation(
                    out=srow[0:1, :], in_=pS[0:1, 0:256], func=AF.Copy,
                    bias=0.0, scale=1.0), [pS[:]], [srow[:]])
                for tt in range(2):
                    sb_ = statb[:, tt * 24:(tt + 1) * 24]
                    cop("tensor", lambda e, t=tt: e.transpose(
                        pS[:, 0:128], srow[:, t * 128:(t + 1) * 128], identf[:]),
                        [srow[:], identf[:]], [pS[:]])
                    cop("vector", lambda e, s=sb_: e.tensor_copy(
                        out=s[:, 0:1], in_=pS[:, 0:1]), [pS[:]], [statb[:]])
                    cop("tensor", lambda e, t=tt: e.transpose(
                        pS[:, 0:128], mfold[:, t * 128:(t + 1) * 128], identf[:]),
                        [mfold[:], identf[:]], [pS[:]])
                    cop("vector", lambda e, s=sb_: e.reduce_max(
                        out=s[:, 1:2], in_=pS[:, 0:128], axis=AX),
                        [pS[:]], [statb[:]])
                    cop("scalar", lambda e, s=sb_: e.activation(
                        out=s[:, 6:7], in_=s[:, 0:1], func=AF.Sqrt,
                        bias=swrep[:, 7:8], scale=1.0 / 2048.0),
                        [statb[:], swrep[:]], [statb[:]])
                    cop("vector", lambda e, s=sb_: e.reciprocal(
                        out=s[:, 7:8], in_=s[:, 6:7]), [statb[:]], [statb[:]])
                    cop("vector", lambda e, s=sb_: e.tensor_scalar(
                        out=s[:, 8:9], in0=s[:, 1:2], scalar1=s[:, 7:8],
                        scalar2=1e-5, op0=AL.mult, op1=AL.max),
                        [statb[:]], [statb[:]])
                    cop("vector", lambda e, s=sb_: e.tensor_scalar(
                        out=s[:, 13:14], in0=s[:, 8:9], scalar1=1.0 / 127.0,
                        scalar2=None, op0=AL.mult), [statb[:]], [statb[:]])
                    cop("vector", lambda e, s=sb_: e.reciprocal(
                        out=s[:, 18:19], in_=s[:, 8:9]), [statb[:]], [statb[:]])
                    cop("vector", lambda e, s=sb_: e.tensor_scalar(
                        out=s[:, 18:19], in0=s[:, 18:19], scalar1=s[:, 7:8],
                        scalar2=127.0, op0=AL.mult, op1=AL.mult),
                        [statb[:]], [statb[:]])
            out = [mk_zf(q) for q in range(4)] + [zstat]

            def zext():
                for tt in range(2):
                    row_extract(statb[:, tt * 24 + 18:tt * 24 + 19],
                                "statb", tt, 0)
                    row_extract(statb[:, tt * 24 + 13:tt * 24 + 14],
                                "statb", tt, 256)
                cop("scalar", lambda e: e.activation(
                    out=urow5[0:1, 0:256], in_=pS[0:1, 0:256], func=AF.Copy,
                    bias=0.0, scale=1.0), [pS[:]], [urow5[:]])
                cop("scalar", lambda e: e.activation(
                    out=irow5[0:1, 0:256], in_=pS[0:1, 256:512], func=AF.Copy,
                    bias=0.0, scale=1.0), [pS[:]], [irow5[:]])
                cop("tensor", lambda e: e.matmul(
                    pS[:, 0:256], ones_r[:], urow5[0:1, 0:256],
                    start=True, stop=True), [urow5[:], ones_r[:]], [pS[:]])
                cop("scalar", lambda e: e.activation(
                    out=Ubuf[:, 0:256], in_=pS[:, 0:256], func=AF.Copy, bias=0.0,
                    scale=1.0), [pS[:]], ["UbufG0"])
                cop("tensor", lambda e: e.matmul(
                    pS[:, 0:256], ones_r[:], irow5[0:1, 0:256],
                    start=True, stop=True), [irow5[:], ones_r[:]], [pS[:]])
                cop("scalar", lambda e: e.activation(
                    out=invsb[:, 0:256], in_=pS[:, 0:256], func=AF.Copy, bias=0.0,
                    scale=1.0), [pS[:]], [invsb[:]])

            def mk_zr(q):
                def zr():
                    for vcg in range(q * 4, q * 4 + 4):
                        rs = rscr if vcg % 2 == 0 else rscr2
                        cop("vector", lambda e, v=vcg, o=ot, r=rs:
                            e.scalar_tensor_tensor(
                                out=r, in0=o[:, v * 256:(v + 1) * 256],
                                scalar=nob[:, v:v + 1], in1=Ubuf[:, 0:256],
                                op0=AL.mult, op1=AL.mult),
                            [ot[:], nob[:], "UbufG0"], ["rscrw"])
                        if vcg % 2 == 1:
                            cop("vector", lambda e, v=vcg, o=ot: e.tensor_scalar(
                                out=o[:, (v - 1) * 256:(v + 1) * 256],
                                in0=rscrw[:], scalar1=MAGIC, scalar2=MAGIC,
                                op0=AL.add, op1=AL.subtract),
                                ["rscrw"], [ot[:]])
                return zr

            return out + [zext] + [mk_zr(q) for q in range(4)]

        def a2_tail(sc):
            t0, par, ot, qgt, xqgt, xt = a2_vars(sc)
            for op2 in range(4):
                pp = pAT if op2 % 2 == 0 else pO
                for oh in range(2):
                    oc = op2 * 2 + oh
                    for vcg in range(16):
                        cop("tensor", lambda e, o=oc, v=vcg, p=pp, z=ot,
                            h2=oh: e.matmul(
                                p[:, h2 * 256:(h2 + 1) * 256],
                                wo_sl(v, o * 128, (o + 1) * 128),
                                z[:, v * 256:(v + 1) * 256],
                                start=(v == 0), stop=(v == 15)),
                            ["T32b", ot[:]], [_nm(pp)])
                ivr = invsb[:, 0:256].rearrange(
                    "p (a t) -> p a t", a=1).to_broadcast([128, 2, 256])
                cop("vector", lambda e, p=pp, iv=ivr: e.scalar_tensor_tensor(
                    out=rscrw[:].rearrange("p (a t) -> p a t", t=256),
                    in0=p[:, 0:512].rearrange("p (a t) -> p a t", t=256),
                    scalar=amrep[:, IWO:IWO + 1],
                    in1=iv, op0=AL.mult, op1=AL.mult),
                    [_nm(pp), amrep[:], invsb[:]], ["rscrw"])
                cop("vector", lambda e, o2=op2, x=xt: e.tensor_tensor(
                    out=gkT_b[:, o2 * 512:(o2 + 1) * 512], in0=rscrw[:],
                    in1=x[:, o2 * 512:(o2 + 1) * 512], op=AL.add),
                    ["rscrw", xt[:]], ["gkT_b"])
            for dc in range(8):
                cop("tensor", lambda e, d=dc: e.matmul(
                    pS[0:1, 0:256], ones_c[:], gkT_b[:, d * 256:(d + 1) * 256],
                    start=(d == 0), stop=(d == 7)),
                    ["gkT_b", ones_c[:]], [pS[:]])
            for d2 in range(4):
                cop("scalar", lambda e, d=d2: e.activation(
                    out=sqw[:], in_=gkT_b[:, d * 512:(d + 1) * 512],
                    func=AF.Square, bias=0.0, scale=1.0), ["gkT_b"], ["sqw"])
                cop("tensor", lambda e, d=d2: e.matmul(
                    pS[0:1, 256:512], ones_c[:], sqw[:, 0:256],
                    start=(d == 0), stop=False), ["sqw", ones_c[:]], [pS[:]])
                cop("tensor", lambda e, d=d2: e.matmul(
                    pS[0:1, 256:512], ones_c[:], sqw[:, 256:512],
                    start=False, stop=(d == 3)), ["sqw", ones_c[:]], [pS[:]])
            bbr_ = Bbc.rearrange("p (a t) -> p a t", a=1).to_broadcast(
                [128, 2, 256])
            emit_ln_apply(
                lambda d: gkT_b[:, d * 512:(d + 1) * 512].rearrange(
                    "p (a t) -> p a t", t=256), "gkT_b",
                lambda d2: cop("vector", lambda e, d=d2, t0_=t0, bb=bbr_:
                               e.tensor_tensor(
                    out=T32c[:].rearrange("p (c t) -> p c t", t=2048)
                    [:, 2 * d:2 * d + 2, t0_:t0_ + SC],
                    in0=rscrw[:].rearrange("p (a t) -> p a t", t=256),
                    in1=bb, op=AL.add),
                    ["rscrw", "UbufG2"], ["T32c"]))
            tap(f"a2s{sc}")

        a2_front(0, [])
        for sc in range(NSC):
            bch = a2_back_chunks(sc)
            if sc < NSC - 1:
                a2_front(sc + 1, bch)
            else:
                for f in bch:
                    f()
            a2_tail(sc)

        def xp_sl(dc, a, b):
            return T32c[:, dc * 2048 + a:dc * 2048 + b]

        def h_sl(unit, a, b):
            if unit < 16:
                return T32a[:, unit * 1024 + a:unit * 1024 + b]
            return T32b[:, (unit - 16) * 1024 + a:(unit - 16) * 1024 + b]

        def h_name(unit):
            return "T32a" if unit < 16 else "T32b"

        def ffnacc_sl(oc, a, b):
            return T32b[:, 8192 + oc * 1024 + a:8192 + oc * 1024 + b]

        dma("sync", rwb[:].rearrange("p (c t) -> p c t", t=16),
            s_rw.rearrange("(c p) t -> p c t", p=128), "rwb")
        GRPS = [list(range(0, 10)), list(range(10, 18))]
        lgb = mfold
        sgf = mfold

        for tile in range(2):
            tb = tile * 1024
            for ns in range(2):
                for dc in range(8):
                    cop("tensor", lambda e, n=ns, d=dc, tb_=tb: e.matmul(
                        pS[0:16, 0:512], rwb[:, d * 16:(d + 1) * 16],
                        xp_sl(d, tb_ + n * 512, tb_ + (n + 1) * 512),
                        start=(d == 0), stop=(d == 7)),
                        [rwb[:], "T32c"], [pS[:]])
                cop("scalar", lambda e, n=ns: e.activation(
                    out=lgb[0:16, n * 512:(n + 1) * 512], in_=pS[0:16, 0:512],
                    func=AF.Copy, bias=0.0, scale=1.0), [pS[:]], ["mfold"])
            Lr = rscrw[:, 0:128]
            Tr = rscrw[:, 128:256]
            smr = rscrw[:, 256:288]
            L3 = Lr.rearrange("p (a t) -> p a t", t=16)
            T3 = Tr.rearrange("p (a t) -> p a t", t=16)

            def sm3(c0):
                return smr[:, c0:c0 + 8].rearrange("p (a o) -> p a o", o=1)

            def smb(c0):
                return smr[:, c0:c0 + 8].rearrange(
                    "p (a o) -> p a o", o=1).to_broadcast([128, 8, 16])

            for tg in range(8):
                rot = tg % 4
                cop("tensor", lambda e, t=tg, r=rot: e.transpose(
                    pS[:, r * 128:(r + 1) * 128],
                    lgb[:, t * 128:(t + 1) * 128], identf[:]),
                    ["mfold", identf[:]], [pS[:]])
                cop("vector", lambda e, t=tg, r=rot: e.tensor_copy(
                    out=Lr[:, t * 16:(t + 1) * 16],
                    in_=pS[:, r * 128:r * 128 + 16]), [pS[:]], ["rscrw"])
            cop("vector", lambda e: e.reduce_max(
                out=sm3(0), in_=L3, axis=AX), ["rscrw"], ["rscrw"])
            cop("vector", lambda e: e.tensor_tensor(
                out=L3, in0=L3, in1=smb(0), op=AL.subtract),
                ["rscrw"], ["rscrw"])
            cop("scalar", lambda e: e.activation(
                out=Lr, in_=Lr, func=AF.Exp, bias=0.0, scale=1.0),
                ["rscrw"], ["rscrw"])
            cop("vector", lambda e: e.reduce_sum(
                out=sm3(8), in_=L3, axis=AX), ["rscrw"], ["rscrw"])
            cop("vector", lambda e: e.reciprocal(
                out=smr[:, 8:16], in_=smr[:, 8:16]), ["rscrw"], ["rscrw"])
            cop("vector", lambda e: e.tensor_tensor(
                out=L3, in0=L3, in1=smb(8), op=AL.mult), ["rscrw"], ["rscrw"])
            cop("vector", lambda e: e.reduce_max(
                out=sm3(0), in_=L3, axis=AX), ["rscrw"], ["rscrw"])
            cop("vector", lambda e: e.tensor_tensor(
                out=T3, in0=L3, in1=smb(0), op=AL.is_equal),
                ["rscrw"], ["rscrw"])
            cop("vector", lambda e: e.scalar_tensor_tensor(
                out=Tr, in0=Tr, scalar=-1e30, in1=Lr, op0=AL.mult, op1=AL.add),
                ["rscrw"], ["rscrw"])
            cop("vector", lambda e: e.reduce_max(
                out=sm3(16), in_=T3, axis=AX), ["rscrw"], ["rscrw"])
            cop("vector", lambda e: e.scalar_tensor_tensor(
                out=smr[:, 24:32], in0=smr[:, 0:8], scalar=1e-9,
                in1=smr[:, 16:24], op0=AL.add, op1=AL.add),
                ["rscrw"], ["rscrw"])
            cop("vector", lambda e: e.reciprocal(
                out=smr[:, 24:32], in_=smr[:, 24:32]), ["rscrw"], ["rscrw"])
            cop("vector", lambda e: e.tensor_tensor(
                out=T3, in0=L3, in1=smb(16), op=AL.is_ge), ["rscrw"], ["rscrw"])
            cop("vector", lambda e: e.tensor_tensor(
                out=L3, in0=L3, in1=smb(24), op=AL.mult), ["rscrw"], ["rscrw"])
            cop("vector", lambda e: e.tensor_tensor(
                out=Tr, in0=Tr, in1=Lr, op=AL.mult), ["rscrw"], ["rscrw"])
            for tg in range(8):
                rot = tg % 4
                cop("tensor", lambda e, t=tg, r=rot: e.transpose(
                    pS[0:16, r * 128:(r + 1) * 128],
                    Tr[:, t * 16:(t + 1) * 16], identf[:]),
                    ["rscrw", identf[:]], [pS[:]])
                cop("scalar", lambda e, t=tg, r=rot: e.activation(
                    out=ctile[0:16, t * 128:(t + 1) * 128],
                    in_=pS[0:16, r * 128:(r + 1) * 128],
                    func=AF.Copy, bias=0.0, scale=1.0), [pS[:]], [ctile[:]])
            tap(f"rt{tile}")

            for gi, grp in enumerate(GRPS):
                unit = 0
                units_of = {}
                for e_id in grp:
                    routed = e_id < 16
                    mid = MIDR if routed else MIDS
                    nkc = mid // 128
                    units_of[e_id] = (unit, nkc)
                    wg_s_ = s_wgr[e_id] if routed else s_wgs[e_id - 16]
                    wu_s_ = s_wur[e_id] if routed else s_wus[e_id - 16]
                    if routed:
                        wgt = (o_st, v_b)[e_id % 2]
                        wut = (kT_b, xqg2)[e_id % 2]
                    else:
                        wgt = sgb
                        wut = (o_st, v_b)[e_id % 2]
                    dma("sync", wgt[:, 0:8 * mid].rearrange(
                        "p (c t) -> p c t", t=mid),
                        wg_s_.rearrange("(c p) t -> p c t", p=128),
                        "bwg_" + _nm(wgt))
                    dma("gpsimd", wut[:, 0:8 * mid].rearrange(
                        "p (c t) -> p c t", t=mid),
                        wu_s_.rearrange("(c p) t -> p c t", p=128), _nm(wut))
                    for ns2 in range(2):
                        ta = tb + ns2 * 512
                        if routed:
                            cop("tensor", lambda e, eid=e_id, n=ns2: e.matmul(
                                pS[:, 0:512],
                                sel16[0:16, eid * 128:(eid + 1) * 128],
                                ctile[0:16, n * 512:(n + 1) * 512],
                                start=True, stop=True),
                                [sel16[:], ctile[:]], [pS[:]])
                        pg = pA if ns2 == 0 else pAT
                        pu = pB if ns2 == 0 else pO
                        sgfv = sgf[:, 0:512] if ns2 == 0 else sgf[:, 512:1024]
                        for m in range(nkc):
                            for dc in range(8):
                                cop("tensor", lambda e, mm=m, d=dc, a=ta, w=wgt,
                                    md=mid, p=pg: e.matmul(
                                        p[:, 0:512],
                                        w[:, d * md + mm * 128:
                                          d * md + (mm + 1) * 128],
                                        xp_sl(d, a, a + 512),
                                        start=(d == 0), stop=(d == 7)),
                                    [_nm(wgt), "T32c"], [_nm(pg)])
                            for dc in range(8):
                                cop("tensor", lambda e, mm=m, d=dc, a=ta, w=wut,
                                    md=mid, p=pu: e.matmul(
                                        p[:, 0:512],
                                        w[:, d * md + mm * 128:
                                          d * md + (mm + 1) * 128],
                                        xp_sl(d, a, a + 512),
                                        start=(d == 0), stop=(d == 7)),
                                    [_nm(wut), "T32c"], [_nm(pu)])
                            cop("scalar", lambda e, p=pg, sg=sgfv: e.activation(
                                out=sg, in_=p[:, 0:512], func=AF.Silu,
                                bias=0.0, scale=1.0), [_nm(pg)], ["mfold"])
                            if routed:
                                cop("vector", lambda e, p=pu, sg=sgfv: e.tensor_tensor(
                                    out=hh[:], in0=sg, in1=p[:, 0:512],
                                    op=AL.mult), ["mfold", _nm(pu)], [hh[:]])
                                cop("vector", lambda e, u=unit, mm=m, n=ns2:
                                    e.tensor_tensor(
                                        out=h_sl(u + mm, n * 512, (n + 1) * 512),
                                        in0=hh[:], in1=pS[:, 0:512], op=AL.mult),
                                    [hh[:], pS[:]], [h_name(unit + m)])
                            else:
                                cop("vector", lambda e, u=unit, mm=m, n=ns2,
                                    p=pu, sg=sgfv: e.tensor_tensor(
                                        out=h_sl(u + mm, n * 512, (n + 1) * 512),
                                        in0=sg, in1=p[:, 0:512],
                                        op=AL.mult),
                                    ["mfold", _nm(pu)], [h_name(unit + m)])
                    unit += nkc
                DPS = {(0, 0): pT0, (1, 0): pT1, (0, 1): pAT, (1, 1): pO}
                for dg in range(4):
                    for ei, e_id in enumerate(grp):
                        routed = e_id < 16
                        u0, nkc = units_of[e_id]
                        wd_s_ = s_wdr[e_id] if routed else s_wds[e_id - 16]
                        if routed:
                            wdt = (qg_st, qT_b)[e_id % 2]
                            so = ((ei // 2) % 2) * 512
                        else:
                            wdt = sgb if e_id == 16 else o_st
                            so = 0
                        sname = (f"wds_{_nm(wdt)}{so}" if routed
                                 else _nm(wdt))
                        dma_key = "wd17" if e_id == 17 else "wd_" + _nm(wdt)
                        S.add("sync" if ei % 2 == 0 else "gpsimd",
                              lambda e, w=wdt, so_=so, nk=nkc, dg_=dg, ws=wd_s_:
                              e.dma_start(
                                  out=w[:, so_:so_ + nk * 256].rearrange(
                                      "p (c t) -> p c t", t=256),
                                  in_=ws.rearrange("(c p) t -> p c t", p=128)
                                  [:, :, dg_ * 256:(dg_ + 1) * 256]),
                              ins=[_nm(wd_s_)], outs=[sname, _nm(wdt)],
                              dma=dma_key)
                        for oo in range(2):
                            oc = dg * 2 + oo
                            for ns2 in range(2):
                                pp = DPS[(oo, ns2)]
                                if gi == 1 and ei == 0:
                                    cop("tensor", lambda e, o=oc, n=ns2, p=pp:
                                        e.matmul(
                                            p[:, 0:512], identb[:],
                                            ffnacc_sl(o, n * 512, (n + 1) * 512),
                                            start=True, stop=False),
                                        [identb[:], "T32b"], [_nm(pp)])
                                for kc in range(nkc):
                                    cop("tensor", lambda e, o=oc, n=ns2, k=kc,
                                        u=u0, p=pp, w=wdt, so_=so,
                                        st=(gi == 0 and ei == 0 and kc == 0),
                                        sp=(ei == len(grp) - 1 and kc == nkc - 1):
                                        e.matmul(
                                            p[:, 0:512],
                                            w[:, so_ + k * 256 + (o % 2) * 128:
                                              so_ + k * 256 + (o % 2 + 1) * 128],
                                            h_sl(u + k, n * 512, (n + 1) * 512),
                                            start=st, stop=sp),
                                        [sname, h_name(u0 + kc)], [_nm(pp)])
                                if ei == len(grp) - 1:
                                    cop("scalar", lambda e, o=oc, n=ns2, p=pp:
                                        e.activation(
                                            out=ffnacc_sl(o, n * 512,
                                                          (n + 1) * 512),
                                            in_=p[:, 0:512], func=AF.Copy,
                                            bias=0.0, scale=1.0),
                                        [_nm(pp)], ["T32b"])
            tap(f"bt{tile}")
            for sci in range(4):
                t0 = tb + sci * SC
                outf = xt0 if sci % 2 == 0 else xt1
                for d2 in range(4):
                    cop("vector", lambda e, d=d2, t0_=t0, tb_=tb: e.tensor_tensor(
                        out=gkT_b[:, d * 512:(d + 1) * 512].rearrange(
                            "p (a t) -> p a t", t=256),
                        in0=T32c[:].rearrange("p (c t) -> p c t", t=2048)
                        [:, 2 * d:2 * d + 2, t0_:t0_ + SC],
                        in1=T32b[:, 8192:16384].rearrange(
                            "p (c t) -> p c t", t=1024)
                        [:, 2 * d:2 * d + 2, t0_ - tb_:t0_ - tb_ + SC],
                        op=AL.add),
                        ["T32c", "T32b"], ["gkT_b"])
                for dc in range(8):
                    cop("tensor", lambda e, d=dc: e.matmul(
                        pS[0:1, 0:256], ones_c[:],
                        gkT_b[:, d * 256:(d + 1) * 256],
                        start=(d == 0), stop=(d == 7)),
                        ["gkT_b", ones_c[:]], [pS[:]])
                for d2 in range(4):
                    cop("scalar", lambda e, d=d2: e.activation(
                        out=sqw[:], in_=gkT_b[:, d * 512:(d + 1) * 512],
                        func=AF.Square, bias=0.0, scale=1.0), ["gkT_b"], ["sqw"])
                    cop("tensor", lambda e, d=d2: e.matmul(
                        pS[0:1, 256:512], ones_c[:], sqw[:, 0:256],
                        start=(d == 0), stop=False), ["sqw", ones_c[:]], [pS[:]])
                    cop("tensor", lambda e, d=d2: e.matmul(
                        pS[0:1, 256:512], ones_c[:], sqw[:, 256:512],
                        start=False, stop=(d == 3)), ["sqw", ones_c[:]], [pS[:]])
                bbr2 = Bbc.rearrange("p (a t) -> p a t", a=1).to_broadcast(
                    [128, 2, 256])
                emit_ln_apply(
                    lambda d: gkT_b[:, d * 512:(d + 1) * 512].rearrange(
                        "p (a t) -> p a t", t=256), "gkT_b",
                    lambda d2: cop("vector", lambda e, d=d2, o=outf, bb=bbr2:
                                   e.tensor_tensor(
                                       out=o[:, d * 512:(d + 1) * 512].rearrange(
                                           "p (a t) -> p a t", t=256),
                                       in0=rscrw[:].rearrange(
                                           "p (a t) -> p a t", t=256),
                                       in1=bb, op=AL.add),
                                   ["rscrw", "UbufG2"], [outf[:]]))
                dma("sync",
                    out_d.rearrange("(c p) t -> p c t", p=128)[:, :, t0:t0 + SC],
                    outf[:].rearrange("p (c t) -> p c t", t=256), "st_out")

        with nc.Block() as block:
            S.emit(nc, block, stack)

    print(f"[build] ops={len(S.ops)}")
    return nc


_NC_CACHE = {}


def _get_nc():
    if "nc" not in _NC_CACHE:
        _NC_CACHE["nc"] = build_program()
    return _NC_CACHE["nc"]


def kernel(**inp):
    nc = _get_nc()
    f32 = np.float32
    tr = lambda a: np.ascontiguousarray(np.asarray(a, dtype=f32).T)
    sel = np.zeros((16, 2048), dtype=BF)
    for e in range(16):
        sel[e, e * 128:(e + 1) * 128] = 1
    base = {
        "wqT": tr(inp["L0_attn_wq"]),
        "wkT": tr(inp["L0_attn_wk"]),
        "wvT": tr(inp["L0_attn_wv"]),
        "wgT": tr(inp["L0_attn_wg"]),
        "wgk1T": tr(inp["L0_attn_wgk1"]),
        "wgk2T": tr(inp["L0_attn_wgk2"]),
        "woT": tr(inp["L0_attn_wo"]),
        "f_rw": np.ascontiguousarray(np.asarray(inp["L0_ffn_router_w"], f32)),
        "f_wg_r": np.ascontiguousarray(np.asarray(inp["L0_ffn_wg_r"], f32)),
        "f_wu_r": np.ascontiguousarray(np.asarray(inp["L0_ffn_wu_r"], f32)),
        "f_wd_r": np.ascontiguousarray(np.asarray(inp["L0_ffn_wd_r"], f32)),
        "f_wg_s": np.ascontiguousarray(np.asarray(inp["L0_ffn_wg_s"], f32)),
        "f_wu_s": np.ascontiguousarray(np.asarray(inp["L0_ffn_wu_s"], f32)),
        "f_wd_s": np.ascontiguousarray(np.asarray(inp["L0_ffn_wd_s"], f32)),
        "ident_f": np.eye(128, dtype=f32),
        "ident_b": np.eye(128).astype(BF),
        "maskU": np.triu(np.ones((128, 128), f32)),
        "sel16_d": sel,
        "ngk2_b": np.ascontiguousarray(
            np.broadcast_to(np.asarray(inp["L0_attn_ngk2"], f32), (128, 16))),
        "bgk2_c": np.ascontiguousarray(
            np.asarray(inp["L0_attn_bgk2"], f32).reshape(8, 128).T),
        "no_c": np.ascontiguousarray(
            np.asarray(inp["L0_attn_no"], f32).reshape(16, 128).T),
    }
    g = np.stack([np.asarray(inp["L0_attn_" + k], f32).reshape(8, 128).T
                  for k in ("nq", "nk", "nv", "ng", "ngk1")], axis=1)
    base["gains"] = np.ascontiguousarray(g.reshape(128, 40))

    x = np.asarray(inp["L0_attn_hidden_states"], f32)
    in_maps = []
    for core in range(8):
        b, half = core // 2, core % 2
        m = dict(base)
        m["xT"] = np.ascontiguousarray(x[b, half * T:(half + 1) * T, :].T)
        m["alpha"] = np.full((128, 1), float(half), f32)
        in_maps.append(m)
    res = run_bass_kernel_spmd(nc, in_maps, core_ids=list(range(8)))
    out = np.empty((4, 4096, 1024), f32)
    for core in range(8):
        b, half = core // 2, core % 2
        out[b, half * T:(half + 1) * T, :] = res.results[core]["out"].T
    return out


# version 49
# speedup vs baseline: 1.0213x; 1.0132x over previous
import numpy as np
import ml_dtypes
import concourse.bass as bass
import concourse.mybir as mybir
from concourse.bass_utils import run_bass_kernel_spmd
from contextlib import ExitStack

F32 = mybir.dt.float32
BF16 = mybir.dt.bfloat16
AX = mybir.AxisListType.X
AL = mybir.AluOpType
AF = mybir.ActivationFunctionType
BF = ml_dtypes.bfloat16
MAGIC = 12582912.0

D = 1024
T = 2048
NSC = 8
SC = 256
HEADS = 4
DVT = 2048
MIDR = 256
MIDS = 512
NR = 16
NS = 2

ENGINES = ("sync", "vector", "scalar", "tensor", "gpsimd")


def _nm(ap):
    t = getattr(ap, "tensor", None)
    return t.name if t is not None else ap.name


class Sched:
    def __init__(self):
        self.ops = []
        self.lastw = {}
        self.readers = {}

    def add(self, eng, fn, ins=(), outs=(), dma=None, tag=""):
        reads = [a if isinstance(a, str) else _nm(a) for a in ins]
        writes = [a if isinstance(a, str) else _nm(a) for a in outs]
        deps = set()
        for r in reads:
            i = self.lastw.get(r)
            if i is not None:
                deps.add(i)
        for w in writes:
            i = self.lastw.get(w)
            if i is not None:
                deps.add(i)
            for rd in self.readers.get(w, ()):
                deps.add(rd)
        idx = len(self.ops)
        self.ops.append(dict(eng=eng, fn=fn, deps=deps, dma=dma, tag=tag))
        for r in reads:
            self.readers.setdefault(r, []).append(idx)
        for w in writes:
            self.lastw[w] = idx
            self.readers[w] = []
        return idx

    def emit(self, nc, block, stack):
        n = len(self.ops)
        dkeys = sorted({op["dma"] for op in self.ops if op["dma"] is not None})
        esem = {e: stack.enter_context(nc.semaphore(f"pg_{e}")) for e in ENGINES}
        dsem = {k: stack.enter_context(nc.semaphore(f"dm_{k}")) for k in dkeys}
        inc_needed = [False] * n

        def same_safe(oi, op):
            return (oi["dma"] is None and op["dma"] is None
                    and oi["eng"] == op["eng"] and oi["eng"] == "tensor")

        for j, op in enumerate(self.ops):
            for i in op["deps"]:
                oi = self.ops[i]
                if oi["dma"] is not None:
                    continue
                if same_safe(oi, op):
                    continue
                inc_needed[i] = True
        ccount = {e: 0 for e in ENGINES}
        dcount = {}
        milestone = [None] * n
        for j, op in enumerate(self.ops):
            if op["dma"] is not None:
                k = op["dma"]
                dcount[k] = dcount.get(k, 0) + (1 if k == "cc" else 16)
                milestone[j] = ("d", k, dcount[k])
            elif inc_needed[j]:
                e = op["eng"]
                ccount[e] += 1
                milestone[j] = ("e", e, ccount[e])
        waits = []
        for j, op in enumerate(self.ops):
            need = {}
            for i in op["deps"]:
                oi = self.ops[i]
                if same_safe(oi, op):
                    continue
                m = milestone[i]
                key = (m[0], m[1])
                need[key] = max(need.get(key, 0), m[2])
            waits.append(need)
        per_engine = {e: [] for e in ENGINES}
        for j, op in enumerate(self.ops):
            per_engine[op["eng"]].append(j)

        def mk(engname):
            def body(e):
                watched = {}
                for j in per_engine[engname]:
                    op = self.ops[j]
                    for (kind, key), v in sorted(waits[j].items()):
                        if watched.get((kind, key), 0) >= v:
                            continue
                        e.wait_ge(esem[key] if kind == "e" else dsem[key], v)
                        watched[(kind, key)] = v
                    ins = op["fn"](e)
                    m = milestone[j]
                    if m is not None:
                        if m[0] == "d":
                            ins.then_inc(dsem[m[1]], 1 if m[1] == "cc" else 16)
                        else:
                            ins.then_inc(esem[m[1]], 1)
                if engname == "sync":
                    for k in sorted(dcount):
                        e.wait_ge(dsem[k], dcount[k])
                    for en in ENGINES:
                        if en != "sync" and ccount[en] > 0:
                            e.wait_ge(esem[en], ccount[en])
            return body

        for name in ENGINES:
            getattr(block, name)(mk(name))


def build_program(taps=()):
    nc = bass.Bass("TRN2", target_bir_lowering=False, debug=False)
    S = Sched()
    BUFS = {}
    TAPS = list(taps)

    def din(name, shape, dtype=F32):
        return nc.dram_tensor(name, shape, dtype, kind="ExternalInput").ap()

    xT = din("xT", [D, T])
    wqT = din("wqT", [D, D])
    wkT = din("wkT", [D, D])
    wvT = din("wvT", [D, DVT])
    wgT = din("wgT", [D, DVT])
    wgk1T = din("wgk1T", [D, 16])
    wgk2T = din("wgk2T", [16, D])
    woT = din("woT", [DVT, D])
    gains = din("gains", [128, 40])
    no_c = din("no_c", [128, 16])
    bgk2_c = din("bgk2_c", [128, 8])
    ngk2_b = din("ngk2_b", [128, 16])
    alpha = din("alpha", [128, 1])
    ident_f = din("ident_f", [128, 128])
    ident_b = din("ident_b", [128, 128], BF16)
    maskU_d = din("maskU", [128, 128])
    sel16_d = din("sel16_d", [16, 2048], BF16)
    f_rw = din("f_rw", [D, 16])
    f_wg_r = din("f_wg_r", [NR, D, MIDR])
    f_wu_r = din("f_wu_r", [NR, D, MIDR])
    f_wd_r = din("f_wd_r", [NR, MIDR, D])
    f_wg_s = din("f_wg_s", [NS, D, MIDS])
    f_wu_s = din("f_wu_s", [NS, D, MIDS])
    f_wd_s = din("f_wd_s", [NS, MIDS, D])
    out_d = nc.dram_tensor("out", [D, T], F32, kind="ExternalOutput").ap()

    def dscr(name, shape, dtype=BF16):
        return nc.dram_tensor(name, shape, dtype, kind="Internal").ap()

    o_scr = dscr("o_scr", [DVT, T])
    qg_scr = dscr("qg_scr", [D, T])
    xqg_scr = dscr("xqg_scr", [D, T])
    wg_scr = dscr("wg_scr", [D, DVT])
    wo_scr = dscr("wo_scr", [DVT, D])
    invsg_scr = dscr("invsg_scr", [8, 256], F32)
    s_wgr = dscr("s_wgr", [NR, D, MIDR])
    s_wur = dscr("s_wur", [NR, D, MIDR])
    s_wdr = dscr("s_wdr", [NR, MIDR, D])
    s_wgs = dscr("s_wgs", [NS, D, MIDS])
    s_wus = dscr("s_wus", [NS, D, MIDS])
    s_wds = dscr("s_wds", [NS, MIDS, D])
    s_rw = dscr("s_rw", [D, 16])
    cc_in = dscr("cc_in", [D, 512])
    cc_out = dscr("cc_out", [2 * D, 512])

    stack = ExitStack()

    def sbuf(name, shape, dtype):
        t = stack.enter_context(nc.sbuf_tensor(name, shape, dtype))
        BUFS[name] = t
        return t

    def psum(name, shape, dtype=F32):
        return stack.enter_context(nc.psum_tensor(name, shape, dtype))

    with stack:
        T32a = sbuf("T32a", [128, 16384], BF16)
        T32b = sbuf("T32b", [128, 16384], BF16)
        T32c = sbuf("T32c", [128, 16384], BF16)
        xt0 = sbuf("xt0", [128, 2048], F32)
        xt1 = sbuf("xt1", [128, 2048], F32)
        qT_b = sbuf("qT_b", [128, 2048], BF16)
        kT_b = sbuf("kT_b", [128, 2048], BF16)
        xqg2 = sbuf("xqg2", [128, 2048], BF16)
        v_b = sbuf("v_b", [128, 4096], BF16)
        gkT_b = sbuf("gkT_b", [128, 2048], F32)
        o_st = sbuf("o_st", [128, 4096], BF16)
        qg_st = sbuf("qg_st", [128, 2048], BF16)
        sgb = sbuf("sgb", [128, 4096], BF16)
        sqw = sbuf("sqw", [128, 512], F32)
        sq0 = sqw[:, 0:256]
        sq1 = sqw[:, 256:512]
        srow = sq0
        mfold = sbuf("mfold", [128, 1280], F32)
        statb = sbuf("statb", [128, 48], F32)
        t1b = sbuf("t1b", [128, 256], F32)
        t1tm = sbuf("t1tm", [128, 36], F32)
        xq2tm = sbuf("xq2tm", [128, 20], F32)
        xq2b = sbuf("xq2b", [16, 256], BF16)
        rscrw = sbuf("rscrw", [128, 512], F32)
        rscr = rscrw[:, 0:256]
        rscr2 = rscrw[:, 256:512]
        ab0, ab1 = rscr, rscr2
        Ubuf = sbuf("Ubuf", [128, 1280], F32)
        invsb = sbuf("invsb", [128, 1280], BF16)
        urow5 = sbuf("urow5", [1, 256], F32)
        irow5 = sbuf("irow5", [1, 256], F32)
        i2row = sbuf("i2row", [1, 256], F32)
        abr = sbuf("abr", [1, 512], F32)
        expP = sbuf("expP", [128, 256], BF16)
        expM = sbuf("expM", [128, 256], BF16)
        gscr = sbuf("gscr", [128, 256], F32)
        Kbtm = sbuf("Kbtm", [128, 256], BF16)
        ATm = sbuf("ATm", [128, 128], BF16)
        ATm2 = sbuf("ATm2", [128, 128], BF16)
        T0e = sbuf("T0e", [128, 1024], BF16)
        Scar = sbuf("Scar", [128, 4096], BF16)
        eclb = sbuf("eclb", [128, 8], F32)
        egsum = sbuf("egsum", [128, 8], F32)
        gsml = sbuf("gsml", [128, 8], F32)
        identf = sbuf("identf", [128, 128], F32)
        identb = sbuf("identb", [128, 128], BF16)
        maskUb = sbuf("maskUb", [128, 128], F32)
        sel16 = sbuf("sel16", [16, 2048], BF16)
        ctile = sbuf("ctile", [16, 1024], BF16)
        ones_c = sbuf("ones_c", [128, 1], F32)
        ones_r = sbuf("ones_r", [1, 128], F32)
        gainb = sbuf("gainb", [128, 40], F32)
        gainab = sbuf("gainab", [128, 40], F32)
        nob = sbuf("nob", [128, 16], F32)
        bgk2b = sbuf("bgk2b", [128, 8], F32)
        ngk2bb = sbuf("ngk2bb", [128, 16], F32)
        alphab = sbuf("alphab", [128, 1], F32)
        amrep = sbuf("amrep", [128, 8], F32)
        swrep = sbuf("swrep", [128, 8], F32)
        amsml = sbuf("amsml", [1, 8], F32)
        ltm = sbuf("ltm", [128, 20], F32)
        combtm = sbuf("combtm", [128, 16], F32)
        rwb = sbuf("rwb", [128, 128], BF16)
        wgk1_i = sbuf("wgk1_i", [128, 128], BF16)
        wgk2_i = sbuf("wgk2_i", [16, 1024], BF16)

        pA = psum("pA", [128, 512])
        pB = psum("pB", [128, 512])
        pT0 = psum("pT0", [128, 512])
        pT1 = psum("pT1", [128, 512])
        pAT = psum("pAT", [128, 512])
        pO = psum("pO", [128, 512])
        pS = psum("pS", [128, 512])
        pBT = psum("pBT", [128, 512], BF16)

        Abc = Ubuf[:, 256:512]
        Bbc = Ubuf[:, 512:768]
        hh = sgb[:, 0:512]
        print(f"[build] sbuf free/partition: {nc.sbuf_bytes_remaining // 128}B")

        def dma(eng, out, in_, key, tag=""):
            S.add(eng, lambda e, o=out, i=in_: e.dma_start(out=o, in_=i),
                  ins=[in_], outs=[out], dma=key, tag=tag)

        def cop(eng, fn, ins, outs, tag=""):
            S.add(eng, fn, ins=ins, outs=outs, tag=tag)

        i2bc = Ubuf[:, 0:256]

        def tap(point):
            for (pt, bn) in TAPS:
                if pt != point:
                    continue
                t = BUFS[bn]
                shp = list(t.shape)
                dt_ = t.dtype
                o = nc.dram_tensor(f"dbg_{point}_{bn}", shp, dt_,
                                   kind="ExternalOutput").ap()
                dma("sync", o, t[:], f"dbg{point}{bn}")

        dma("sync", identf[:], ident_f[:], "identf")
        dma("sync", identb[:], ident_b[:], "identb")
        dma("sync", maskUb[:], maskU_d[:], "maskUb")
        dma("sync", sel16[:], sel16_d[:], "sel16")
        dma("sync", gainb[:], gains[:], "gainb")
        dma("sync", nob[:], no_c[:], "nob")
        dma("sync", bgk2b[:], bgk2_c[:], "bgk2b")
        dma("sync", ngk2bb[:], ngk2_b[:], "ngk2bb")
        dma("sync", alphab[:], alpha[:], "alphab")
        cop("gpsimd", lambda e: e.memset(ones_c[:], 1.0), [], [ones_c[:]])
        cop("gpsimd", lambda e: e.memset(ones_r[:], 1.0), [], [ones_r[:]])
        cop("gpsimd", lambda e: e.memset(t1b[:], 0.0), [], [t1b[:]])
        cop("gpsimd", lambda e: e.memset(Scar[:], 0.0), [], [Scar[:]])
        cop("gpsimd", lambda e: e.memset(swrep[:, 7:8], 1e-8), [], [swrep[:]])
        cop("gpsimd", lambda e: e.memset(amrep[:, 7:8], 1e-5), [], [amrep[:]])
        cop("scalar", lambda e: e.activation(out=gainab[:], in_=gainb[:],
                                             func=AF.Abs, bias=0.0, scale=1.0),
            [gainb[:]], [gainab[:]])
        for kcg in range(8):
            cop("vector", lambda e, k=kcg: e.tensor_copy(out=egsum[:, k:k + 1],
                                                         in_=alphab[:]),
                [alphab[:]], [egsum[:]])

        cop("gpsimd", lambda e: e.memset(amsml[:], 0.0), [], [amsml[:]])

        def quant_weight(widx, dram, Din, Dout, dest, tag):
            ndc = (Din + 127) // 128
            ncs = (Dout + 1023) // 1024
            colw = min(Dout, 1024)
            for dc in range(ndc):
                h = min(128, Din - dc * 128)
                for cs in range(ncs):
                    st = xt0 if (dc * ncs + cs) % 2 == 0 else xt1
                    src = dram[dc * 128:dc * 128 + h, cs * 1024:cs * 1024 + colw]
                    dma("sync", st[0:h, 0:colw], src, _nm(st), tag=f"P0L{tag}")
                    cop("scalar", lambda e, s=st, hh_=h: e.activation(
                        out=s[0:hh_, 0:colw], in_=s[0:hh_, 0:colw], func=AF.Abs,
                        bias=0.0, scale=1.0), [st[:]], [st[:]])
                    nns = (colw + 511) // 512
                    for ns in range(nns):
                        a, b = ns * 512, min(colw, ns * 512 + 512)
                        cop("tensor", lambda e, s=st, hh_=h, a_=a, b_=b,
                            first=(ns == 0), last=(ns == nns - 1): e.matmul(
                                pS[0:1, 0:b_ - a_], ones_c[0:hh_, :],
                                s[0:hh_, a_:b_], start=first, stop=last),
                            [st[:], ones_c[:]], [pS[:]])
                    red_w = min(512, Dout)
                    cop("vector", lambda e: e.reduce_sum(
                        out=amsml[0:1, 7:8], in_=pS[0:1, 0:red_w], axis=AX),
                        [pS[:]], [amsml[:]])
                    cop("vector", lambda e: e.tensor_tensor(
                        out=amsml[0:1, widx:widx + 1],
                        in0=amsml[0:1, widx:widx + 1], in1=amsml[0:1, 7:8],
                        op=AL.add), [amsml[:]], [amsml[:]])
            cnt = float(Din * Dout)
            cop("vector", lambda e: e.tensor_scalar(
                out=amsml[0:1, widx:widx + 1], in0=amsml[0:1, widx:widx + 1],
                scalar1=1.0 / cnt, scalar2=1e-5, op0=AL.mult, op1=AL.max),
                [amsml[:]], [amsml[:]])
            cop("tensor", lambda e: e.matmul(pS[:, 0:1], ones_r[:],
                                             amsml[0:1, widx:widx + 1],
                                             start=True, stop=True),
                [amsml[:], ones_r[:]], [pS[:]])
            sc_mul = (1.0 / 16.0) if widx == 0 else 1.0
            cop("scalar", lambda e, m=sc_mul: e.activation(
                out=amrep[:, widx:widx + 1], in_=pS[:, 0:1], func=AF.Copy,
                bias=0.0, scale=m), [pS[:]], [amrep[:]])
            cop("vector", lambda e: e.reciprocal(out=swrep[:, widx:widx + 1],
                                                 in_=amrep[:, widx:widx + 1]),
                [amrep[:]], [swrep[:]])
            if widx == 0:
                cop("vector", lambda e: e.tensor_scalar(
                    out=swrep[:, 0:1], in0=swrep[:, 0:1], scalar1=1.0 / 16.0,
                    scalar2=None, op0=AL.mult), [swrep[:]], [swrep[:]])
            for dc in range(ndc):
                h = min(128, Din - dc * 128)
                for cs in range(ncs):
                    st = xt0 if (dc * ncs + cs) % 2 == 0 else xt1
                    src = dram[dc * 128:dc * 128 + h, cs * 1024:cs * 1024 + colw]
                    dma("sync", st[0:h, 0:colw], src, _nm(st), tag=f"P0R{tag}")
                    cop("vector", lambda e, s=st, hh_=h: e.tensor_scalar(
                        out=s[0:hh_, 0:colw], in0=s[0:hh_, 0:colw],
                        scalar1=swrep[0:hh_, widx:widx + 1], scalar2=MAGIC,
                        op0=AL.mult, op1=AL.add), [st[:], swrep[:]], [st[:]])
                    cop("vector", lambda e, s=st, hh_=h: e.tensor_scalar(
                        out=s[0:hh_, 0:colw], in0=s[0:hh_, 0:colw],
                        scalar1=MAGIC, scalar2=1.0, op0=AL.subtract, op1=AL.min),
                        [st[:]], [st[:]])
                    if dest[0] == "res":
                        dst, dname = dest[1](dc, cs, colw)
                        cop("vector", lambda e, s=st, hh_=h, d=dst: e.tensor_scalar(
                            out=d, in0=s[0:hh_, 0:colw], scalar1=-1.0,
                            scalar2=None, op0=AL.max), [st[:]], [dname])
                    else:
                        cb = qT_b if (dc * ncs + cs) % 2 == 0 else kT_b
                        cop("vector", lambda e, s=st, hh_=h, c=cb: e.tensor_scalar(
                            out=c[0:hh_, 0:colw], in0=s[0:hh_, 0:colw],
                            scalar1=-1.0, scalar2=None, op0=AL.max),
                            [st[:]], [cb[:]])
                        dma("gpsimd",
                            dest[1][dc * 128:dc * 128 + h,
                                    cs * 1024:cs * 1024 + colw],
                            cb[0:h, 0:colw], "st_" + _nm(cb), tag=f"P0S{tag}")

        IWQ, IWK, IWV, IWG, IWO, IGK1, IGK2 = 0, 1, 2, 3, 4, 5, 6
        quant_weight(IWQ, wqT, D, D,
                     ("res", lambda dc, cs, w: (T32b[:, dc * D:dc * D + w],
                                                "T32b")), "wq")
        quant_weight(IWK, wkT, D, D,
                     ("res", lambda dc, cs, w: (T32b[:, 8192 + dc * D:
                                                     8192 + dc * D + w],
                                                "T32b")), "wk")
        quant_weight(IGK1, wgk1T, D, 16,
                     ("res", lambda dc, cs, w: (wgk1_i[:, dc * 16:dc * 16 + w],
                                                "wgk1_i")), "wgk1")
        quant_weight(IGK2, wgk2T, 16, D,
                     ("res", lambda dc, cs, w: (wgk2_i[0:16, 0:w], "wgk2_i")),
                     "wgk2")
        quant_weight(IWV, wvT, D, DVT,
                     ("res", lambda dc, cs, w: (T32a[:, dc * DVT + cs * 1024:
                                                     dc * DVT + cs * 1024 + w],
                                                "T32a")), "wv")
        quant_weight(IWG, wgT, D, DVT, ("scr", wg_scr), "wg")
        quant_weight(IWO, woT, DVT, D, ("scr", wo_scr), "wo")

        pairs = []
        for ei in range(NR):
            pairs += [(f_wg_r[ei], s_wgr[ei], D, MIDR),
                      (f_wu_r[ei], s_wur[ei], D, MIDR),
                      (f_wd_r[ei], s_wdr[ei], MIDR, D)]
        for ei in range(NS):
            pairs += [(f_wg_s[ei], s_wgs[ei], D, MIDS),
                      (f_wu_s[ei], s_wus[ei], D, MIDS),
                      (f_wd_s[ei], s_wds[ei], MIDS, D)]
        pairs += [(f_rw, s_rw, D, 16)]
        cast_units = []
        _ci = [0]

        def mk_cast_unit(src, dst, rows, cols, c0, nch):
            def emit():
                par = _ci[0] % 2
                _ci[0] += 1
                slot = sgb[:, par * 2048:par * 2048 + nch * cols]
                rname = f"sgbH{par}"
                S.add("gpsimd", lambda e: e.dma_start(
                    out=slot.rearrange("p (c t) -> p c t", t=cols),
                    in_=src.rearrange("(c p) t -> p c t", p=128)[:, c0:c0 + nch]),
                    ins=[_nm(src)], outs=[rname], dma=f"ffc{par}")
                S.add("gpsimd", lambda e: e.dma_start(
                    out=dst.rearrange("(c p) t -> p c t", p=128)[:, c0:c0 + nch],
                    in_=slot.rearrange("p (c t) -> p c t", t=cols)),
                    ins=[rname], outs=[_nm(dst)], dma=f"ffs{par}")
            return emit

        for (src, dst, rows, cols) in pairs:
            nch_tot = (rows + 127) // 128
            step = max(1, 2048 // cols)
            for c0 in range(0, nch_tot, step):
                nch = min(step, nch_tot - c0)
                cast_units.append(mk_cast_unit(src, dst, rows, cols, c0, nch))

        def emit_cast_units(k):
            while k > 0 and cast_units:
                cast_units.pop(0)()
                k -= 1

        GQ, GK, GV, GG, GGK1 = 0, 1, 2, 3, 4

        def xq_sl(n, dc, a, b):
            return T32c[:, n * 2048 + dc * 256 + a:n * 2048 + dc * 256 + b]

        def wq_sl(dc, a, b):
            return T32b[:, dc * D + a:dc * D + b]

        def wk_sl(dc, a, b):
            return T32b[:, 8192 + dc * D + a:8192 + dc * D + b]

        def wv_sl(dc, a, b):
            return T32a[:, dc * DVT + a:dc * DVT + b]

        def row_extract(col_ap, src_name, tt, region, P=None):
            P = pS if P is None else P
            cop("tensor", lambda e, c=col_ap, r=region, t=tt, p=P: e.transpose(
                p[0:1, r + t * 128:r + (t + 1) * 128], c, identf[:]),
                [src_name, identf[:]], [_nm(P)])

        def a1_spine_chunks(sc):
            t0 = sc * SC
            xt = xt0 if sc % 2 == 0 else xt1

            def c_load():
                dma("sync", xt[:].rearrange("p (c t) -> p c t", t=256),
                    xT.rearrange("(c p) t -> p c t", p=128)[:, :, t0:t0 + SC],
                    _nm(xt), tag="xload")

            def mk_fold(pr):
                def c_fold():
                    for dc in (2 * pr, 2 * pr + 1):
                        sq = sq0 if dc % 2 == 0 else sq1
                        ab = ab0 if dc % 2 == 0 else ab1
                        cop("scalar", lambda e, d=dc, s=sq, x=xt: e.activation(
                            out=s[:], in_=x[:, d * 256:(d + 1) * 256],
                            func=AF.Square, bias=0.0, scale=1.0),
                            [xt[:]], [sq[:]])
                        cop("tensor", lambda e, s=sq, first=(dc == 0),
                            last=(dc == 7): e.matmul(
                                pS[0:1, 0:256], ones_c[:], s[:],
                                start=first, stop=last),
                            [sq[:], ones_c[:]], [pS[:]])
                        cop("scalar", lambda e, d=dc, a=ab, x=xt: e.activation(
                            out=a[:], in_=x[:, d * 256:(d + 1) * 256],
                            func=AF.Abs, bias=0.0, scale=1.0), [xt[:]], [ab[:]])
                        for n in range(5):
                            if dc == 0:
                                cop("vector", lambda e, a=ab, g=n: e.tensor_scalar(
                                    out=mfold[:, g * 256:(g + 1) * 256], in0=a[:],
                                    scalar1=gainab[:, g * 8:g * 8 + 1],
                                    scalar2=None, op0=AL.mult),
                                    [ab[:], gainab[:]], [mfold[:]])
                            else:
                                cop("vector", lambda e, a=ab, g=n, d=dc:
                                    e.scalar_tensor_tensor(
                                        out=mfold[:, g * 256:(g + 1) * 256],
                                        in0=a[:],
                                        scalar=gainab[:, g * 8 + d:g * 8 + d + 1],
                                        in1=mfold[:, g * 256:(g + 1) * 256],
                                        op0=AL.mult, op1=AL.max),
                                    [ab[:], gainab[:], mfold[:]], [mfold[:]])
                return c_fold

            def c_stat():
                cop("scalar", lambda e: e.activation(
                    out=srow[0:1, :], in_=pS[0:1, 0:256],
                    func=AF.Copy, bias=0.0, scale=1.0), [pS[:]], [srow[:]])
                for tt in range(2):
                    sb_ = statb[:, tt * 24:(tt + 1) * 24]
                    cop("tensor", lambda e, t=tt: e.transpose(
                        pS[:, 0:128], srow[:, t * 128:(t + 1) * 128], identf[:]),
                        [srow[:], identf[:]], [pS[:]])
                    cop("vector", lambda e, s=sb_: e.tensor_copy(
                        out=s[:, 0:1], in_=pS[:, 0:1]), [pS[:]], [statb[:]])
                    for n in range(5):
                        cop("tensor", lambda e, t=tt, g=n: e.transpose(
                            pS[:, 0:128],
                            mfold[:, g * 256 + t * 128:g * 256 + (t + 1) * 128],
                            identf[:]), [mfold[:], identf[:]], [pS[:]])
                        cop("vector", lambda e, s=sb_, g=n: e.reduce_max(
                            out=s[:, 1 + g:2 + g], in_=pS[:, 0:128], axis=AX),
                            [pS[:]], [statb[:]])
                    cop("scalar", lambda e, s=sb_: e.activation(
                        out=s[:, 6:7], in_=s[:, 0:1], func=AF.Sqrt,
                        bias=swrep[:, 7:8], scale=1.0 / 1024.0),
                        [statb[:], swrep[:]], [statb[:]])
                    cop("vector", lambda e, s=sb_: e.reciprocal(
                        out=s[:, 7:8], in_=s[:, 6:7]), [statb[:]], [statb[:]])
                    cop("vector", lambda e, s=sb_: e.tensor_scalar(
                        out=s[:, 8:13], in0=s[:, 1:6], scalar1=s[:, 7:8],
                        scalar2=1e-5, op0=AL.mult, op1=AL.max),
                        [statb[:]], [statb[:]])
                    cop("vector", lambda e, s=sb_: e.tensor_scalar(
                        out=s[:, 13:18], in0=s[:, 8:13], scalar1=1.0 / 127.0,
                        scalar2=None, op0=AL.mult), [statb[:]], [statb[:]])
                    cop("vector", lambda e, s=sb_: e.reciprocal(
                        out=s[:, 18:23], in_=s[:, 8:13]), [statb[:]], [statb[:]])
                    cop("vector", lambda e, s=sb_: e.tensor_scalar(
                        out=s[:, 18:23], in0=s[:, 18:23], scalar1=s[:, 7:8],
                        scalar2=127.0, op0=AL.mult, op1=AL.mult),
                        [statb[:]], [statb[:]])

            def emit_extract_bcast(n):
                for tt in range(2):
                    row_extract(statb[:, tt * 24 + 18 + n:tt * 24 + 19 + n],
                                "statb", tt, 0)
                    row_extract(statb[:, tt * 24 + 13 + n:tt * 24 + 14 + n],
                                "statb", tt, 256)
                cop("scalar", lambda e: e.activation(
                    out=urow5[0:1, :], in_=pS[0:1, 0:256],
                    func=AF.Copy, bias=0.0, scale=1.0), [pS[:]], [urow5[:]])
                cop("scalar", lambda e: e.activation(
                    out=irow5[0:1, :], in_=pS[0:1, 256:512],
                    func=AF.Copy, bias=0.0, scale=1.0), [pS[:]], [irow5[:]])
                if n == GG:
                    dma("sync", invsg_scr[sc:sc + 1, :], irow5[0:1, :],
                        "st_invsg")
                cop("tensor", lambda e: e.matmul(
                    pS[:, 0:256], ones_r[:], urow5[0:1, :],
                    start=True, stop=True), [urow5[:], ones_r[:]], [pS[:]])
                cop("scalar", lambda e, g=n: e.activation(
                    out=Ubuf[:, g * 256:(g + 1) * 256], in_=pS[:, 0:256],
                    func=AF.Copy, bias=0.0, scale=1.0), [pS[:]], [f"UbufG{n}"])
                cop("tensor", lambda e: e.matmul(
                    pS[:, 0:256], ones_r[:], irow5[0:1, :],
                    start=True, stop=True), [irow5[:], ones_r[:]], [pS[:]])
                cop("scalar", lambda e, g=n: e.activation(
                    out=invsb[:, g * 256:(g + 1) * 256], in_=pS[:, 0:256],
                    func=AF.Copy, bias=0.0, scale=1.0), [pS[:]], [invsb[:]])

            def emit_rounds(n):
                for dc in range(8):
                    rs = rscr if dc % 2 == 0 else rscr2
                    cop("vector", lambda e, d=dc, g=n, r=rs, x=xt:
                        e.scalar_tensor_tensor(
                            out=r, in0=x[:, d * 256:(d + 1) * 256],
                            scalar=gainb[:, g * 8 + d:g * 8 + d + 1],
                            in1=Ubuf[:, g * 256:(g + 1) * 256],
                            op0=AL.mult, op1=AL.mult),
                        [xt[:], gainb[:], f"UbufG{n}"], ["rscrw"])
                    if dc % 2 == 1:
                        cop("vector", lambda e, d=dc, g=n: e.tensor_scalar(
                            out=xq_sl(g, d - 1, 0, 512), in0=rscrw[:],
                            scalar1=MAGIC, scalar2=MAGIC,
                            op0=AL.add, op1=AL.subtract),
                            ["rscrw"], [f"xqG{n}"])

            def mk_gain(n):
                def c_gain():
                    emit_extract_bcast(n)
                    emit_rounds(n)
                return c_gain

            return ([c_load] + [mk_fold(p) for p in range(4)] + [c_stat]
                    + [mk_gain(n) for n in (GQ, GGK1, GK, GV, GG)])

        def a1_heavy_a(sc):
            t0 = sc * SC

            def proj_fm(xqn, wsl, wname, outb, widx, gidx):
                for op2 in range(4):
                    pp = pA if op2 % 2 == 0 else pB
                    for oh in range(2):
                        oc = op2 * 2 + oh
                        for dc in range(8):
                            cop("tensor", lambda e, o=oc, d=dc, p=pp, h2=oh:
                                e.matmul(
                                    p[:, h2 * 256:(h2 + 1) * 256],
                                    wsl(d, o * 128, (o + 1) * 128),
                                    xq_sl(xqn, d, 0, 256),
                                    start=(d == 0), stop=(d == 7)),
                                [f"xqG{xqn}", wname], [_nm(pp)])
                    ivr = invsb[:, gidx * 256:(gidx + 1) * 256].rearrange(
                        "p (a t) -> p a t", a=1).to_broadcast([128, 2, 256])
                    cop("vector", lambda e, o2=op2, p=pp, iv=ivr:
                        e.scalar_tensor_tensor(
                            out=outb[:, o2 * 512:(o2 + 1) * 512].rearrange(
                                "p (a t) -> p a t", t=256),
                            in0=p[:, 0:512].rearrange("p (a t) -> p a t", t=256),
                            scalar=amrep[:, widx:widx + 1],
                            in1=iv, op0=AL.mult, op1=AL.mult),
                        [_nm(pp), amrep[:], invsb[:]], [outb[:]])

            proj_fm(GQ, wq_sl, "T32b", qT_b, IWQ, GQ)
            proj_fm(GK, wk_sl, "T32b", kT_b, IWK, GK)
            for dc in range(8):
                cop("tensor", lambda e, d=dc: e.matmul(
                    pB[0:16, 0:256], wgk1_i[:, d * 16:(d + 1) * 16],
                    xq_sl(GGK1, d, 0, 256), start=(d == 0), stop=(d == 7)),
                    [f"xqG{GGK1}", "wgk1_i"], [pB[:]])
            cop("vector", lambda e: e.scalar_tensor_tensor(
                out=t1b[0:16, :], in0=pB[0:16, 0:256],
                scalar=amrep[0:16, IGK1:IGK1 + 1],
                in1=invsb[0:16, GGK1 * 256:(GGK1 + 1) * 256],
                op0=AL.mult, op1=AL.mult), [pB[:], amrep[:], invsb[:]], [t1b[:]])
            for tc in range(2):
                for nv in range(4):
                    pp = pA if nv % 2 == 0 else pB
                    for dc in range(8):
                        cop("tensor", lambda e, t=tc, n=nv, d=dc, p=pp: e.matmul(
                            p[:, 0:512], xq_sl(GV, d, t * 128, (t + 1) * 128),
                            wv_sl(d, n * 512, (n + 1) * 512),
                            start=(d == 0), stop=(d == 7)),
                            [f"xqG{GV}", "T32a"], [_nm(pp)])
                    cop("vector", lambda e, t=tc, n=nv, p=pp: e.tensor_scalar(
                        out=v_b[:, t * 2048 + n * 512:t * 2048 + (n + 1) * 512],
                        in0=p[:, 0:512],
                        scalar1=statb[:, t * 24 + 13 + GV:t * 24 + 14 + GV],
                        scalar2=amrep[:, IWV:IWV + 1], op0=AL.mult, op1=AL.mult),
                        [_nm(pp), statb[:], amrep[:]], [v_b[:]])
            S.add("sync", lambda e, t0_=t0: e.dma_start(
                out=xqg_scr.rearrange("(c p) t -> p c t", p=128)[:, :, t0_:t0_ + SC],
                in_=T32c[:, GG * 2048:(GG + 1) * 2048].rearrange(
                    "p (c t) -> p c t", t=256)),
                ins=[f"xqG{GG}"], outs=["xqg_scr"], dma="st_xqg")

        def a1_heavy_b(sc, nxt):
            t0 = sc * SC

            def pop(k):
                for _ in range(k):
                    if nxt:
                        nxt.pop(0)()

            pop(2)
            for tt in range(2):
                cop("tensor", lambda e, t=tt: e.transpose(
                    pT0[:, t * 128:(t + 1) * 128],
                    t1b[:, t * 128:(t + 1) * 128], identf[:]),
                    [t1b[:], identf[:]], [_nm(pT0)])
            mchains = []
            for tt in range(2):
                c0 = tt * 18
                X = xq2tm if tt == 0 else gscr
                Xn = xq2tm[:] if tt == 0 else "gscr"
                ops = [
                    ("vector", lambda e, c=c0, t=tt: e.tensor_copy(
                        out=t1tm[:, c:c + 16],
                        in_=pT0[:, t * 128:t * 128 + 16]),
                     [_nm(pT0)], [t1tm[:]]),
                    ("scalar", lambda e, c=c0, x=X: e.activation(
                        out=x[:, 0:16], in_=t1tm[:, c:c + 16], func=AF.Square,
                        bias=0.0, scale=1.0), [t1tm[:]], [Xn]),
                    ("vector", lambda e, c=c0, x=X: e.reduce_sum(
                        out=t1tm[:, c + 16:c + 17], in_=x[:, 0:16], axis=AX),
                     [Xn], [t1tm[:]]),
                    ("scalar", lambda e, c=c0: e.activation(
                        out=t1tm[:, c + 16:c + 17], in_=t1tm[:, c + 16:c + 17],
                        func=AF.Sqrt, bias=swrep[:, 7:8], scale=1.0 / 16.0),
                     [t1tm[:], swrep[:]], [t1tm[:]]),
                    ("vector", lambda e, c=c0: e.reciprocal(
                        out=t1tm[:, c + 17:c + 18], in_=t1tm[:, c + 16:c + 17]),
                     [t1tm[:]], [t1tm[:]]),
                    ("vector", lambda e, c=c0, x=X: e.tensor_tensor(
                        out=x[:, 0:16], in0=t1tm[:, c:c + 16], in1=ngk2bb[:],
                        op=AL.mult), [t1tm[:], ngk2bb[:]], [Xn]),
                    ("vector", lambda e, x=X: e.reduce_max(
                        out=x[:, 16:17], in_=x[:, 0:16], axis=AX,
                        apply_absolute_value=True), [Xn], [Xn]),
                    ("vector", lambda e, c=c0, x=X: e.tensor_scalar(
                        out=x[:, 16:17], in0=x[:, 16:17],
                        scalar1=t1tm[:, c + 17:c + 18], scalar2=1e-5,
                        op0=AL.mult, op1=AL.max), [Xn, t1tm[:]], [Xn]),
                    ("vector", lambda e, x=X: e.tensor_scalar(
                        out=x[:, 18:19], in0=x[:, 16:17], scalar1=1.0 / 127.0,
                        scalar2=None, op0=AL.mult), [Xn], [Xn]),
                    ("vector", lambda e, x=X: e.reciprocal(
                        out=x[:, 17:18], in_=x[:, 16:17]), [Xn], [Xn]),
                    ("vector", lambda e, c=c0, x=X: e.tensor_scalar(
                        out=x[:, 17:18], in0=x[:, 17:18],
                        scalar1=t1tm[:, c + 17:c + 18], scalar2=127.0,
                        op0=AL.mult, op1=AL.mult), [Xn, t1tm[:]], [Xn]),
                    ("vector", lambda e, x=X: e.tensor_scalar(
                        out=x[:, 0:16], in0=x[:, 0:16],
                        scalar1=x[:, 17:18], scalar2=MAGIC,
                        op0=AL.mult, op1=AL.add), [Xn], [Xn]),
                    ("vector", lambda e, x=X: e.tensor_scalar(
                        out=x[:, 0:16], in0=x[:, 0:16], scalar1=MAGIC,
                        scalar2=None, op0=AL.subtract), [Xn], [Xn]),
                ]
                mchains.append(ops)
            for i in range(13):
                for tt in range(2):
                    eng, fn, ri, ro = mchains[tt][i]
                    cop(eng, fn, ri, ro)
            for tt in range(2):
                X = xq2tm if tt == 0 else gscr
                Xn = xq2tm[:] if tt == 0 else "gscr"
                cop("tensor", lambda e, x=X: e.transpose(
                    pT0[0:16, 0:128], x[:, 0:16], identf[:]),
                    [Xn, identf[:]], [_nm(pT0)])
                cop("vector", lambda e, t=tt: e.tensor_copy(
                    out=xq2b[:, t * 128:(t + 1) * 128], in_=pT0[0:16, 0:128]),
                    [_nm(pT0)], [xq2b[:]])
                row_extract(X[:, 18:19], Xn, tt, 256, P=pT0)
            cop("scalar", lambda e: e.activation(
                out=i2row[0:1, :], in_=pT0[0:1, 256:512], func=AF.Copy,
                bias=0.0, scale=1.0), [_nm(pT0)], [i2row[:]])
            cop("tensor", lambda e: e.matmul(pT1[:, 0:256], ones_r[:],
                                             i2row[0:1, :], start=True, stop=True),
                [i2row[:], ones_r[:]], [_nm(pT1)])
            cop("scalar", lambda e: e.activation(out=t1b[:], in_=pT1[:, 0:256],
                                                 func=AF.Copy, bias=0.0, scale=1.0),
                [_nm(pT1)], [t1b[:]])
            pop(1)
            for oc in range(8):
                pp = pA if oc % 2 == 0 else pB
                cop("tensor", lambda e, o=oc, p=pp: e.matmul(
                    p[:, 0:256], wgk2_i[0:16, o * 128:(o + 1) * 128], xq2b[:],
                    start=True, stop=True), ["wgk2_i", xq2b[:]], [_nm(pp)])
                cop("vector", lambda e, p=pp: e.scalar_tensor_tensor(
                    out=gscr[:], in0=p[:, 0:256], scalar=amrep[:, IGK2:IGK2 + 1],
                    in1=t1b[:], op0=AL.mult, op1=AL.mult),
                    [_nm(pp), amrep[:], t1b[:]], [gscr[:]])
                cop("scalar", lambda e, o=oc: e.activation(
                    out=gkT_b[:, o * 256:(o + 1) * 256], in_=gscr[:],
                    func=AF.Sigmoid, bias=bgk2b[:, o:o + 1], scale=1.0),
                    [gscr[:], bgk2b[:]], [gkT_b[:]])
            for oc in range(8):
                cop("scalar", lambda e, o=oc: e.activation(
                    out=gkT_b[:, o * 256:(o + 1) * 256],
                    in_=gkT_b[:, o * 256:(o + 1) * 256], func=AF.Ln,
                    bias=0.0, scale=1.0), [gkT_b[:]], [gkT_b[:]])
                cop("vector", lambda e, o=oc: e.tensor_tensor_scan(
                    out=gkT_b[:, o * 256:(o + 1) * 256],
                    data0=gkT_b[:, o * 256:(o + 1) * 256],
                    data1=gkT_b[:, o * 256:(o + 1) * 256],
                    initial=0.0, op0=AL.add, op1=AL.bypass),
                    [gkT_b[:]], [gkT_b[:]])
            cumT = gkT_b
            pop(1)

            def qb_sl(kcg, a, b):
                return T32c[:, 10240 + kcg * 256 + a:10240 + kcg * 256 + b]

            def kb_sl(kcg, a, b):
                return T32c[:, 12288 + kcg * 256 + a:12288 + kcg * 256 + b]

            def kbt_sl(kcg, a, b):
                return T32c[:, 14336 + kcg * 256 + a:14336 + kcg * 256 + b]

            for kcg in range(8):
                cop("scalar", lambda e, k=kcg: e.activation(
                    out=eclb[:, k:k + 1],
                    in_=cumT[:, k * 256 + 255:k * 256 + 256],
                    func=AF.Exp, bias=0.0, scale=1.0 / 16.0),
                    [cumT[:]], [eclb[:]])
                for c in range(2):
                    cs_ = cumT[:, kcg * 256 + c * 128:kcg * 256 + (c + 1) * 128]
                    ep = expP[:, c * 128:(c + 1) * 128]
                    em = expM[:, c * 128:(c + 1) * 128]
                    cop("scalar", lambda e, i=cs_, o=ep: e.activation(
                        out=o, in_=i, func=AF.Exp, bias=0.0, scale=1.0 / 16.0),
                        [cumT[:]], [expP[:]])
                    cop("scalar", lambda e, i=cs_, o=em: e.activation(
                        out=o, in_=i, func=AF.Exp, bias=0.0, scale=-1.0 / 16.0),
                        [cumT[:]], [expM[:]])
                    cop("vector", lambda e, k=kcg, cc=c, o=ep: e.tensor_tensor(
                        out=qb_sl(k, cc * 128, (cc + 1) * 128),
                        in0=qT_b[:, k * 256 + cc * 128:k * 256 + (cc + 1) * 128],
                        in1=o, op=AL.mult),
                        [qT_b[:], expP[:]], ["scnQ"])
                    cop("vector", lambda e, k=kcg, cc=c, o=em: e.tensor_tensor(
                        out=kb_sl(k, cc * 128, (cc + 1) * 128),
                        in0=kT_b[:, k * 256 + cc * 128:k * 256 + (cc + 1) * 128],
                        in1=o, op=AL.mult),
                        [kT_b[:], expM[:]], ["scnK"])
                    cop("vector", lambda e, k=kcg, cc=c: e.tensor_scalar(
                        out=qg_st[:, k * 256 + cc * 128:k * 256 + (cc + 1) * 128],
                        in0=qb_sl(k, cc * 128, (cc + 1) * 128),
                        scalar1=egsum[:, k:k + 1], scalar2=None, op0=AL.mult),
                        ["scnQ", egsum[:]], [qg_st[:]])
                    par = (kcg * 2 + c) % 2
                    cop("tensor", lambda e, k=kcg, cc=c, p=par: e.transpose(
                        pBT[:, p * 256 + cc * 0:p * 256 + 128],
                        kb_sl(k, cc * 128, (cc + 1) * 128), identb[:]),
                        ["scnK", identb[:]], [f"pBT{par}"])
                    cop("scalar", lambda e, k=kcg, cc=c, p=par: e.activation(
                        out=kbt_sl(k, cc * 128, (cc + 1) * 128),
                        in_=pBT[:, p * 256:p * 256 + 128], func=AF.Copy,
                        bias=0.0, scale=1.0), [f"pBT{par}"], ["scnKT"])
            for h in range(HEADS):
                pop(2 if h < 2 else (2 if h == 2 else 1))
                for kc in range(2):
                    kcg = h * 2 + kc
                    pT = pT0 if kc == 0 else pT1
                    cop("tensor", lambda e, k=kcg, p=pT: e.matmul(
                        p[:], identb[:], Scar[:, k * 512:(k + 1) * 512],
                        start=True, stop=False), [identb[:], "Scar"], [_nm(pT)])
                for c in range(2):
                    hp = (h * 2 + c) % 2
                    atm = ATm if hp == 0 else ATm2
                    for kc in range(2):
                        kcg = h * 2 + kc
                        cop("tensor", lambda e, k=kcg, cc=c, p=hp,
                            st=(kc == 0), sp=(kc == 1): e.matmul(
                            pAT[:, p * 128:(p + 1) * 128],
                            kb_sl(k, cc * 128, (cc + 1) * 128),
                            qb_sl(k, cc * 128, (cc + 1) * 128),
                            start=st, stop=sp),
                            ["scnK", "scnQ"], [f"pAT{hp}"])
                    cop("vector", lambda e, p=hp, a=atm: e.tensor_tensor(
                        out=a[:], in0=pAT[:, p * 128:(p + 1) * 128],
                        in1=maskUb[:], op=AL.mult),
                        [f"pAT{hp}", maskUb[:]], [_nm(atm)])
                    for vc in range(4):
                        cop("tensor", lambda e, v=vc, cc=c, h_=h, a=atm: e.matmul(
                            pO[:, v * 128:(v + 1) * 128],
                            v_b[:, cc * 2048 + h_ * 512 + v * 128:
                                cc * 2048 + h_ * 512 + (v + 1) * 128],
                            a[:], start=True, stop=False),
                            [v_b[:], _nm(atm)], [pO[:]])
                        for kc in range(2):
                            kcg = h * 2 + kc
                            if c == 0:
                                lhs = Scar[:, kcg * 512 + vc * 128:
                                           kcg * 512 + (vc + 1) * 128]
                                ln = "Scar"
                            else:
                                lhs = T0e[:, kc * 512 + vc * 128:
                                          kc * 512 + (vc + 1) * 128]
                                ln = "T0e"
                            cop("tensor", lambda e, v=vc, k=kcg, cc=c, L=lhs,
                                last=(kc == 1): e.matmul(
                                    pO[:, v * 128:(v + 1) * 128], L,
                                    qb_sl(k, cc * 128, (cc + 1) * 128),
                                    start=False, stop=last),
                                [ln, "scnQ"], [pO[:]])
                    cop("scalar", lambda e, h_=h, cc=c: e.activation(
                        out=o_st[:].rearrange("p (v t) -> p v t", t=256)
                        [:, h_ * 4:(h_ + 1) * 4, cc * 128:(cc + 1) * 128],
                        in_=pO[:].rearrange("p (v t) -> p v t", t=128),
                        func=AF.Copy, bias=0.0, scale=1.0), [pO[:]], [o_st[:]])
                    for kc in range(2):
                        pT = pT0 if kc == 0 else pT1
                        kcg = h * 2 + kc
                        cop("tensor", lambda e, k=kcg, cc=c, h_=h, p=pT,
                            last=(c == 1): e.matmul(
                                p[:], kbt_sl(k, cc * 128, (cc + 1) * 128),
                                v_b[:, cc * 2048 + h_ * 512:
                                    cc * 2048 + (h_ + 1) * 512],
                                start=False, stop=last),
                            ["scnKT", v_b[:]], [_nm(pT)])
                    if c == 0:
                        for kc in range(2):
                            pT = pT0 if kc == 0 else pT1
                            cop("scalar", lambda e, k=kc, p=pT: e.activation(
                                out=T0e[:, k * 512:(k + 1) * 512], in_=p[:],
                                func=AF.Copy, bias=0.0, scale=1.0),
                                [_nm(pT)], ["T0e"])
                for kc in range(2):
                    pT = pT0 if kc == 0 else pT1
                    kcg = h * 2 + kc
                    cop("scalar", lambda e, k=kcg, p=pT: e.activation(
                        out=Scar[:, k * 512:(k + 1) * 512], in_=p[:], func=AF.Copy,
                        bias=0.0, scale=eclb[:, k:k + 1]),
                        [_nm(pT), eclb[:]], ["Scar"])
                    cop("vector", lambda e, k=kcg: e.tensor_tensor(
                        out=egsum[:, k:k + 1], in0=egsum[:, k:k + 1],
                        in1=eclb[:, k:k + 1], op=AL.mult),
                        [egsum[:], eclb[:]], [egsum[:]])
            dma("sync",
                o_scr.rearrange("(c p) t -> p c t", p=128)[:, :, t0:t0 + SC],
                o_st[:].rearrange("p (c t) -> p c t", t=256), "st_o")
            dma("sync",
                qg_scr.rearrange("(c p) t -> p c t", p=128)[:, :, t0:t0 + SC],
                qg_st[:].rearrange("p (c t) -> p c t", t=256), "st_qg")
            pop(len(nxt))
            emit_cast_units(8 if sc < 7 else 1000)
            tap(f"a1s{sc}")

        for f in a1_spine_chunks(0):
            f()
        for sc in range(NSC):
            a1_heavy_a(sc)
            nxt = a1_spine_chunks(sc + 1) if sc < NSC - 1 else []
            a1_heavy_b(sc, nxt)

        cop("gpsimd", lambda e: e.memset(sgb[0:1, 0:1], 0.0),
            ["sgbH0", "sgbH1"], [sgb[:]])

        dma("sync", cc_in.rearrange("(c p) v -> p c v", p=128),
            Scar[:].rearrange("p (c v) -> p c v", v=512), "st_Scar")
        S.add("gpsimd", lambda e: e.collective_compute(
            "AllGather", AL.bypass, [[0, 1], [2, 3], [4, 5], [6, 7]],
            ins=[cc_in.tensor.ap().opt()], outs=[cc_out.tensor.ap().opt()]),
            ins=["cc_in"], outs=["cc_out"], dma="cc")
        Sin = Scar
        dma("gpsimd", Sin[:].rearrange("p (c v) -> p c v", v=512),
            cc_out.rearrange("(c p) v -> p c v", p=128)[:, 0:8], "ld_Sin")
        tap("cc")

        dma("scalar", T32a[:].rearrange("p (c t) -> p c t", t=2048),
            wg_scr.rearrange("(c p) t -> p c t", p=128), "T32a")
        dma("scalar", T32b[:].rearrange("p (c t) -> p c t", t=1024),
            wo_scr.rearrange("(c p) t -> p c t", p=128), "T32b")

        def wg_sl(dc, a, b):
            return T32a[:, dc * DVT + a:dc * DVT + b]

        def wo_sl(vcg, a, b):
            return T32b[:, vcg * D + a:vcg * D + b]

        ot_bufs = (o_st, v_b)
        qgt_bufs = (qg_st, qT_b)
        xqgt_bufs = (kT_b, xqg2)

        def emit_ln_apply(src_sl2, src_name, dst_writer):
            cop("scalar", lambda e: e.activation(
                out=srow[0:1, 0:256], in_=pS[0:1, 0:256], func=AF.Copy,
                bias=0.0, scale=1.0 / 1024.0), [pS[:]], [srow[:]])
            cop("scalar", lambda e: e.activation(
                out=t1b[0:1, 0:256], in_=pS[0:1, 256:512], func=AF.Copy,
                bias=0.0, scale=1.0 / 1024.0), [pS[:]], [t1b[:]])
            GB = (gsml, t1tm)
            GN = (gsml[:], t1tm[:])
            PP = (pT0, pT1)
            for tt in range(2):
                cop("tensor", lambda e, t=tt, p=PP[tt]: e.transpose(
                    p[:, 0:128], srow[:, t * 128:(t + 1) * 128], identf[:]),
                    [srow[:], identf[:]], [_nm(PP[tt])])
                cop("tensor", lambda e, t=tt, p=PP[tt]: e.transpose(
                    p[:, 128:256], t1b[:, t * 128:(t + 1) * 128], identf[:]),
                    [t1b[:], identf[:]], [_nm(PP[tt])])
            lops = []
            for tt in range(2):
                g, gn, p = GB[tt], GN[tt], PP[tt]
                pn = _nm(p)
                lops.append([
                    ("vector", lambda e, g=g, p=p: e.tensor_copy(
                        out=g[:, 0:1], in_=p[:, 0:1]), [pn], [gn]),
                    ("vector", lambda e, g=g, p=p: e.tensor_copy(
                        out=g[:, 1:2], in_=p[:, 128:129]), [pn], [gn]),
                    ("scalar", lambda e, g=g: e.activation(
                        out=g[:, 2:3], in_=g[:, 0:1], func=AF.Square,
                        bias=0.0, scale=1.0), [gn], [gn]),
                    ("vector", lambda e, g=g: e.tensor_tensor(
                        out=g[:, 3:4], in0=g[:, 1:2], in1=g[:, 2:3],
                        op=AL.subtract), [gn], [gn]),
                    ("scalar", lambda e, g=g: e.activation(
                        out=g[:, 3:4], in_=g[:, 3:4], func=AF.Sqrt,
                        bias=amrep[:, 7:8], scale=1.0), [gn, amrep[:]], [gn]),
                    ("vector", lambda e, g=g: e.reciprocal(
                        out=g[:, 4:5], in_=g[:, 3:4]), [gn], [gn]),
                    ("vector", lambda e, g=g: e.tensor_tensor(
                        out=g[:, 5:6], in0=g[:, 0:1], in1=g[:, 4:5],
                        op=AL.mult), [gn], [gn]),
                    ("vector", lambda e, g=g: e.tensor_scalar(
                        out=g[:, 5:6], in0=g[:, 5:6], scalar1=-1.0,
                        scalar2=None, op0=AL.mult), [gn], [gn]),
                ])
            for i in range(8):
                for tt in range(2):
                    eng, fn, ri, ro = lops[tt][i]
                    cop(eng, fn, ri, ro)
            for tt in range(2):
                g, gn = GB[tt], GN[tt]
                cop("tensor", lambda e, t=tt, g=g, gn=gn: e.transpose(
                    pT0[0:1, 256 + t * 128:256 + (t + 1) * 128], g[:, 4:5],
                    identf[:]), [gn, identf[:]], [_nm(pT0)])
                cop("tensor", lambda e, t=tt, g=g, gn=gn: e.transpose(
                    pT1[0:1, 256 + t * 128:256 + (t + 1) * 128], g[:, 5:6],
                    identf[:]), [gn, identf[:]], [_nm(pT1)])
            cop("scalar", lambda e: e.activation(
                out=abr[0:1, 0:256], in_=pT0[0:1, 256:512], func=AF.Copy,
                bias=0.0, scale=1.0), [_nm(pT0)], [abr[:]])
            cop("scalar", lambda e: e.activation(
                out=abr[0:1, 256:512], in_=pT1[0:1, 256:512], func=AF.Copy,
                bias=0.0, scale=1.0), [_nm(pT1)], [abr[:]])
            cop("tensor", lambda e: e.matmul(
                pT0[:, 0:256], ones_r[:], abr[0:1, 0:256], start=True, stop=True),
                [abr[:], ones_r[:]], [_nm(pT0)])
            cop("scalar", lambda e: e.activation(
                out=Abc[:], in_=pT0[:, 0:256], func=AF.Copy, bias=0.0, scale=1.0),
                [_nm(pT0)], ["UbufG1"])
            cop("tensor", lambda e: e.matmul(
                pT1[:, 0:256], ones_r[:], abr[0:1, 256:512], start=True, stop=True),
                [abr[:], ones_r[:]], [_nm(pT1)])
            cop("scalar", lambda e: e.activation(
                out=Bbc[:], in_=pT1[:, 0:256], func=AF.Copy, bias=0.0, scale=1.0),
                [_nm(pT1)], ["UbufG2"])
            abr_ = Abc.rearrange("p (a t) -> p a t", a=1).to_broadcast(
                [128, 2, 256])
            for d2 in range(4):
                cop("vector", lambda e, d=d2, ab=abr_: e.tensor_tensor(
                    out=rscrw[:].rearrange("p (a t) -> p a t", t=256),
                    in0=src_sl2(d), in1=ab, op=AL.mult),
                    [src_name, "UbufG1"], ["rscrw"])
                dst_writer(d2)

        def a2_vars(sc):
            par = sc % 2
            return (sc * SC, par, ot_bufs[par], qgt_bufs[par], xqgt_bufs[par],
                    xt0 if par == 0 else xt1)

        def a2_front(sc, weave):
            t0, par, ot, qgt, xqgt, xt = a2_vars(sc)

            def wv(k):
                for _ in range(k):
                    if weave:
                        weave.pop(0)()

            dma("sync", ot[:].rearrange("p (c t) -> p c t", t=256),
                o_scr.rearrange("(c p) t -> p c t", p=128)[:, :, t0:t0 + SC],
                _nm(ot))
            dma("sync", qgt[:].rearrange("p (c t) -> p c t", t=256),
                qg_scr.rearrange("(c p) t -> p c t", p=128)[:, :, t0:t0 + SC],
                _nm(qgt))
            dma("sync", xqgt[:].rearrange("p (c t) -> p c t", t=256),
                xqg_scr.rearrange("(c p) t -> p c t", p=128)[:, :, t0:t0 + SC],
                _nm(xqgt))
            dma("sync", xt[:].rearrange("p (c t) -> p c t", t=256),
                xT.rearrange("(c p) t -> p c t", p=128)[:, :, t0:t0 + SC],
                "a2x0" if par == 0 else "a2x1")

            def corr_part():
                for h in range(HEADS):
                    for vc in range(4):
                        vcg = h * 4 + vc
                        pp = pT0 if vc < 2 else pT1
                        ph = (vc % 2) * 256
                        cop("tensor", lambda e, v=vcg, p=pp, o=ot, q=ph: e.matmul(
                            p[:, q:q + 256], identb[:],
                            o[:, v * 256:(v + 1) * 256],
                            start=True, stop=False), [identb[:], ot[:]], [_nm(pp)])
                        for kc in range(2):
                            kcg = h * 2 + kc
                            cop("tensor", lambda e, k=kcg, v=vc, p=pp, q=qgt,
                                g=ph, last=(kc == 1): e.matmul(
                                    p[:, g:g + 256],
                                    Sin[:, k * 512 + v * 128:
                                        k * 512 + (v + 1) * 128],
                                    q[:, k * 256:(k + 1) * 256],
                                    start=False, stop=last),
                                ["Scar", qgt[:]], [_nm(pp)])
                        if vc % 2 == 1:
                            cop("scalar", lambda e, v=vcg, p=pp, o=ot:
                                e.activation(
                                    out=o[:, (v - 1) * 256:(v + 1) * 256],
                                    in_=p[:, 0:512], func=AF.Copy, bias=0.0,
                                    scale=1.0), [_nm(pp)], [ot[:]])
                    wv(1)
                wv(1)
            if sc > 0:
                corr_part()
            dma("sync", i2row[0:1, :], invsg_scr[sc:sc + 1, :], "i2row")
            i2a = Ubuf[:, 768:1024]
            cop("tensor", lambda e: e.matmul(
                pS[:, 0:256], ones_r[:], i2row[0:1, :], start=True, stop=True),
                [i2row[:], ones_r[:]], [pS[:]])
            cop("scalar", lambda e, o=i2a: e.activation(out=o, in_=pS[:, 0:256],
                                                 func=AF.Copy, bias=0.0, scale=1.0),
                [pS[:]], ["UbufG3"])
            for op2 in range(8):
                pp = pA if op2 % 2 == 0 else pB
                for oh in range(2):
                    ocg = op2 * 2 + oh
                    for dc in range(8):
                        cop("tensor", lambda e, o=ocg, d=dc, p=pp, x=xqgt,
                            h2=oh: e.matmul(
                                p[:, h2 * 256:(h2 + 1) * 256],
                                wg_sl(d, o * 128, (o + 1) * 128),
                                x[:, d * 256:(d + 1) * 256],
                                start=(d == 0), stop=(d == 7)),
                            ["T32a", xqgt[:]], [_nm(pp)])
                for oh in range(2):
                    gs = Ubuf[:, 1024:1280] if oh == 0 else gscr[:]
                    gn = "UbufG4" if oh == 0 else "gscr"
                    cop("vector", lambda e, p=pp, h2=oh, g=gs, iv=i2a:
                        e.scalar_tensor_tensor(
                            out=g, in0=p[:, h2 * 256:(h2 + 1) * 256],
                            scalar=amrep[:, IWG:IWG + 1], in1=iv,
                            op0=AL.mult, op1=AL.mult),
                        [_nm(pp), amrep[:], "UbufG3"], [gn])
                    cop("scalar", lambda e, o2=op2, h2=oh, g=gs: e.activation(
                        out=sgb[:, o2 * 512 + h2 * 256:o2 * 512 + (h2 + 1) * 256],
                        in_=g, func=AF.Silu, bias=0.0, scale=1.0),
                        [gn], [sgb[:]])
                if op2 <= 4:
                    wv(1)
            if sc == 0:
                corr_part()
            for v2 in range(8):
                cop("vector", lambda e, v=v2, o=ot: e.tensor_tensor(
                    out=o[:, v * 512:(v + 1) * 512],
                    in0=o[:, v * 512:(v + 1) * 512],
                    in1=sgb[:, v * 512:(v + 1) * 512], op=AL.mult),
                    [ot[:], sgb[:]], [ot[:]])
            wv(len(weave))

        def a2_back_chunks(sc):
            t0, par, ot, qgt, xqgt, xt = a2_vars(sc)
            out = []

            def mk_zf(q):
                def zf():
                    for vcg in range(q * 4, q * 4 + 4):
                        if vcg == 0:
                            cop("vector", lambda e, o=ot: e.tensor_scalar(
                                out=mfold[:, 0:256], in0=o[:, 0:256],
                                scalar1=nob[:, 0:1], scalar2=None, op0=AL.mult),
                                [ot[:], nob[:]], [mfold[:]])
                            cop("vector", lambda e: e.tensor_copy(
                                out=mfold[:, 256:512], in_=mfold[:, 0:256]),
                                [mfold[:]], [mfold[:]])
                        else:
                            cop("vector", lambda e, o=ot, v=vcg:
                                e.scalar_tensor_tensor(
                                    out=mfold[:, 0:256],
                                    in0=o[:, v * 256:(v + 1) * 256],
                                    scalar=nob[:, v:v + 1], in1=mfold[:, 0:256],
                                    op0=AL.mult, op1=AL.max),
                                [ot[:], nob[:], mfold[:]], [mfold[:]])
                            cop("vector", lambda e, o=ot, v=vcg:
                                e.scalar_tensor_tensor(
                                    out=mfold[:, 256:512],
                                    in0=o[:, v * 256:(v + 1) * 256],
                                    scalar=nob[:, v:v + 1], in1=mfold[:, 256:512],
                                    op0=AL.mult, op1=AL.min),
                                [ot[:], nob[:], mfold[:]], [mfold[:]])
                        cop("scalar", lambda e, o=ot, v=vcg: e.activation(
                            out=gscr[:], in_=o[:, v * 256:(v + 1) * 256],
                            func=AF.Square, bias=0.0, scale=1.0),
                            [ot[:]], ["gscr"])
                        cop("tensor", lambda e, first=(vcg == 0), last=(vcg == 15):
                            e.matmul(pS[0:1, 0:256], ones_c[:], gscr[:],
                                     start=first, stop=last),
                            ["gscr", ones_c[:]], [pS[:]])
                return zf

            def zstat():
                cop("vector", lambda e: e.tensor_scalar(
                    out=mfold[:, 256:512], in0=mfold[:, 256:512], scalar1=-1.0,
                    scalar2=None, op0=AL.mult), [mfold[:]], [mfold[:]])
                cop("vector", lambda e: e.tensor_tensor(
                    out=mfold[:, 0:256], in0=mfold[:, 0:256],
                    in1=mfold[:, 256:512], op=AL.max), [mfold[:]], [mfold[:]])
                cop("scalar", lambda e: e.activation(
                    out=srow[0:1, :], in_=pS[0:1, 0:256], func=AF.Copy,
                    bias=0.0, scale=1.0), [pS[:]], [srow[:]])
                for tt in range(2):
                    sb_ = statb[:, tt * 24:(tt + 1) * 24]
                    cop("tensor", lambda e, t=tt: e.transpose(
                        pS[:, 0:128], srow[:, t * 128:(t + 1) * 128], identf[:]),
                        [srow[:], identf[:]], [pS[:]])
                    cop("vector", lambda e, s=sb_: e.tensor_copy(
                        out=s[:, 0:1], in_=pS[:, 0:1]), [pS[:]], [statb[:]])
                    cop("tensor", lambda e, t=tt: e.transpose(
                        pS[:, 0:128], mfold[:, t * 128:(t + 1) * 128], identf[:]),
                        [mfold[:], identf[:]], [pS[:]])
                    cop("vector", lambda e, s=sb_: e.reduce_max(
                        out=s[:, 1:2], in_=pS[:, 0:128], axis=AX),
                        [pS[:]], [statb[:]])
                    cop("scalar", lambda e, s=sb_: e.activation(
                        out=s[:, 6:7], in_=s[:, 0:1], func=AF.Sqrt,
                        bias=swrep[:, 7:8], scale=1.0 / 2048.0),
                        [statb[:], swrep[:]], [statb[:]])
                    cop("vector", lambda e, s=sb_: e.reciprocal(
                        out=s[:, 7:8], in_=s[:, 6:7]), [statb[:]], [statb[:]])
                    cop("vector", lambda e, s=sb_: e.tensor_scalar(
                        out=s[:, 8:9], in0=s[:, 1:2], scalar1=s[:, 7:8],
                        scalar2=1e-5, op0=AL.mult, op1=AL.max),
                        [statb[:]], [statb[:]])
                    cop("vector", lambda e, s=sb_: e.tensor_scalar(
                        out=s[:, 13:14], in0=s[:, 8:9], scalar1=1.0 / 127.0,
                        scalar2=None, op0=AL.mult), [statb[:]], [statb[:]])
                    cop("vector", lambda e, s=sb_: e.reciprocal(
                        out=s[:, 18:19], in_=s[:, 8:9]), [statb[:]], [statb[:]])
                    cop("vector", lambda e, s=sb_: e.tensor_scalar(
                        out=s[:, 18:19], in0=s[:, 18:19], scalar1=s[:, 7:8],
                        scalar2=127.0, op0=AL.mult, op1=AL.mult),
                        [statb[:]], [statb[:]])
            out = [mk_zf(q) for q in range(4)] + [zstat]

            def zext():
                for tt in range(2):
                    row_extract(statb[:, tt * 24 + 18:tt * 24 + 19],
                                "statb", tt, 0)
                    row_extract(statb[:, tt * 24 + 13:tt * 24 + 14],
                                "statb", tt, 256)
                cop("scalar", lambda e: e.activation(
                    out=urow5[0:1, 0:256], in_=pS[0:1, 0:256], func=AF.Copy,
                    bias=0.0, scale=1.0), [pS[:]], [urow5[:]])
                cop("scalar", lambda e: e.activation(
                    out=irow5[0:1, 0:256], in_=pS[0:1, 256:512], func=AF.Copy,
                    bias=0.0, scale=1.0), [pS[:]], [irow5[:]])
                cop("tensor", lambda e: e.matmul(
                    pS[:, 0:256], ones_r[:], urow5[0:1, 0:256],
                    start=True, stop=True), [urow5[:], ones_r[:]], [pS[:]])
                cop("scalar", lambda e: e.activation(
                    out=Ubuf[:, 0:256], in_=pS[:, 0:256], func=AF.Copy, bias=0.0,
                    scale=1.0), [pS[:]], ["UbufG0"])
                cop("tensor", lambda e: e.matmul(
                    pS[:, 0:256], ones_r[:], irow5[0:1, 0:256],
                    start=True, stop=True), [irow5[:], ones_r[:]], [pS[:]])
                cop("scalar", lambda e: e.activation(
                    out=invsb[:, 0:256], in_=pS[:, 0:256], func=AF.Copy, bias=0.0,
                    scale=1.0), [pS[:]], [invsb[:]])

            def mk_zr(q):
                def zr():
                    for vcg in range(q * 4, q * 4 + 4):
                        rs = rscr if vcg % 2 == 0 else rscr2
                        cop("vector", lambda e, v=vcg, o=ot, r=rs:
                            e.scalar_tensor_tensor(
                                out=r, in0=o[:, v * 256:(v + 1) * 256],
                                scalar=nob[:, v:v + 1], in1=Ubuf[:, 0:256],
                                op0=AL.mult, op1=AL.mult),
                            [ot[:], nob[:], "UbufG0"], ["rscrw"])
                        if vcg % 2 == 1:
                            cop("vector", lambda e, v=vcg, o=ot: e.tensor_scalar(
                                out=o[:, (v - 1) * 256:(v + 1) * 256],
                                in0=rscrw[:], scalar1=MAGIC, scalar2=MAGIC,
                                op0=AL.add, op1=AL.subtract),
                                ["rscrw"], [ot[:]])
                return zr

            return out + [zext] + [mk_zr(q) for q in range(4)]

        def a2_tail(sc):
            t0, par, ot, qgt, xqgt, xt = a2_vars(sc)
            for op2 in range(4):
                pp = pAT if op2 % 2 == 0 else pO
                for oh in range(2):
                    oc = op2 * 2 + oh
                    for vcg in range(16):
                        cop("tensor", lambda e, o=oc, v=vcg, p=pp, z=ot,
                            h2=oh: e.matmul(
                                p[:, h2 * 256:(h2 + 1) * 256],
                                wo_sl(v, o * 128, (o + 1) * 128),
                                z[:, v * 256:(v + 1) * 256],
                                start=(v == 0), stop=(v == 15)),
                            ["T32b", ot[:]], [_nm(pp)])
                ivr = invsb[:, 0:256].rearrange(
                    "p (a t) -> p a t", a=1).to_broadcast([128, 2, 256])
                cop("vector", lambda e, p=pp, iv=ivr: e.scalar_tensor_tensor(
                    out=rscrw[:].rearrange("p (a t) -> p a t", t=256),
                    in0=p[:, 0:512].rearrange("p (a t) -> p a t", t=256),
                    scalar=amrep[:, IWO:IWO + 1],
                    in1=iv, op0=AL.mult, op1=AL.mult),
                    [_nm(pp), amrep[:], invsb[:]], ["rscrw"])
                cop("vector", lambda e, o2=op2, x=xt: e.tensor_tensor(
                    out=gkT_b[:, o2 * 512:(o2 + 1) * 512], in0=rscrw[:],
                    in1=x[:, o2 * 512:(o2 + 1) * 512], op=AL.add),
                    ["rscrw", xt[:]], ["gkT_b"])
            for dc in range(8):
                cop("tensor", lambda e, d=dc: e.matmul(
                    pS[0:1, 0:256], ones_c[:], gkT_b[:, d * 256:(d + 1) * 256],
                    start=(d == 0), stop=(d == 7)),
                    ["gkT_b", ones_c[:]], [pS[:]])
            for d2 in range(4):
                cop("scalar", lambda e, d=d2: e.activation(
                    out=sqw[:], in_=gkT_b[:, d * 512:(d + 1) * 512],
                    func=AF.Square, bias=0.0, scale=1.0), ["gkT_b"], ["sqw"])
                cop("tensor", lambda e, d=d2: e.matmul(
                    pS[0:1, 256:512], ones_c[:], sqw[:, 0:256],
                    start=(d == 0), stop=False), ["sqw", ones_c[:]], [pS[:]])
                cop("tensor", lambda e, d=d2: e.matmul(
                    pS[0:1, 256:512], ones_c[:], sqw[:, 256:512],
                    start=False, stop=(d == 3)), ["sqw", ones_c[:]], [pS[:]])
            bbr_ = Bbc.rearrange("p (a t) -> p a t", a=1).to_broadcast(
                [128, 2, 256])
            emit_ln_apply(
                lambda d: gkT_b[:, d * 512:(d + 1) * 512].rearrange(
                    "p (a t) -> p a t", t=256), "gkT_b",
                lambda d2: cop("vector", lambda e, d=d2, t0_=t0, bb=bbr_:
                               e.tensor_tensor(
                    out=T32c[:].rearrange("p (c t) -> p c t", t=2048)
                    [:, 2 * d:2 * d + 2, t0_:t0_ + SC],
                    in0=rscrw[:].rearrange("p (a t) -> p a t", t=256),
                    in1=bb, op=AL.add),
                    ["rscrw", "UbufG2"], ["T32c"]))
            tap(f"a2s{sc}")

        a2_front(0, [])
        for sc in range(NSC):
            bch = a2_back_chunks(sc)
            if sc < NSC - 1:
                a2_front(sc + 1, bch)
            else:
                for f in bch:
                    f()
            a2_tail(sc)

        def xp_sl(dc, a, b):
            return T32c[:, dc * 2048 + a:dc * 2048 + b]

        def h_sl(unit, a, b):
            if unit < 16:
                return T32a[:, unit * 1024 + a:unit * 1024 + b]
            return T32b[:, (unit - 16) * 1024 + a:(unit - 16) * 1024 + b]

        def h_name(unit):
            return "T32a" if unit < 16 else "T32b"

        def ffnacc_sl(oc, a, b):
            return T32b[:, 8192 + oc * 1024 + a:8192 + oc * 1024 + b]

        dma("sync", rwb[:].rearrange("p (c t) -> p c t", t=16),
            s_rw.rearrange("(c p) t -> p c t", p=128), "rwb")
        GRPS = [list(range(0, 10)), list(range(10, 18))]
        lgb = mfold
        sgf = mfold

        for tile in range(2):
            tb = tile * 1024
            for ns in range(2):
                for dc in range(8):
                    cop("tensor", lambda e, n=ns, d=dc, tb_=tb: e.matmul(
                        pS[0:16, 0:512], rwb[:, d * 16:(d + 1) * 16],
                        xp_sl(d, tb_ + n * 512, tb_ + (n + 1) * 512),
                        start=(d == 0), stop=(d == 7)),
                        [rwb[:], "T32c"], [pS[:]])
                cop("scalar", lambda e, n=ns: e.activation(
                    out=lgb[0:16, n * 512:(n + 1) * 512], in_=pS[0:16, 0:512],
                    func=AF.Copy, bias=0.0, scale=1.0), [pS[:]], ["mfold"])
            Lr = rscrw[:, 0:128]
            Tr = rscrw[:, 128:256]
            smr = rscrw[:, 256:288]
            L3 = Lr.rearrange("p (a t) -> p a t", t=16)
            T3 = Tr.rearrange("p (a t) -> p a t", t=16)

            def sm3(c0):
                return smr[:, c0:c0 + 8].rearrange("p (a o) -> p a o", o=1)

            def smb(c0):
                return smr[:, c0:c0 + 8].rearrange(
                    "p (a o) -> p a o", o=1).to_broadcast([128, 8, 16])

            for tg in range(8):
                rot = tg % 4
                cop("tensor", lambda e, t=tg, r=rot: e.transpose(
                    pS[:, r * 128:(r + 1) * 128],
                    lgb[:, t * 128:(t + 1) * 128], identf[:]),
                    ["mfold", identf[:]], [pS[:]])
                cop("vector", lambda e, t=tg, r=rot: e.tensor_copy(
                    out=Lr[:, t * 16:(t + 1) * 16],
                    in_=pS[:, r * 128:r * 128 + 16]), [pS[:]], ["rscrw"])
            cop("vector", lambda e: e.reduce_max(
                out=sm3(0), in_=L3, axis=AX), ["rscrw"], ["rscrw"])
            cop("vector", lambda e: e.tensor_tensor(
                out=L3, in0=L3, in1=smb(0), op=AL.subtract),
                ["rscrw"], ["rscrw"])
            cop("scalar", lambda e: e.activation(
                out=Lr, in_=Lr, func=AF.Exp, bias=0.0, scale=1.0),
                ["rscrw"], ["rscrw"])
            cop("vector", lambda e: e.reduce_sum(
                out=sm3(8), in_=L3, axis=AX), ["rscrw"], ["rscrw"])
            cop("vector", lambda e: e.reciprocal(
                out=smr[:, 8:16], in_=smr[:, 8:16]), ["rscrw"], ["rscrw"])
            cop("vector", lambda e: e.tensor_tensor(
                out=L3, in0=L3, in1=smb(8), op=AL.mult), ["rscrw"], ["rscrw"])
            cop("vector", lambda e: e.reduce_max(
                out=sm3(0), in_=L3, axis=AX), ["rscrw"], ["rscrw"])
            cop("vector", lambda e: e.tensor_tensor(
                out=T3, in0=L3, in1=smb(0), op=AL.is_equal),
                ["rscrw"], ["rscrw"])
            cop("vector", lambda e: e.scalar_tensor_tensor(
                out=Tr, in0=Tr, scalar=-1e30, in1=Lr, op0=AL.mult, op1=AL.add),
                ["rscrw"], ["rscrw"])
            cop("vector", lambda e: e.reduce_max(
                out=sm3(16), in_=T3, axis=AX), ["rscrw"], ["rscrw"])
            cop("vector", lambda e: e.scalar_tensor_tensor(
                out=smr[:, 24:32], in0=smr[:, 0:8], scalar=1e-9,
                in1=smr[:, 16:24], op0=AL.add, op1=AL.add),
                ["rscrw"], ["rscrw"])
            cop("vector", lambda e: e.reciprocal(
                out=smr[:, 24:32], in_=smr[:, 24:32]), ["rscrw"], ["rscrw"])
            cop("vector", lambda e: e.tensor_tensor(
                out=T3, in0=L3, in1=smb(16), op=AL.is_ge), ["rscrw"], ["rscrw"])
            cop("vector", lambda e: e.tensor_tensor(
                out=L3, in0=L3, in1=smb(24), op=AL.mult), ["rscrw"], ["rscrw"])
            cop("vector", lambda e: e.tensor_tensor(
                out=Tr, in0=Tr, in1=Lr, op=AL.mult), ["rscrw"], ["rscrw"])
            for tg in range(8):
                rot = tg % 4
                cop("tensor", lambda e, t=tg, r=rot: e.transpose(
                    pS[0:16, r * 128:(r + 1) * 128],
                    Tr[:, t * 16:(t + 1) * 16], identf[:]),
                    ["rscrw", identf[:]], [pS[:]])
                cop("scalar", lambda e, t=tg, r=rot: e.activation(
                    out=ctile[0:16, t * 128:(t + 1) * 128],
                    in_=pS[0:16, r * 128:(r + 1) * 128],
                    func=AF.Copy, bias=0.0, scale=1.0), [pS[:]], [ctile[:]])
            tap(f"rt{tile}")

            for gi, grp in enumerate(GRPS):
                unit = 0
                units_of = {}
                for e_id in grp:
                    routed = e_id < 16
                    mid = MIDR if routed else MIDS
                    nkc = mid // 128
                    units_of[e_id] = (unit, nkc)
                    wg_s_ = s_wgr[e_id] if routed else s_wgs[e_id - 16]
                    wu_s_ = s_wur[e_id] if routed else s_wus[e_id - 16]
                    if routed:
                        wgt = (o_st, v_b)[e_id % 2]
                        wut = (kT_b, xqg2)[e_id % 2]
                    else:
                        wgt = sgb
                        wut = (o_st, v_b)[e_id % 2]
                    dma("sync", wgt[:, 0:8 * mid].rearrange(
                        "p (c t) -> p c t", t=mid),
                        wg_s_.rearrange("(c p) t -> p c t", p=128),
                        "bwg_" + _nm(wgt))
                    dma("gpsimd", wut[:, 0:8 * mid].rearrange(
                        "p (c t) -> p c t", t=mid),
                        wu_s_.rearrange("(c p) t -> p c t", p=128), _nm(wut))
                    for ns2 in range(2):
                        ta = tb + ns2 * 512
                        if routed:
                            cop("tensor", lambda e, eid=e_id, n=ns2: e.matmul(
                                pS[:, 0:512],
                                sel16[0:16, eid * 128:(eid + 1) * 128],
                                ctile[0:16, n * 512:(n + 1) * 512],
                                start=True, stop=True),
                                [sel16[:], ctile[:]], [pS[:]])
                        pg = pA if ns2 == 0 else pAT
                        pu = pB if ns2 == 0 else pO
                        sgfv = sgf[:, 0:512] if ns2 == 0 else sgf[:, 512:1024]
                        for m in range(nkc):
                            for dc in range(8):
                                cop("tensor", lambda e, mm=m, d=dc, a=ta, w=wgt,
                                    md=mid, p=pg: e.matmul(
                                        p[:, 0:512],
                                        w[:, d * md + mm * 128:
                                          d * md + (mm + 1) * 128],
                                        xp_sl(d, a, a + 512),
                                        start=(d == 0), stop=(d == 7)),
                                    [_nm(wgt), "T32c"], [_nm(pg)])
                            for dc in range(8):
                                cop("tensor", lambda e, mm=m, d=dc, a=ta, w=wut,
                                    md=mid, p=pu: e.matmul(
                                        p[:, 0:512],
                                        w[:, d * md + mm * 128:
                                          d * md + (mm + 1) * 128],
                                        xp_sl(d, a, a + 512),
                                        start=(d == 0), stop=(d == 7)),
                                    [_nm(wut), "T32c"], [_nm(pu)])
                            cop("scalar", lambda e, p=pg, sg=sgfv: e.activation(
                                out=sg, in_=p[:, 0:512], func=AF.Silu,
                                bias=0.0, scale=1.0), [_nm(pg)], ["mfold"])
                            if routed:
                                cop("vector", lambda e, p=pu, sg=sgfv: e.tensor_tensor(
                                    out=hh[:], in0=sg, in1=p[:, 0:512],
                                    op=AL.mult), ["mfold", _nm(pu)], [hh[:]])
                                cop("vector", lambda e, u=unit, mm=m, n=ns2:
                                    e.tensor_tensor(
                                        out=h_sl(u + mm, n * 512, (n + 1) * 512),
                                        in0=hh[:], in1=pS[:, 0:512], op=AL.mult),
                                    [hh[:], pS[:]], [h_name(unit + m)])
                            else:
                                cop("vector", lambda e, u=unit, mm=m, n=ns2,
                                    p=pu, sg=sgfv: e.tensor_tensor(
                                        out=h_sl(u + mm, n * 512, (n + 1) * 512),
                                        in0=sg, in1=p[:, 0:512],
                                        op=AL.mult),
                                    ["mfold", _nm(pu)], [h_name(unit + m)])
                    unit += nkc
                DPS = {(0, 0): pT0, (1, 0): pT1, (0, 1): pAT, (1, 1): pO}
                for dg in range(4):
                    for ei, e_id in enumerate(grp):
                        routed = e_id < 16
                        u0, nkc = units_of[e_id]
                        wd_s_ = s_wdr[e_id] if routed else s_wds[e_id - 16]
                        if routed:
                            wdt = (qg_st, qT_b)[e_id % 2]
                            so = ((ei // 2) % 2) * 512
                        else:
                            wdt = sgb if e_id == 16 else o_st
                            so = 0
                        sname = (f"wds_{_nm(wdt)}{so}" if routed
                                 else _nm(wdt))
                        dma_key = "wd17" if e_id == 17 else "wd_" + _nm(wdt)
                        S.add("sync" if ei % 2 == 0 else "gpsimd",
                              lambda e, w=wdt, so_=so, nk=nkc, dg_=dg, ws=wd_s_:
                              e.dma_start(
                                  out=w[:, so_:so_ + nk * 256].rearrange(
                                      "p (c t) -> p c t", t=256),
                                  in_=ws.rearrange("(c p) t -> p c t", p=128)
                                  [:, :, dg_ * 256:(dg_ + 1) * 256]),
                              ins=[_nm(wd_s_)], outs=[sname, _nm(wdt)],
                              dma=dma_key)
                        for oo in range(2):
                            oc = dg * 2 + oo
                            if gi == 1 and ei == 0:
                                for ns2 in range(2):
                                    pp = DPS[(oo, ns2)]
                                    cop("tensor", lambda e, o=oc, n=ns2, p=pp:
                                        e.matmul(
                                            p[:, 0:512], identb[:],
                                            ffnacc_sl(o, n * 512, (n + 1) * 512),
                                            start=True, stop=False),
                                        [identb[:], "T32b"], [_nm(pp)])
                            for kc in range(nkc):
                                for ns2 in range(2):
                                    pp = DPS[(oo, ns2)]
                                    cop("tensor", lambda e, o=oc, n=ns2, k=kc,
                                        u=u0, p=pp, w=wdt, so_=so,
                                        st=(gi == 0 and ei == 0 and kc == 0),
                                        sp=(ei == len(grp) - 1 and kc == nkc - 1):
                                        e.matmul(
                                            p[:, 0:512],
                                            w[:, so_ + k * 256 + (o % 2) * 128:
                                              so_ + k * 256 + (o % 2 + 1) * 128],
                                            h_sl(u + k, n * 512, (n + 1) * 512),
                                            start=st, stop=sp),
                                        [sname, h_name(u0 + kc)], [_nm(pp)])
                            for ns2 in range(2):
                                pp = DPS[(oo, ns2)]
                                if ei == len(grp) - 1:
                                    cop("scalar", lambda e, o=oc, n=ns2, p=pp:
                                        e.activation(
                                            out=ffnacc_sl(o, n * 512,
                                                          (n + 1) * 512),
                                            in_=p[:, 0:512], func=AF.Copy,
                                            bias=0.0, scale=1.0),
                                        [_nm(pp)], ["T32b"])
            tap(f"bt{tile}")
            for sci in range(4):
                t0 = tb + sci * SC
                outf = xt0 if sci % 2 == 0 else xt1
                for d2 in range(4):
                    cop("vector", lambda e, d=d2, t0_=t0, tb_=tb: e.tensor_tensor(
                        out=gkT_b[:, d * 512:(d + 1) * 512].rearrange(
                            "p (a t) -> p a t", t=256),
                        in0=T32c[:].rearrange("p (c t) -> p c t", t=2048)
                        [:, 2 * d:2 * d + 2, t0_:t0_ + SC],
                        in1=T32b[:, 8192:16384].rearrange(
                            "p (c t) -> p c t", t=1024)
                        [:, 2 * d:2 * d + 2, t0_ - tb_:t0_ - tb_ + SC],
                        op=AL.add),
                        ["T32c", "T32b"], ["gkT_b"])
                for dc in range(8):
                    cop("tensor", lambda e, d=dc: e.matmul(
                        pS[0:1, 0:256], ones_c[:],
                        gkT_b[:, d * 256:(d + 1) * 256],
                        start=(d == 0), stop=(d == 7)),
                        ["gkT_b", ones_c[:]], [pS[:]])
                for d2 in range(4):
                    cop("scalar", lambda e, d=d2: e.activation(
                        out=sqw[:], in_=gkT_b[:, d * 512:(d + 1) * 512],
                        func=AF.Square, bias=0.0, scale=1.0), ["gkT_b"], ["sqw"])
                    cop("tensor", lambda e, d=d2: e.matmul(
                        pS[0:1, 256:512], ones_c[:], sqw[:, 0:256],
                        start=(d == 0), stop=False), ["sqw", ones_c[:]], [pS[:]])
                    cop("tensor", lambda e, d=d2: e.matmul(
                        pS[0:1, 256:512], ones_c[:], sqw[:, 256:512],
                        start=False, stop=(d == 3)), ["sqw", ones_c[:]], [pS[:]])
                bbr2 = Bbc.rearrange("p (a t) -> p a t", a=1).to_broadcast(
                    [128, 2, 256])
                emit_ln_apply(
                    lambda d: gkT_b[:, d * 512:(d + 1) * 512].rearrange(
                        "p (a t) -> p a t", t=256), "gkT_b",
                    lambda d2: cop("vector", lambda e, d=d2, o=outf, bb=bbr2:
                                   e.tensor_tensor(
                                       out=o[:, d * 512:(d + 1) * 512].rearrange(
                                           "p (a t) -> p a t", t=256),
                                       in0=rscrw[:].rearrange(
                                           "p (a t) -> p a t", t=256),
                                       in1=bb, op=AL.add),
                                   ["rscrw", "UbufG2"], [outf[:]]))
                dma("sync",
                    out_d.rearrange("(c p) t -> p c t", p=128)[:, :, t0:t0 + SC],
                    outf[:].rearrange("p (c t) -> p c t", t=256), "st_out")

        with nc.Block() as block:
            S.emit(nc, block, stack)

    print(f"[build] ops={len(S.ops)}")
    return nc


_NC_CACHE = {}


def _get_nc():
    if "nc" not in _NC_CACHE:
        _NC_CACHE["nc"] = build_program()
    return _NC_CACHE["nc"]


def kernel(**inp):
    nc = _get_nc()
    f32 = np.float32
    tr = lambda a: np.ascontiguousarray(np.asarray(a, dtype=f32).T)
    sel = np.zeros((16, 2048), dtype=BF)
    for e in range(16):
        sel[e, e * 128:(e + 1) * 128] = 1
    base = {
        "wqT": tr(inp["L0_attn_wq"]),
        "wkT": tr(inp["L0_attn_wk"]),
        "wvT": tr(inp["L0_attn_wv"]),
        "wgT": tr(inp["L0_attn_wg"]),
        "wgk1T": tr(inp["L0_attn_wgk1"]),
        "wgk2T": tr(inp["L0_attn_wgk2"]),
        "woT": tr(inp["L0_attn_wo"]),
        "f_rw": np.ascontiguousarray(np.asarray(inp["L0_ffn_router_w"], f32)),
        "f_wg_r": np.ascontiguousarray(np.asarray(inp["L0_ffn_wg_r"], f32)),
        "f_wu_r": np.ascontiguousarray(np.asarray(inp["L0_ffn_wu_r"], f32)),
        "f_wd_r": np.ascontiguousarray(np.asarray(inp["L0_ffn_wd_r"], f32)),
        "f_wg_s": np.ascontiguousarray(np.asarray(inp["L0_ffn_wg_s"], f32)),
        "f_wu_s": np.ascontiguousarray(np.asarray(inp["L0_ffn_wu_s"], f32)),
        "f_wd_s": np.ascontiguousarray(np.asarray(inp["L0_ffn_wd_s"], f32)),
        "ident_f": np.eye(128, dtype=f32),
        "ident_b": np.eye(128).astype(BF),
        "maskU": np.triu(np.ones((128, 128), f32)),
        "sel16_d": sel,
        "ngk2_b": np.ascontiguousarray(
            np.broadcast_to(np.asarray(inp["L0_attn_ngk2"], f32), (128, 16))),
        "bgk2_c": np.ascontiguousarray(
            np.asarray(inp["L0_attn_bgk2"], f32).reshape(8, 128).T),
        "no_c": np.ascontiguousarray(
            np.asarray(inp["L0_attn_no"], f32).reshape(16, 128).T),
    }
    g = np.stack([np.asarray(inp["L0_attn_" + k], f32).reshape(8, 128).T
                  for k in ("nq", "nk", "nv", "ng", "ngk1")], axis=1)
    base["gains"] = np.ascontiguousarray(g.reshape(128, 40))

    x = np.asarray(inp["L0_attn_hidden_states"], f32)
    in_maps = []
    for core in range(8):
        b, half = core // 2, core % 2
        m = dict(base)
        m["xT"] = np.ascontiguousarray(x[b, half * T:(half + 1) * T, :].T)
        m["alpha"] = np.full((128, 1), float(half), f32)
        in_maps.append(m)
    res = run_bass_kernel_spmd(nc, in_maps, core_ids=list(range(8)))
    out = np.empty((4, 4096, 1024), f32)
    for core in range(8):
        b, half = core // 2, core % 2
        out[b, half * T:(half + 1) * T, :] = res.results[core]["out"].T
    return out
